# version 05
# baseline (speedup 1.0000x reference)
import numpy as np
import ml_dtypes
import concourse.bass as bass
import concourse.mybir as mybir
from concourse.bass_utils import run_bass_kernel_spmd
from concourse.alu_op_type import AluOpType as Op

F32 = mybir.dt.float32
BF16 = mybir.dt.bfloat16
I16 = mybir.dt.int16
AF = mybir.ActivationFunctionType
AX = mybir.AxisListType

NCORES = 8
B, S, D = 16, 2048, 1024
SEQ_PER_CORE = B // NCORES
T = SEQ_PER_CORE * S
NT = T // 128
NT_SEQ = S // 128
IF = 2816
NH, KD, HD, VD = 8, 128, 256, 2048
EPS = 1e-5
SC_TILES = 8
N_SC = NT // SC_TILES
FFN_GROUPS = [6, 6, 5, 5]
FFN_GOFF = [0, 6, 12, 17]
GSUB = {6: [512, 256], 5: [512, 128]}
BF = np.dtype(ml_dtypes.bfloat16)


def _host_tables():
    lin = np.linspace(0.0, 1.0, KD // 2)
    angle = np.repeat(1.0 / 10000.0 ** lin, 2)
    ar32 = np.arange(NH, dtype=np.float32)
    decay = np.log1p(-(np.float32(2.0) ** (np.float32(-5.0) - ar32)).astype(np.float32)
                     ).astype(np.float32).astype(np.float64)
    pos = np.arange(S, dtype=np.float64)
    sin = np.sin(pos[:, None] * angle[None, :])
    cos = np.cos(pos[:, None] * angle[None, :])
    e32 = np.exp(decay[:, None] * np.arange(S)[None, :]).astype(np.float32).astype(np.float64)
    nm = 1.0 / np.sqrt(np.cumsum(e32, axis=1))
    Dp = np.exp(pos[None, :] * decay[:, None]) * nm
    Ep = np.exp(-pos[None, :] * decay[:, None]) * (KD ** -0.5)
    rotq = np.empty((NH, 2, KD, S), dtype=np.float32)
    rotk = np.empty((NH, 2, KD, S), dtype=np.float32)
    rq64 = np.empty((NH, 2, KD, S), dtype=np.float64)
    rk64 = np.empty((NH, 2, KD, S), dtype=np.float64)
    for h in range(NH):
        rq64[h, 0] = cos.T * Dp[h][None, :]
        rq64[h, 1] = sin.T * Dp[h][None, :]
        rk64[h, 0] = cos.T * Ep[h][None, :]
        rk64[h, 1] = sin.T * Ep[h][None, :]
    rotq[:] = rq64.astype(np.float32)
    rotk[:] = rk64.astype(np.float32)
    LOSCALE = 2.0 ** 40
    rotqlo = ((rq64 - rotq.astype(np.float64)) * LOSCALE).astype(BF)
    rotklo = ((rk64 - rotk.astype(np.float64)) * LOSCALE).astype(BF)
    masks = np.zeros((4, 128, 512), dtype=BF)
    jj = np.arange(128)[:, None]
    ii = np.arange(512)[None, :]
    for d in range(4):
        masks[d] = (jj <= ii - 128 * d).astype(BF)
    rmat = np.zeros((KD, KD), dtype=BF)
    for m in range(KD // 2):
        rmat[2 * m + 1, 2 * m] = -1.0
        rmat[2 * m, 2 * m + 1] = 1.0
    return rotq, rotk, rotqlo, rotklo, masks, rmat


def _r3(ap, k=16):
    return ap.rearrange("p (b k) -> p b k", k=k)


def build_nc(dump=False, upto=99):
    import os as _os
    import contextlib
    import itertools
    nc = bass.Bass("TRN2", target_bir_lowering=False)
    dt = lambda name, shape, ty, kind: nc.dram_tensor(name, shape, ty, kind=kind).ap()
    x_in = dt("x", [T, D], F32, "ExternalInput")
    wq_d = dt("wq", [D, D], F32, "ExternalInput")
    wk_d = dt("wk", [D, D], F32, "ExternalInput")
    wv_d = dt("wv", [D, VD], F32, "ExternalInput")
    wgt_d = dt("wgt", [D, VD], F32, "ExternalInput")
    wo_d = dt("wo", [VD, D], F32, "ExternalInput")
    wfg_d = dt("wfg", [IF, D], F32, "ExternalInput")
    wfu_d = dt("wfu", [IF, D], F32, "ExternalInput")
    wfd_d = dt("wfd", [D, IF], F32, "ExternalInput")
    rotq_d = dt("rotq", [NH, 2, KD, S], F32, "ExternalInput")
    rotk_d = dt("rotk", [NH, 2, KD, S], F32, "ExternalInput")
    rotqlo_d = dt("rotqlo", [NH, 2, KD, S], BF16, "ExternalInput")
    rotklo_d = dt("rotklo", [NH, 2, KD, S], BF16, "ExternalInput")
    masks_d = dt("masks", [4, 128, 512], BF16, "ExternalInput")
    rmat_d = dt("rmat", [KD, KD], BF16, "ExternalInput")
    out_d = dt("out", [T, D], F32, "ExternalOutput")
    wfgqT = dt("wfgqT", [D, IF], BF16, "Internal")
    wfuqT = dt("wfuqT", [D, IF], BF16, "Internal")
    wfdqT = dt("wfdqT", [IF, D], BF16, "Internal")
    dbg_d = [dt(f"dbg{i}", [T, D], F32, "ExternalOutput") for i in range(3)] if dump else None

    _uc = itertools.count()
    un = lambda n: f"{n}_{next(_uc)}"
    with contextlib.ExitStack() as st:
        sb = lambda name, shape, ty: st.enter_context(nc.sbuf_tensor(un(name), shape, ty))
        R = sb("R", [128, NT * D], BF16)
        ident = sb("ident", [128, 128], BF16)
        identi = sb("identi", [128, 128], mybir.dt.int32)
        rmat = sb("rmat", [128, 128], BF16)
        masks = sb("masks", [128, 4 * 512], BF16)
        ones1 = sb("ones1", [128, 1], BF16)
        msum = sb("msum", [128, NT_SEQ], F32)
        ssq = sb("ssq", [128, NT_SEQ], F32)
        negm = sb("negm", [128, NT_SEQ], F32)
        rstd = sb("rstd", [128, NT_SEQ], F32)
        stA = sb("stA", [128, NT_SEQ], F32)
        stB = sb("stB", [128, NT_SEQ], F32)
        stC = sb("stC", [128, NT_SEQ], F32)
        bm = sb("bm", [128, 192], F32)
        scq = sb("scq", [128, 192], F32)
        hscq = sb("hscq", [128, 192], F32)
        rscq = sb("rscq", [128, 192], F32)
        sc2 = sb("sc2", [128, 192], F32)
        QW = 1024
        qb = dict(
            xn=sb("q_xn", [128, QW], F32), i1=sb("q_i1", [128, QW], F32),
            a=sb("q_a", [128, QW], F32),
            b=sb("q_b", [128, QW], F32), c=sb("q_c", [128, QW], F32),
        )
        xqtile = sb("xqtile", [128, D], BF16)
        t1a = xqtile
        T1B = {}
        SEMS = ["dma", "dmg", "ssq", "stats", "std", "rstd", "t1", "xq", "xqtr",
                "xqev", "gu", "guev", "hq", "hread", "hqtr", "hqev", "down", "radd",
                "aux", "at1", "atr", "atev", "qmm", "qev", "rot", "qd",
                "vg", "vgev", "smm", "sev", "omm", "yev", "csum", "phi", "gat",
                "gtr", "gtev", "dl", "dlev", "store", "vch", "sch",
                "xin", "cst1", "cst2", "wta", "wua", "wda", "wtb", "wub", "wdb",
                "tq", "tk", "lwq", "lwk", "lwv", "lwg", "lwo", "qxn", "qab",
                "tqa", "tqb", "tka", "tkb", "tcb"]
        sms = {n: st.enter_context(nc.semaphore("sem_" + n)) for n in SEMS}
        C = {n: 0 for n in SEMS}
        LIVE = {n: 0 for n in SEMS}
        CHV = {"sem": "vch", "val": 0}
        CHS = {"sem": "sch", "val": 0}

        def pub(ins, name, k=1):
            ins.then_inc(sms[name], k)
            LIVE[name] += k
            return ins

        def gpub(ins, name, k=1):
            return pub(ins, name, k)

        def vop(v, meth, *a, _pub=None, **kw):
            v.wait_ge(sms[CHV["sem"]], CHV["val"])
            ins = meth(*a, **kw)
            nm = _pub if _pub is not None else "vch"
            pub(ins, nm, 1)
            CHV["sem"], CHV["val"] = nm, LIVE[nm]
            return ins

        def sop(s, meth, *a, _pub=None, **kw):
            s.wait_ge(sms[CHS["sem"]], CHS["val"])
            ins = meth(*a, **kw)
            nm = _pub if _pub is not None else "sch"
            pub(ins, nm, 1)
            CHS["sem"], CHS["val"] = nm, LIVE[nm]
            return ins

        QCV = [0]
        QCS = [0]

        def quant_chain(v, src_ap, out_ap, nf, rs_b, hs_b, u1_buf=None, pub_first=None, pub_last=None):
            xn, b_, c_ = qb["xn"], qb["b"], qb["c"]
            A = qb["a"][:, :nf]
            SG = qb["i1"][:, :nf]
            qi = QCV[0]
            QCV[0] += 1
            if pub_first is None:
                vop(v, v.tensor_tensor, out=_r3(xn[:, :nf]), in0=_r3(src_ap), in1=rs_b, op=Op.mult, _pub="qxn")
            else:
                vop(v, v.tensor_tensor, out=_r3(xn[:, :nf]), in0=_r3(src_ap), in1=rs_b, op=Op.mult, _pub=pub_first)
                vop(v, v.tensor_copy, qb["c"][:, 0:1], qb["xn"][:, 0:1], _pub="qxn")
            v.wait_ge(sms["qab"], qi + 1)
            vop(v, v.tensor_scalar, out=b_[:, :nf], in0=A, scalar1=0.25, scalar2=None, op0=Op.is_gt)
            vop(v, v.scalar_tensor_tensor, out=b_[:, :nf], in0=A, scalar=0.75, in1=b_[:, :nf], op0=Op.is_gt, op1=Op.add)
            vop(v, v.scalar_tensor_tensor, out=b_[:, :nf], in0=A, scalar=1.25, in1=b_[:, :nf], op0=Op.is_gt, op1=Op.add)
            vop(v, v.scalar_tensor_tensor, out=b_[:, :nf], in0=A, scalar=1.75, in1=b_[:, :nf], op0=Op.is_gt, op1=Op.add)
            vop(v, v.tensor_scalar, out=c_[:, :nf], in0=A, scalar1=2.5, scalar2=2.0, op0=Op.is_gt, op1=Op.mult)
            vop(v, v.tensor_tensor, out=b_[:, :nf], in0=b_[:, :nf], in1=c_[:, :nf], op=Op.add)
            vop(v, v.tensor_scalar, out=c_[:, :nf], in0=A, scalar1=3.5, scalar2=2.0, op0=Op.is_gt, op1=Op.mult)
            vop(v, v.tensor_tensor, out=b_[:, :nf], in0=b_[:, :nf], in1=c_[:, :nf], op=Op.add)
            vop(v, v.tensor_scalar, out=c_[:, :nf], in0=A, scalar1=5.0, scalar2=4.0, op0=Op.is_gt, op1=Op.mult)
            vop(v, v.tensor_tensor, out=b_[:, :nf], in0=b_[:, :nf], in1=c_[:, :nf], op=Op.add)
            vop(v, v.tensor_tensor, out=b_[:, :nf], in0=b_[:, :nf], in1=SG, op=Op.mult)
            return vop(v, v.tensor_tensor, out=_r3(out_ap), in0=_r3(b_[:, :nf]), in1=hs_b, op=Op.mult, _pub=pub_last)

        def quant_abs(s, nf):
            qi = QCS[0]
            QCS[0] += 1
            s.wait_ge(sms["qxn"], qi + 1)
            sop(s, s.activation, qb["a"][:, :nf], qb["xn"][:, :nf], AF.Abs)
            sop(s, s.activation, qb["i1"][:, :nf], qb["xn"][:, :nf], AF.Sign, _pub="qab")

        def quant_scale(v, bmn, nb):
            vop(v, v.tensor_scalar, out=scq[:, :nb], in0=bmn, scalar1=1.0 / 6.0, scalar2=1e-12, op0=Op.mult, op1=Op.max)
            vop(v, v.tensor_scalar, out=hscq[:, :nb], in0=scq[:, :nb], scalar1=0.5, scalar2=None, op0=Op.mult)
            vop(v, v.reciprocal, rscq[:, :nb], scq[:, :nb])

        with nc.Block() as blk:
            @blk.gpsimd
            def _(g):
                gpub(g.iota(identi[:], [[1, 128]], base=0, channel_multiplier=-1), "aux")
                xi = x_in.rearrange("(t p) d -> p t d", p=128)
                Ri = R[:].rearrange("p (t d) -> p t d", d=D)
                g.dma_start(Ri, xi).then_inc(sms["xin"], 16)
            @blk.sync
            def _(sy):
                sy.dma_start(rmat[:], rmat_d).then_inc(sms["cst1"], 16)
                sy.dma_start(_r3(masks[:], k=512), masks_d.transpose([1, 0, 2])).then_inc(sms["cst2"], 16)
            @blk.vector
            def _(v):
                vop(v, v.memset, ones1[:], 1.0)
                v.wait_ge(sms["aux"], C["aux"] + 1)
                vop(v, v.tensor_scalar, out=ident[:], in0=identi[:], scalar1=0, scalar2=None, op0=Op.is_equal)
            @blk.scalar
            def _(s):
                s.wait_ge(sms["xin"], 16)
                s.wait_ge(sms["cst1"], 16)
                s.wait_ge(sms["cst2"], 16)
        C["xin"] += 16
        C["cst1"] += 16
        C["cst2"] += 16
        C["aux"] += 1

        prep = [(wfg_d, wfgqT, 22, D), (wfu_d, wfuqT, 22, D), (wfd_d, wfdqT, 8, IF)]
        if upto < 1:
            prep = []
        nprep = sum(p[2] for p in prep)
        with nc.Block() as blk:
            with (
                nc.psum_tensor(un("ptrW"), [128, IF], BF16, is_transpose_dest=True) as ptrW,
                nc.sbuf_tensor(un("wld"), [128, IF], F32) as wld,
                nc.sbuf_tensor(un("wqt"), [128, IF], BF16) as wqt,
                nc.sbuf_tensor(un("wtr"), [128, IF], BF16) as wtr,
            ):
                @blk.gpsimd
                def _(g):
                    n = 0
                    for src, dst, nchunk, width in prep:
                        for ch in range(nchunk):
                            if n >= 1:
                                g.wait_ge(sms["aux"], C["aux"] + n)
                            g.dma_start(wld[:, :width], src[ch * 128:(ch + 1) * 128, :]).then_inc(sms["dmg"], 16)
                            n += 1
                @blk.vector
                def _(v):
                    n = 0
                    for src, dst, nchunk, width in prep:
                        for ch in range(nchunk):
                            v.wait_ge(sms["dmg"], C["dmg"] + 16 * (n + 1))
                            if n >= 1:
                                v.wait_ge(sms["xqtr"], C["xqtr"] + n)
                            pieces = [(0, width)] if width <= QW else [(o, 704) for o in range(0, width, 704)]
                            nball = width // 16
                            vop(v, v.tensor_reduce, out=bm[:, :nball], in_=_r3(wld[:, :width]), axis=AX.X,
                                op=Op.max, apply_absolute_value=True)
                            quant_scale(v, bm[:, :nball], nball)
                            for pi, (off, w2) in enumerate(pieces):
                                nb = w2 // 16
                                rs_b = rscq[:, off // 16:off // 16 + nb].unsqueeze(2).broadcast_to([128, nb, 16])
                                hs_b = hscq[:, off // 16:off // 16 + nb].unsqueeze(2).broadcast_to([128, nb, 16])
                                quant_chain(v, wld[:, off:off + w2], wqt[:, off:off + w2], w2, rs_b, hs_b,
                                            u1_buf=qb["a"][:, :w2],
                                            pub_first="aux" if pi == len(pieces) - 1 else None,
                                            pub_last="xq" if pi == len(pieces) - 1 else None)
                            n += 1
                @blk.tensor
                def _(t_):
                    n = 0
                    for src, dst, nchunk, width in prep:
                        ntr = width // 128
                        for ch in range(nchunk):
                            t_.wait_ge(sms["xq"], C["xq"] + n + 1)
                            if n >= 1:
                                t_.wait_ge(sms["xqev"], C["xqev"] + n)
                            for k in range(ntr):
                                ins = t_.transpose(ptrW[:, k * 128:(k + 1) * 128], wqt[:, k * 128:(k + 1) * 128], ident[:])
                            pub(ins, "xqtr")
                            n += 1
                @blk.scalar
                def _(s):
                    n = 0
                    for src, dst, nchunk, width in prep:
                        ntr = width // 128
                        for ch in range(nchunk):
                            npieces = 1 if width <= QW else width // 704
                            for pi in range(npieces):
                                quant_abs(s, width if width <= QW else 704)
                            s.wait_ge(sms["xqtr"], C["xqtr"] + n + 1)
                            if n >= 1:
                                s.wait_ge(sms["dma"], C["dma"] + 16 * n)
                            sop(s, s.copy, wtr[:, :ntr * 128], ptrW[:, :ntr * 128], _pub="xqev")
                            n += 1
                @blk.sync
                def _(sy):
                    n = 0
                    for src, dst, nchunk, width in prep:
                        ntr = width // 128
                        for ch in range(nchunk):
                            sy.wait_ge(sms["xqev"], C["xqev"] + n + 1)
                            dsti = dst[0:width, ch * 128:(ch + 1) * 128].rearrange("(k p) c -> p k c", p=128)
                            sy.dma_start(dsti, wtr[:, :ntr * 128].rearrange("p (k c) -> p k c", c=128)).then_inc(sms["dma"], 16)
                            n += 1
                    if n:
                        sy.wait_ge(sms["dma"], C["dma"] + 16 * n)
        C["dmg"] += 16 * nprep
        C["xq"] += nprep
        C["aux"] += nprep
        C["xqtr"] += nprep
        C["xqev"] += nprep
        C["dma"] += 16 * nprep

        WG_MAX = 6
        FB = {}

        def ffn_stage_a(tbase):
            xqT = FB["xqT"]
            t1b = T1B["b"]
            with nc.Block() as blk:
                with nc.psum_tensor(un("ptrA"), [128, 2 * 8 * 128], BF16, is_transpose_dest=True) as ptrA:
                    @blk.vector
                    def _(v):
                        nco = SC_TILES
                        for t in range(SC_TILES):
                            vop(v, v.tensor_reduce, out=msum[:, t:t + 1], in_=R[:, (tbase + t) * D:(tbase + t + 1) * D],
                                axis=AX.X, op=Op.add)
                        v.wait_ge(sms["ssq"], C["ssq"] + 1)
                        vop(v, v.tensor_scalar, out=negm[:, :nco], in0=msum[:, :nco], scalar1=-1.0 / D, scalar2=None, op0=Op.mult)
                        vop(v, v.tensor_scalar, out=stA[:, :nco], in0=ssq[:, :nco], scalar1=1.0 / D, scalar2=None, op0=Op.mult)
                        vop(v, v.tensor_tensor, out=stB[:, :nco], in0=negm[:, :nco], in1=negm[:, :nco], op=Op.mult)
                        vop(v, v.tensor_tensor, out=stC[:, :nco], in0=stA[:, :nco], in1=stB[:, :nco], op=Op.subtract)
                        vop(v, v.tensor_scalar, out=stA[:, :nco], in0=stC[:, :nco], scalar1=EPS, scalar2=None,
                            op0=Op.add, _pub="stats")
                        v.wait_ge(sms["std"], C["std"] + 1)
                        vop(v, v.reciprocal, rstd[:, :nco], stB[:, :nco], _pub="rstd")
                        for t in range(SC_TILES):
                            v.wait_ge(sms["t1"], C["t1"] + t + 1)
                            if t >= 1:
                                v.wait_ge(sms["xqtr"], C["xqtr"] + t)
                            nb = D // 16
                            vop(v, v.tensor_reduce, out=bm[:, :nb], in_=_r3(t1b[:]), axis=AX.X,
                                op=Op.max, apply_absolute_value=True)
                            vop(v, v.tensor_scalar, out=sc2[:, :nb], in0=bm[:, :nb], scalar1=rstd[:, t:t + 1],
                                scalar2=None, op0=Op.mult)
                            quant_scale(v, sc2[:, :nb], nb)
                            vop(v, v.tensor_scalar, out=sc2[:, :nb], in0=rscq[:, :nb], scalar1=rstd[:, t:t + 1],
                                scalar2=None, op0=Op.mult)
                            rs_b = sc2[:, :nb].unsqueeze(2).broadcast_to([128, nb, 16])
                            hs_b = hscq[:, :nb].unsqueeze(2).broadcast_to([128, nb, 16])
                            quant_chain(v, t1b[:], xqtile[:], D, rs_b, hs_b, u1_buf=qb["a"][:, :D], pub_last="xq")
                    @blk.scalar
                    def _(s):
                        for t in range(SC_TILES):
                            sop(s, s.activation, qb["a"][:, :D], R[:, (tbase + t) * D:(tbase + t + 1) * D],
                                AF.Square, accum_out=ssq[:, t:t + 1],
                                _pub="ssq" if t == SC_TILES - 1 else None)
                        s.wait_ge(sms["stats"], C["stats"] + 1)
                        sop(s, s.activation, stB[:, :SC_TILES], stA[:, :SC_TILES], AF.Sqrt, _pub="std")
                        s.wait_ge(sms["rstd"], C["rstd"] + 1)
                        for t in range(SC_TILES):
                            if t >= 1:
                                s.wait_ge(sms["xq"], C["xq"] + t)
                                s.wait_ge(sms["atr"], 0)
                            sop(s, s.activation, t1b[:], R[:, (tbase + t) * D:(tbase + t + 1) * D], AF.Identity,
                                bias=negm[:, t:t + 1], scale=1.0, _pub="t1")
                            quant_abs(s, D)
                            if t >= 1:
                                s.wait_ge(sms["xqtr"], C["xqtr"] + t)
                                par = (t - 1) % 2
                                sop(s, s.copy,
                                    xqT[:].rearrange("p (c x) -> p c x", x=SC_TILES * 128)[:, :, (t - 1) * 128:t * 128],
                                    ptrA[:, par * 1024:(par + 1) * 1024].rearrange("p (c x) -> p c x", x=128)
                                    , _pub="xqev")
                        s.wait_ge(sms["xqtr"], C["xqtr"] + SC_TILES)
                        par = (SC_TILES - 1) % 2
                        sop(s, s.copy,
                            xqT[:].rearrange("p (c x) -> p c x", x=SC_TILES * 128)[:, :, (SC_TILES - 1) * 128:SC_TILES * 128],
                            ptrA[:, par * 1024:(par + 1) * 1024].rearrange("p (c x) -> p c x", x=128)
                            , _pub="xqev")
                    @blk.tensor
                    def _(t_):
                        for t in range(SC_TILES):
                            t_.wait_ge(sms["xq"], C["xq"] + t + 1)
                            if t >= 2:
                                t_.wait_ge(sms["xqev"], C["xqev"] + t - 1)
                            par = t % 2
                            for k in range(8):
                                ins = t_.transpose(ptrA[:, par * 1024 + k * 128:par * 1024 + (k + 1) * 128],
                                                   xqtile[:, k * 128:(k + 1) * 128], ident[:])
                            pub(ins, "xqtr")
            C["ssq"] += 1; C["stats"] += 1; C["std"] += 1; C["rstd"] += 1
            C["t1"] += SC_TILES; C["xq"] += SC_TILES; C["xqtr"] += SC_TILES; C["xqev"] += SC_TILES

        def ffn_stage_b(tbase):
            xqT, wgq, wuq, wdq = FB["xqT"], FB["wgq"], FB["wuq"], FB["wdq"]
            sgt, upt, hqt, hqT = FB["sgt"], FB["upt"], FB["hqt"], FB["hqT"]
            with nc.Block() as blk:
                with (
                    nc.psum_tensor(un("pg"), [128, 512], F32) as pg,
                    nc.psum_tensor(un("pu"), [128, 512], F32) as pu,
                    nc.psum_tensor(un("pd"), [128, 2 * 1024], F32) as pd,
                    nc.psum_tensor(un("ptrB"), [128, WG_MAX * 128], BF16, is_transpose_dest=True) as ptrB,
                ):
                    @blk.sync
                    def _(sy):
                        for g in range(4):
                            if g >= 2:
                                sy.wait_ge(sms["gu"], C["gu"] + (g - 1) * 2 * SC_TILES)
                                sy.wait_ge(sms["down"], C["down"] + (g - 1) * SC_TILES)
                            nch = FFN_GROUPS[g]
                            off = FFN_GOFF[g]
                            buf = g % 2
                            src_g = wfgqT[:, off * 128:(off + nch) * 128].rearrange("(c p) w -> p c w", p=128)
                            src_u = wfuqT[:, off * 128:(off + nch) * 128].rearrange("(c p) w -> p c w", p=128)
                            wsem = [("wta", "wua", "wda"), ("wtb", "wub", "wdb")][buf]
                            sy.dma_start(wgq[buf][:, :8 * nch * 128].rearrange("p (c w) -> p c w", w=nch * 128), src_g).then_inc(sms[wsem[0]], 16)
                            sy.dma_start(wuq[buf][:, :8 * nch * 128].rearrange("p (c w) -> p c w", w=nch * 128), src_u).then_inc(sms[wsem[1]], 16)
                            src_d = wfdqT[off * 128:(off + nch) * 128, :].rearrange("(c p) w -> p c w", p=128)
                            sy.dma_start(wdq[buf][:, :nch * 1024].rearrange("p (c w) -> p c w", w=1024), src_d).then_inc(sms[wsem[2]], 16)
                    @blk.tensor
                    def _(t_):
                        gu_n = 0
                        for g in range(4):
                            nch = FFN_GROUPS[g]
                            subs = GSUB[nch]
                            buf = g % 2
                            wsem = [("wta", "wua", "wda"), ("wtb", "wub", "wdb")][g % 2]
                            for _w in wsem:
                                t_.wait_ge(sms[_w], C[_w] + 16 * (g // 2 + 1))
                            for t in range(SC_TILES):
                                dn = g * SC_TILES + t
                                for si, sub in enumerate(subs):
                                    soff = sum(subs[:si])
                                    if gu_n >= 1:
                                        t_.wait_ge(sms["guev"], C["guev"] + gu_n)
                                    for c in range(8):
                                        stat = xqT[:, c * SC_TILES * 128 + t * 128: c * SC_TILES * 128 + (t + 1) * 128]
                                        t_.matmul(pg[:, :sub], stat,
                                                  wgq[buf][:, c * nch * 128 + soff: c * nch * 128 + soff + sub],
                                                  start=(c == 0), stop=(c == 7))
                                    for c in range(8):
                                        stat = xqT[:, c * SC_TILES * 128 + t * 128: c * SC_TILES * 128 + (t + 1) * 128]
                                        ins = t_.matmul(pu[:, :sub], stat,
                                                        wuq[buf][:, c * nch * 128 + soff: c * nch * 128 + soff + sub],
                                                        start=(c == 0), stop=(c == 7))
                                    gu_n += 1
                                    pub(ins, "gu")
                                t_.wait_ge(sms["hq"], C["hq"] + dn + 1)
                                if dn >= 1:
                                    t_.wait_ge(sms["hqev"], C["hqev"] + dn)
                                for k in range(nch):
                                    ins = t_.transpose(ptrB[:, k * 128:(k + 1) * 128], hqt[:, k * 128:(k + 1) * 128], ident[:])
                                pub(ins, "hqtr")
                                t_.wait_ge(sms["hqev"], C["hqev"] + dn + 1)
                                par = dn % 2
                                if dn >= 2:
                                    t_.wait_ge(sms["radd"], C["radd"] + dn - 1)
                                for c in range(nch):
                                    for half in range(2):
                                        ins = t_.matmul(pd[:, par * 1024 + half * 512: par * 1024 + (half + 1) * 512],
                                                        hqT[:, c * 128:(c + 1) * 128],
                                                        wdq[buf][:, c * 1024 + half * 512: c * 1024 + (half + 1) * 512],
                                                        start=(c == 0), stop=(c == nch - 1))
                                pub(ins, "down")
                    @blk.scalar
                    def _(s):
                        gu_n = 0
                        for g in range(4):
                            nch = FFN_GROUPS[g]
                            subs = GSUB[nch]
                            for t in range(SC_TILES):
                                dn = g * SC_TILES + t
                                for si, sub in enumerate(subs):
                                    soff = sum(subs[:si])
                                    s.wait_ge(sms["gu"], C["gu"] + gu_n + 1)
                                    if dn >= 1 and si == 0:
                                        s.wait_ge(sms["hread"], C["hread"] + dn)
                                    sop(s, s.activation, sgt[:, soff:soff + sub], pg[:, :sub], AF.Silu)
                                    sop(s, s.copy, upt[:, soff:soff + sub], pu[:, :sub], _pub="guev")
                                    gu_n += 1
                                quant_abs(s, nch * 128)
                                s.wait_ge(sms["hqtr"], C["hqtr"] + dn + 1)
                                sop(s, s.copy, hqT[:, :nch * 128], ptrB[:, :nch * 128], _pub="hqev")
                    @blk.vector
                    def _(v):
                        for g in range(4):
                            nch = FFN_GROUPS[g]
                            width = nch * 128
                            nb = width // 16
                            for t in range(SC_TILES):
                                dn = g * SC_TILES + t
                                v.wait_ge(sms["guev"], C["guev"] + (dn + 1) * 2)
                                vop(v, v.tensor_tensor, out=qb["a"][:, :width], in0=sgt[:, :width],
                                    in1=upt[:, :width], op=Op.mult, _pub="hread")
                                if dn >= 1:
                                    v.wait_ge(sms["hqtr"], C["hqtr"] + dn)
                                vop(v, v.tensor_reduce, out=bm[:, :nb], in_=_r3(qb["a"][:, :width]), axis=AX.X,
                                    op=Op.max, apply_absolute_value=True)
                                quant_scale(v, bm[:, :nb], nb)
                                rs_b = rscq[:, :nb].unsqueeze(2).broadcast_to([128, nb, 16])
                                hs_b = hscq[:, :nb].unsqueeze(2).broadcast_to([128, nb, 16])
                                quant_chain(v, qb["a"][:, :width], hqt[:, :width], width, rs_b, hs_b,
                                            u1_buf=qb["a"][:, :width], pub_last="hq")
                                v.wait_ge(sms["down"], C["down"] + dn + 1)
                                par = dn % 2
                                tile = tbase + t
                                vop(v, v.scalar_tensor_tensor, out=R[:, tile * D:(tile + 1) * D],
                                    in0=pd[:, par * 1024:(par + 1) * 1024], scalar=0.5,
                                    in1=R[:, tile * D:(tile + 1) * D], op0=Op.mult, op1=Op.add, _pub="radd")
            for _w in ("wta", "wua", "wda", "wtb", "wub", "wdb"):
                C[_w] += 32
            C["gu"] += 2 * 4 * SC_TILES
            C["guev"] += 2 * 4 * SC_TILES
            C["hq"] += 4 * SC_TILES; C["hread"] += 4 * SC_TILES
            C["hqtr"] += 4 * SC_TILES; C["hqev"] += 4 * SC_TILES
            C["down"] += 4 * SC_TILES; C["radd"] += 4 * SC_TILES

        def emit_ffn():
            import contextlib as _ctx
            with _ctx.ExitStack() as fst:
                fsb = lambda name, shape, ty: fst.enter_context(nc.sbuf_tensor(un(name), shape, ty))
                FB["xqT"] = fsb("xqT", [128, 8 * SC_TILES * 128], BF16)
                T1B["b"] = fsb("t1b", [128, D], F32)
                FB["wgq"] = [fsb(f"wgq{i}", [128, 8 * WG_MAX * 128], BF16) for i in range(2)]
                FB["wuq"] = [fsb(f"wuq{i}", [128, 8 * WG_MAX * 128], BF16) for i in range(2)]
                FB["wdq"] = [fsb(f"wdq{i}", [128, WG_MAX * 1024], BF16) for i in range(2)]
                FB["sgt"] = fsb("sgt", [128, WG_MAX * 128], F32)
                FB["upt"] = fsb("upt", [128, WG_MAX * 128], F32)
                FB["hqt"] = fsb("hqt", [128, WG_MAX * 128], BF16)
                FB["hqT"] = fsb("hqT", [128, WG_MAX * 128], BF16)
                for sc in range(int(_os.environ.get("FFNSC", N_SC))):
                    ffn_stage_a(sc * SC_TILES)
                    if _os.environ.get("FFNB", "1") == "1":
                        ffn_stage_b(sc * SC_TILES)

        AB = {}

        def attn_ln(sbase):
            xaT = AB["xaT"]
            with nc.Block() as blk:
                with nc.psum_tensor(un("ptrL"), [128, 2 * 8 * 128], BF16, is_transpose_dest=True) as ptrL:
                    @blk.vector
                    def _(v):
                        nco = NT_SEQ
                        for t in range(NT_SEQ):
                            vop(v, v.tensor_reduce, out=msum[:, t:t + 1], in_=R[:, (sbase + t) * D:(sbase + t + 1) * D],
                                axis=AX.X, op=Op.add)
                        v.wait_ge(sms["ssq"], C["ssq"] + 1)
                        vop(v, v.tensor_scalar, out=negm[:, :nco], in0=msum[:, :nco], scalar1=-1.0 / D, scalar2=None, op0=Op.mult)
                        vop(v, v.tensor_scalar, out=stA[:, :nco], in0=ssq[:, :nco], scalar1=1.0 / D, scalar2=None, op0=Op.mult)
                        vop(v, v.tensor_tensor, out=stB[:, :nco], in0=negm[:, :nco], in1=negm[:, :nco], op=Op.mult)
                        vop(v, v.tensor_tensor, out=stC[:, :nco], in0=stA[:, :nco], in1=stB[:, :nco], op=Op.subtract)
                        vop(v, v.tensor_scalar, out=stA[:, :nco], in0=stC[:, :nco], scalar1=EPS, scalar2=None,
                            op0=Op.add, _pub="stats")
                        v.wait_ge(sms["std"], C["std"] + 1)
                        vop(v, v.reciprocal, rstd[:, :nco], stB[:, :nco])
                        vop(v, v.tensor_tensor, out=stC[:, :nco], in0=negm[:, :nco], in1=rstd[:, :nco],
                            op=Op.mult, _pub="rstd")
                    @blk.scalar
                    def _(s):
                        for t in range(NT_SEQ):
                            sop(s, s.activation, qb["a"][:, :D], R[:, (sbase + t) * D:(sbase + t + 1) * D],
                                AF.Square, accum_out=ssq[:, t:t + 1],
                                _pub="ssq" if t == NT_SEQ - 1 else None)
                        s.wait_ge(sms["stats"], C["stats"] + 1)
                        sop(s, s.activation, stB[:, :NT_SEQ], stA[:, :NT_SEQ], AF.Sqrt, _pub="std")
                        s.wait_ge(sms["rstd"], C["rstd"] + 1)
                        for t in range(NT_SEQ):
                            if t >= 1:
                                s.wait_ge(sms["atr"], C["atr"] + t)
                            sop(s, s.activation, t1a[:], R[:, (sbase + t) * D:(sbase + t + 1) * D], AF.Identity,
                                bias=stC[:, t:t + 1], scale=rstd[:, t:t + 1], _pub="at1")
                            if t >= 1:
                                par = (t - 1) % 2
                                sop(s, s.copy,
                                    xaT[:].rearrange("p (c x) -> p c x", x=S)[:, :, (t - 1) * 128:t * 128],
                                    ptrL[:, par * 1024:(par + 1) * 1024].rearrange("p (c x) -> p c x", x=128)
                                    , _pub="atev")
                        s.wait_ge(sms["atr"], C["atr"] + NT_SEQ)
                        par = (NT_SEQ - 1) % 2
                        sop(s, s.copy,
                            xaT[:].rearrange("p (c x) -> p c x", x=S)[:, :, (NT_SEQ - 1) * 128:NT_SEQ * 128],
                            ptrL[:, par * 1024:(par + 1) * 1024].rearrange("p (c x) -> p c x", x=128)
                            , _pub="atev")
                    @blk.tensor
                    def _(t_):
                        for t in range(NT_SEQ):
                            t_.wait_ge(sms["at1"], C["at1"] + t + 1)
                            if t >= 2:
                                t_.wait_ge(sms["atev"], C["atev"] + t - 1)
                            par = t % 2
                            for k in range(8):
                                ins = t_.transpose(ptrL[:, par * 1024 + k * 128:par * 1024 + (k + 1) * 128],
                                                   t1a[:, k * 128:(k + 1) * 128], ident[:])
                            pub(ins, "atr")
            C["ssq"] += 1; C["stats"] += 1; C["std"] += 1; C["rstd"] += 1
            C["at1"] += NT_SEQ; C["atr"] += NT_SEQ; C["atev"] += NT_SEQ

        def emit_head(s_i, h, sbase):
            xaT, wqh, wkh, wvh, wgh, woh = AB["xaT"], AB["wqh"], AB["wkh"], AB["wvh"], AB["wgh"], AB["woh"]
            tabq, tabk, qsb, ksb, trot = AB["tabq"], AB["tabk"], AB["qsb"], AB["ksb"], AB["trot"]
            lostage = AB["lostage"]
            qD, kE = qsb, ksb
            vsb, gsb, ysb, sT, csb = AB["vsb"], AB["gsb"], AB["ysb"], AB["sT"], AB["csb"]
            cT, phi, yss, gat, gatT, dlb = AB["cT"], AB["phi"], AB["yss"], AB["gat"], AB["gatT"], AB["dlb"]
            with nc.Block() as blk:
                with (
                    nc.psum_tensor(un("pq"), [128, 2 * 512], F32) as pq,
                    nc.psum_tensor(un("pr"), [128, 2 * 512], F32) as pr,
                    nc.psum_tensor(un("pv2"), [128, 2 * 512], F32) as pv2,
                ):
                    @blk.sync
                    def _(sy):
                        sy.wait_ge(sms["qd"], C["qd"])
                        for idx in range(8):
                            qc, which = idx // 2, idx % 2
                            slot = qc % 2
                            if qc >= 2:
                                sy.wait_ge(sms["qd"], C["qd"] + 2 * (qc - 2) + which + 1)
                            tab_d = rotq_d if which == 0 else rotk_d
                            tab_s = tabq if which == 0 else tabk
                            semn = [("tqa", "tqb"), ("tka", "tkb")][which][slot]
                            sy.dma_start(
                                tab_s[:, slot * 1024:(slot + 1) * 1024].rearrange("p (a x) -> p a x", x=512),
                                tab_d[h][:, :, qc * 512:(qc + 1) * 512].transpose([1, 0, 2])
                            ).then_inc(sms[semn], 16)
                            lo_d = rotqlo_d if which == 0 else rotklo_d
                            n_lo = C["tq"] // 16 + idx
                            sy.wait_ge(sms["tcb"], n_lo)
                            sy.dma_start(
                                lostage[:].rearrange("p (a x) -> p a x", x=512),
                                lo_d[h][:, :, qc * 512:(qc + 1) * 512].transpose([1, 0, 2])
                            ).then_inc(sms["tq"], 16)
                    @blk.gpsimd
                    def _(g):
                        g.wait_ge(sms["qmm"], C["qmm"])
                        g.dma_start(wqh[:].rearrange("p (c w) -> p c w", w=128),
                                    wq_d[:, h * 128:(h + 1) * 128].rearrange("(c p) w -> p c w", p=128)).then_inc(sms["lwq"], 16)
                        g.dma_start(wkh[:].rearrange("p (c w) -> p c w", w=128),
                                    wk_d[:, h * 128:(h + 1) * 128].rearrange("(c p) w -> p c w", p=128)).then_inc(sms["lwk"], 16)
                        g.wait_ge(sms["vg"], C["vg"])
                        g.dma_start(wvh[:].rearrange("p (c w) -> p c w", w=HD),
                                    wv_d[:, h * HD:(h + 1) * HD].rearrange("(c p) w -> p c w", p=128)).then_inc(sms["lwv"], 16)
                        g.dma_start(wgh[:].rearrange("p (c w) -> p c w", w=HD),
                                    wgt_d[:, h * HD:(h + 1) * HD].rearrange("(c p) w -> p c w", p=128)).then_inc(sms["lwg"], 16)
                        g.wait_ge(sms["dl"], C["dl"])
                        g.dma_start(woh[:].rearrange("p (c w) -> p c w", w=D),
                                    wo_d[h * HD:(h + 1) * HD, :].rearrange("(c p) w -> p c w", p=128)).then_inc(sms["lwo"], 16)
                    @blk.tensor
                    def _(t_):
                        t_.wait_ge(sms["lwq"], C["lwq"] + 16)
                        t_.wait_ge(sms["lwk"], C["lwk"] + 16)
                        for qc in range(4):
                            par = qc % 2
                            if qc >= 1:
                                t_.wait_ge(sms["qev"], C["qev"] + 2 * qc)
                            for c in range(8):
                                ins = t_.matmul(pq[:, par * 512:(par + 1) * 512],
                                                wqh[:, c * 128:(c + 1) * 128],
                                                xaT[:, c * S + qc * 512:c * S + (qc + 1) * 512], start=(c == 0), stop=(c == 7))
                            pub(ins, "qmm")
                            for c in range(8):
                                ins = t_.matmul(pq[:, (1 - par) * 512:(2 - par) * 512],
                                                wkh[:, c * 128:(c + 1) * 128],
                                                xaT[:, c * S + qc * 512:c * S + (qc + 1) * 512], start=(c == 0), stop=(c == 7))
                            pub(ins, "qmm")
                        for idx in range(8):
                            qc, which = idx // 2, idx % 2
                            t_.wait_ge(sms["qev"], C["qev"] + 2 * qc + which + 1)
                            if idx >= 2:
                                t_.wait_ge(sms["qd"], C["qd"] + idx - 1)
                            par = idx % 2
                            src = qsb if which == 0 else ksb
                            pub(t_.matmul(pr[:, par * 512:(par + 1) * 512], rmat[:], src[:, qc * 512:(qc + 1) * 512],
                                          start=True, stop=True), "rot")
                        t_.wait_ge(sms["lwv"], C["lwv"] + 16)
                        t_.wait_ge(sms["lwg"], C["lwg"] + 16)
                        t_.wait_ge(sms["lwo"], C["lwo"] + 16)
                        t_.wait_ge(sms["qd"], C["qd"] + 8)
                        for t in range(NT_SEQ):
                            par = t % 2
                            if t >= 2:
                                t_.wait_ge(sms["vgev"], C["vgev"] + t - 1)
                            for c in range(8):
                                stat = xaT[:, c * S + t * 128:c * S + (t + 1) * 128]
                                t_.matmul(pv2[:, par * 512:par * 512 + 256], stat, wvh[:, c * HD:(c + 1) * HD],
                                          start=(c == 0), stop=(c == 7))
                            for c in range(8):
                                stat = xaT[:, c * S + t * 128:c * S + (t + 1) * 128]
                                ins = t_.matmul(pv2[:, par * 512 + 256:(par + 1) * 512], stat, wgh[:, c * HD:(c + 1) * HD],
                                                start=(c == 0), stop=(c == 7))
                            pub(ins, "vg")
                    @blk.scalar
                    def _(s):
                        for idx in range(8):
                            qc, which = idx // 2, idx % 2
                            par = qc % 2 if which == 0 else 1 - (qc % 2)
                            s.wait_ge(sms["qmm"], C["qmm"] + idx + 1)
                            dst = qsb if which == 0 else ksb
                            sop(s, s.copy, dst[:, qc * 512:(qc + 1) * 512], pq[:, par * 512:(par + 1) * 512]
                                , _pub="qev")
                        for t in range(NT_SEQ):
                            par = t % 2
                            s.wait_ge(sms["vg"], C["vg"] + t + 1)
                            sop(s, s.copy, vsb[:, t * HD:(t + 1) * HD], pv2[:, par * 512:par * 512 + 256])
                            sop(s, s.activation, gsb[:, t * HD:(t + 1) * HD], pv2[:, par * 512 + 256:(par + 1) * 512],
                                AF.Silu, _pub="vgev")
                    @blk.vector
                    def _(v):
                        for idx in range(8):
                            qc, which = idx // 2, idx % 2
                            slot = qc % 2
                            semn = [("tqa", "tqb"), ("tka", "tkb")][which][slot]
                            v.wait_ge(sms[semn], C[semn] + 16 * (qc // 2 + 1))
                            v.wait_ge(sms["tq"], C["tq"] + 16 * (idx + 1))
                            tab = tabq if which == 0 else tabk
                            vop(v, v.scalar_tensor_tensor, out=tab[:, slot * 1024:(slot + 1) * 1024],
                                in0=lostage[:], scalar=2.0 ** -40,
                                in1=tab[:, slot * 1024:(slot + 1) * 1024],
                                op0=Op.mult, op1=Op.add, _pub="tcb")
                            v.wait_ge(sms["rot"], C["rot"] + idx + 1)
                            v.wait_ge(sms["qev"], C["qev"] + 2 * qc + which + 1)
                            par = idx % 2
                            tab = tabq if which == 0 else tabk
                            src = qsb if which == 0 else ksb
                            dst = qD if which == 0 else kE
                            sl = slice(qc * 512, (qc + 1) * 512)
                            vop(v, v.tensor_tensor, out=trot[:], in0=pr[:, par * 512:(par + 1) * 512],
                                in1=tab[:, slot * 1024 + 512:slot * 1024 + 1024], op=Op.mult)
                            vop(v, v.tensor_tensor, out=dst[:, sl], in0=src[:, sl],
                                in1=tab[:, slot * 1024:slot * 1024 + 512], op=Op.mult)
                            vop(v, v.tensor_tensor, out=dst[:, sl], in0=dst[:, sl], in1=trot[:],
                                op=Op.add, _pub="qd")
                with (
                    nc.psum_tensor(un("ps"), [128, 2 * 512], F32) as ps,
                    nc.psum_tensor(un("po"), [128, 4 * 512], F32) as po,
                    nc.psum_tensor(un("pc"), [128, 512], F32) as pc,
                ):
                    @blk.tensor
                    def _(t_):
                        t_.wait_ge(sms["qd"], C["qd"] + 8)
                        t_.wait_ge(sms["vgev"], C["vgev"] + NT_SEQ)
                        sn = 0
                        for qc in range(4):
                            nkt = (qc + 1) * 4
                            if qc >= 1:
                                t_.wait_ge(sms["yev"], C["yev"] + qc)
                            for kt in range(nkt):
                                par = sn % 2
                                if sn >= 2:
                                    t_.wait_ge(sms["sev"], C["sev"] + sn - 1)
                                pub(t_.matmul(ps[:, par * 512:(par + 1) * 512], kE[:, kt * 128:(kt + 1) * 128],
                                              qD[:, qc * 512:(qc + 1) * 512], start=True, stop=True), "smm")
                                t_.wait_ge(sms["sev"], C["sev"] + sn + 1)
                                last_po = t_.matmul(pc[0:1, :512], ones1[:], sT[par][:], start=(kt == 0), stop=(kt == nkt - 1))
                                for isub in range(4):
                                    if kt <= qc * 4 + isub:
                                        last_po = t_.matmul(po[:, isub * 512:isub * 512 + 256],
                                                            sT[par][:, isub * 128:(isub + 1) * 128],
                                                            vsb[:, kt * HD:(kt + 1) * HD],
                                                            start=(kt == 0), stop=(kt == qc * 4 + isub))
                                sn += 1
                            pub(last_po, "omm")
                    @blk.vector
                    def _(v):
                        sn = 0
                        for qc in range(4):
                            nkt = (qc + 1) * 4
                            for kt in range(nkt):
                                par = sn % 2
                                v.wait_ge(sms["smm"], C["smm"] + sn + 1)
                                delta = kt - qc * 4
                                if delta >= 0:
                                    vop(v, v.tensor_tensor, out=sT[par][:], in0=ps[:, par * 512:(par + 1) * 512],
                                        in1=masks[:, delta * 512:(delta + 1) * 512], op=Op.mult, _pub="sev")
                                else:
                                    vop(v, v.tensor_copy, sT[par][:], ps[:, par * 512:(par + 1) * 512], _pub="sev")
                                sn += 1
                            v.wait_ge(sms["omm"], C["omm"] + qc + 1)
                            vop(v, v.tensor_copy, csb[0:1, qc * 512:(qc + 1) * 512], pc[0:1, :512])
                            vop(v, v.tensor_copy, ysb[:, qc * 4 * HD:(qc + 1) * 4 * HD].rearrange("p (i w) -> p i w", w=256),
                                po[:].rearrange("p (i z w) -> p i (z w)", z=2, w=256)[:, :, 0:256], _pub="yev")
                with (
                    nc.psum_tensor(un("pt2"), [128, 512], BF16, is_transpose_dest=True) as pt2,
                    nc.psum_tensor(un("pct"), [128, 512], F32) as pct,
                    nc.psum_tensor(un("pdl"), [128, 2 * 1024], F32) as pdl,
                ):
                    @blk.tensor
                    def _(t_):
                        t_.wait_ge(sms["yev"], C["yev"] + 4)
                        for i in range(NT_SEQ):
                            ins = t_.matmul(pct[:, i:i + 1], csb[0:1, i * 128:(i + 1) * 128], ones1[0:1, 0:1],
                                            start=True, stop=True)
                        pub(ins, "csum")
                        for t in range(NT_SEQ):
                            t_.wait_ge(sms["gat"], C["gat"] + t + 1)
                            par = t % 2
                            if t >= 2:
                                t_.wait_ge(sms["gtev"], C["gtev"] + t - 1)
                            for k in range(2):
                                ins = t_.transpose(pt2[:, par * 256 + k * 128:par * 256 + (k + 1) * 128],
                                                   gat[:, par * HD + k * 128:par * HD + (k + 1) * 128], ident[:])
                            pub(ins, "gtr")
                            t_.wait_ge(sms["gtev"], C["gtev"] + t + 1)
                            if t >= 2:
                                t_.wait_ge(sms["dlev"], C["dlev"] + t - 1)
                            for k in range(2):
                                for half in range(2):
                                    ins = t_.matmul(pdl[:, par * 1024 + half * 512:par * 1024 + (half + 1) * 512],
                                                    gatT[:, par * 256 + k * 128:par * 256 + (k + 1) * 128],
                                                    woh[:, k * D + half * 512:k * D + (half + 1) * 512],
                                                    start=(k == 0), stop=(k == 1))
                            pub(ins, "dl")
                    @blk.vector
                    def _(v):
                        v.wait_ge(sms["csum"], C["csum"] + 1)
                        vop(v, v.tensor_copy, cT[:, :NT_SEQ], pct[:, :NT_SEQ])
                        vop(v, v.tensor_reduce, out=stA[:, :NT_SEQ], in_=cT[:, :NT_SEQ].unsqueeze(2), axis=AX.X,
                            op=Op.max, apply_absolute_value=True)
                        vop(v, v.tensor_scalar, out=stB[:, :NT_SEQ], in0=stA[:, :NT_SEQ], scalar1=1.0, scalar2=None, op0=Op.max)
                        vop(v, v.reciprocal, stA[:, :NT_SEQ], stB[:, :NT_SEQ])
                        vop(v, v.tensor_tensor, out=stB[:, :NT_SEQ], in0=stA[:, :NT_SEQ], in1=stA[:, :NT_SEQ], op=Op.mult)
                        v.wait_ge(sms["phi"], C["phi"] + 1)
                        vop(v, v.tensor_tensor, out=stC[:, :NT_SEQ], in0=stB[:, :NT_SEQ], in1=yss[:, :NT_SEQ], op=Op.mult)
                        vop(v, v.tensor_scalar, out=stB[:, :NT_SEQ], in0=stC[:, :NT_SEQ], scalar1=1.0 / HD, scalar2=EPS,
                            op0=Op.mult, op1=Op.add, _pub="aux")
                        v.wait_ge(sms["std"], C["std"] + 1)
                        vop(v, v.reciprocal, stC[:, :NT_SEQ], stB[:, :NT_SEQ])
                        vop(v, v.tensor_tensor, out=phi[:, :NT_SEQ], in0=stA[:, :NT_SEQ], in1=stC[:, :NT_SEQ], op=Op.mult)
                        for t in range(NT_SEQ):
                            par = t % 2
                            if t >= 2:
                                v.wait_ge(sms["gtr"], C["gtr"] + t - 1)
                            vop(v, v.scalar_tensor_tensor, out=gat[:, par * HD:(par + 1) * HD],
                                in0=ysb[:, t * HD:(t + 1) * HD], scalar=phi[:, t:t + 1],
                                in1=gsb[:, t * HD:(t + 1) * HD], op0=Op.mult, op1=Op.mult, _pub="gat")
                            if t >= 1:
                                v.wait_ge(sms["dlev"], C["dlev"] + t)
                                tile = sbase + t - 1
                                vop(v, v.tensor_tensor, out=R[:, tile * D:(tile + 1) * D],
                                    in0=R[:, tile * D:(tile + 1) * D], in1=dlb[:], op=Op.add, _pub="radd")
                        v.wait_ge(sms["dlev"], C["dlev"] + NT_SEQ)
                        tile = sbase + NT_SEQ - 1
                        vop(v, v.tensor_tensor, out=R[:, tile * D:(tile + 1) * D],
                            in0=R[:, tile * D:(tile + 1) * D], in1=dlb[:], op=Op.add, _pub="radd")
                    @blk.scalar
                    def _(s):
                        s.wait_ge(sms["yev"], C["yev"] + 4)
                        for t in range(NT_SEQ):
                            sop(s, s.activation, qb["b"][:, :HD], ysb[:, t * HD:(t + 1) * HD], AF.Square,
                                accum_out=yss[:, t:t + 1],
                                _pub="phi" if t == NT_SEQ - 1 else None)
                        s.wait_ge(sms["aux"], C["aux"] + 1)
                        sop(s, s.activation, stB[:, :NT_SEQ], stB[:, :NT_SEQ], AF.Sqrt, _pub="std")
                        for t in range(NT_SEQ):
                            par = t % 2
                            s.wait_ge(sms["gtr"], C["gtr"] + t + 1)
                            sop(s, s.copy, gatT[:, par * 256:(par + 1) * 256], pt2[:, par * 256:(par + 1) * 256]
                                , _pub="gtev")
                            s.wait_ge(sms["dl"], C["dl"] + t + 1)
                            if t >= 1:
                                s.wait_ge(sms["radd"], C["radd"] + t)
                            sop(s, s.copy, dlb[:], pdl[:, par * 1024:(par + 1) * 1024], _pub="dlev")
            for _w in ("tqa", "tqb", "tka", "tkb"):
                C[_w] += 32
            C["tq"] += 16 * 8
            C["tcb"] += 8
            for _w in ("lwq", "lwk", "lwv", "lwg", "lwo"):
                C[_w] += 16
            C["qmm"] += 8; C["qev"] += 8; C["rot"] += 8; C["qd"] += 8
            C["vg"] += NT_SEQ; C["vgev"] += NT_SEQ
            C["smm"] += 40; C["sev"] += 40; C["omm"] += 4; C["yev"] += 4
            C["csum"] += 1; C["phi"] += 1; C["std"] += 1; C["aux"] += 1
            C["gat"] += NT_SEQ; C["gtr"] += NT_SEQ; C["gtev"] += NT_SEQ
            C["dl"] += NT_SEQ; C["dlev"] += NT_SEQ; C["radd"] += NT_SEQ

        def emit_attn():
            import contextlib as _ctx
            with _ctx.ExitStack() as ast_:
                asb = lambda name, shape, ty: ast_.enter_context(nc.sbuf_tensor(un(name), shape, ty))
                AB["xaT"] = asb("xaT", [128, 8 * S], BF16)
                AB["wqh"] = asb("wqh", [128, 8 * 128], BF16)
                AB["wkh"] = asb("wkh", [128, 8 * 128], BF16)
                AB["wvh"] = asb("wvh", [128, 8 * HD], BF16)
                AB["wgh"] = asb("wgh", [128, 8 * HD], BF16)
                AB["woh"] = asb("woh", [128, 2 * D], BF16)
                AB["tabq"] = asb("tabq", [128, 2 * 1024], F32)
                AB["tabk"] = asb("tabk", [128, 2 * 1024], F32)
                AB["qsb"] = asb("qsb", [128, S], BF16)
                AB["ksb"] = asb("ksb", [128, S], BF16)
                AB["trot"] = asb("trot", [128, 512], BF16)
                AB["lostage"] = asb("lostage", [128, 1024], BF16)
                AB["vsb"] = asb("vsb", [128, NT_SEQ * HD], BF16)
                AB["gsb"] = asb("gsb", [128, NT_SEQ * HD], BF16)
                AB["ysb"] = asb("ysb", [128, NT_SEQ * HD], BF16)
                AB["sT"] = [asb(f"sT{i}", [128, 512], BF16) for i in range(2)]
                AB["csb"] = asb("csb", [128, 2048], BF16)
                AB["cT"] = asb("cT", [128, NT_SEQ], F32)
                AB["phi"] = asb("phi", [128, NT_SEQ], F32)
                AB["yss"] = asb("yss", [128, NT_SEQ], F32)
                AB["gat"] = asb("gat", [128, 2 * HD], BF16)
                AB["gatT"] = asb("gatT", [128, 2 * 256], BF16)
                AB["dlb"] = asb("dlb", [128, D], BF16)
                for s_i in range(int(_os.environ.get("ATTNS", SEQ_PER_CORE))):
                    sbase = s_i * NT_SEQ
                    attn_ln(sbase)
                    for h in range(int(_os.environ.get("ATTNH", NH))):
                        emit_head(s_i, h, sbase)

        def emit_dump(i):
            with nc.Block() as blk:
                @blk.gpsimd
                def _(g):
                    g.dma_start(dbg_d[i].rearrange("(t p) d -> p t d", p=128),
                                R[:].rearrange("p (t d) -> p t d", d=D)).then_inc(sms["store"], 16)
                    g.wait_ge(sms["store"], C["store"] + 16)
            C["store"] += 16

        if upto >= 2:
            emit_ffn()
        if dump:
            emit_dump(0)
        if upto >= 3:
            emit_attn()
        if dump:
            emit_dump(1)
        if upto >= 4:
            emit_ffn()
        if dump:
            emit_dump(2)
        with nc.Block() as blk:
            @blk.gpsimd
            def _(g):
                g.dma_start(out_d.rearrange("(t p) d -> p t d", p=128),
                            R[:].rearrange("p (t d) -> p t d", d=D)).then_inc(sms["store"], 16)
                g.wait_ge(sms["store"], C["store"] + 16)
        C["store"] += 16
    return nc


def make_in_maps(L0_attn_x, L0_attn_wq, L0_attn_wk, L0_attn_wv, L0_attn_wg, L0_attn_wo,
                 L0_ffn_w_gate, L0_ffn_w_up, L0_ffn_w_down):
    rotq, rotk, rotqlo, rotklo, masks, rmat = _host_tables()
    x = np.ascontiguousarray(L0_attn_x, dtype=np.float32).reshape(B, S, D)
    in_maps = []
    for core in range(NCORES):
        xs = np.ascontiguousarray(x[core * SEQ_PER_CORE:(core + 1) * SEQ_PER_CORE].reshape(T, D))
        in_maps.append({
            "x": xs,
            "wq": np.ascontiguousarray(L0_attn_wq, dtype=np.float32),
            "wk": np.ascontiguousarray(L0_attn_wk, dtype=np.float32),
            "wv": np.ascontiguousarray(L0_attn_wv, dtype=np.float32),
            "wgt": np.ascontiguousarray(L0_attn_wg, dtype=np.float32),
            "wo": np.ascontiguousarray(L0_attn_wo, dtype=np.float32),
            "wfg": np.ascontiguousarray(L0_ffn_w_gate, dtype=np.float32),
            "wfu": np.ascontiguousarray(L0_ffn_w_up, dtype=np.float32),
            "wfd": np.ascontiguousarray(L0_ffn_w_down, dtype=np.float32),
            "rotq": rotq, "rotk": rotk, "rotqlo": rotqlo, "rotklo": rotklo,
            "masks": masks, "rmat": rmat,
        })
    return in_maps


def kernel(L0_attn_x, L0_attn_wq, L0_attn_wk, L0_attn_wv, L0_attn_wg, L0_attn_wo,
           L0_ffn_w_gate, L0_ffn_w_up, L0_ffn_w_down):
    nc = build_nc(dump=False)
    in_maps = make_in_maps(L0_attn_x, L0_attn_wq, L0_attn_wk, L0_attn_wv, L0_attn_wg,
                           L0_attn_wo, L0_ffn_w_gate, L0_ffn_w_up, L0_ffn_w_down)
    res = run_bass_kernel_spmd(nc, in_maps, core_ids=list(range(NCORES)))
    shards = [np.asarray(res.results[core]["out"]) for core in range(NCORES)]
    out = np.empty((B, S, D), dtype=shards[0].dtype)
    for core in range(NCORES):
        out[core * SEQ_PER_CORE:(core + 1) * SEQ_PER_CORE] = shards[core].reshape(SEQ_PER_CORE, S, D)
    return out


# version 08
# speedup vs baseline: 1.0709x; 1.0709x over previous
import numpy as np
import ml_dtypes
import concourse.bass as bass
import concourse.mybir as mybir
from concourse.bass_utils import run_bass_kernel_spmd
from concourse.alu_op_type import AluOpType as Op

F32 = mybir.dt.float32
BF16 = mybir.dt.bfloat16
I16 = mybir.dt.int16
AF = mybir.ActivationFunctionType
AX = mybir.AxisListType

NCORES = 8
B, S, D = 16, 2048, 1024
SEQ_PER_CORE = B // NCORES
T = SEQ_PER_CORE * S
NT = T // 128
NT_SEQ = S // 128
IF = 2816
NH, KD, HD, VD = 8, 128, 256, 2048
EPS = 1e-5
SC_TILES = 8
N_SC = NT // SC_TILES
FFN_GROUPS = [6, 6, 5, 5]
FFN_GOFF = [0, 6, 12, 17]
GSUB = {6: [512, 256], 5: [512, 128]}
BF = np.dtype(ml_dtypes.bfloat16)


def _host_tables():
    lin = np.linspace(0.0, 1.0, KD // 2)
    angle = np.repeat(1.0 / 10000.0 ** lin, 2)
    ar32 = np.arange(NH, dtype=np.float32)
    decay = np.log1p(-(np.float32(2.0) ** (np.float32(-5.0) - ar32)).astype(np.float32)
                     ).astype(np.float32).astype(np.float64)
    pos = np.arange(S, dtype=np.float64)
    sin = np.sin(pos[:, None] * angle[None, :])
    cos = np.cos(pos[:, None] * angle[None, :])
    e32 = np.exp(decay[:, None] * np.arange(S)[None, :]).astype(np.float32).astype(np.float64)
    nm = 1.0 / np.sqrt(np.cumsum(e32, axis=1))
    Dp = np.exp(pos[None, :] * decay[:, None]) * nm
    Ep = np.exp(-pos[None, :] * decay[:, None]) * (KD ** -0.5)
    rotq = np.empty((NH, 2, KD, S), dtype=np.float32)
    rotk = np.empty((NH, 2, KD, S), dtype=np.float32)
    rq64 = np.empty((NH, 2, KD, S), dtype=np.float64)
    rk64 = np.empty((NH, 2, KD, S), dtype=np.float64)
    for h in range(NH):
        rq64[h, 0] = cos.T * Dp[h][None, :]
        rq64[h, 1] = sin.T * Dp[h][None, :]
        rk64[h, 0] = cos.T * Ep[h][None, :]
        rk64[h, 1] = sin.T * Ep[h][None, :]
    rotq[:] = rq64.astype(np.float32)
    rotk[:] = rk64.astype(np.float32)
    LOSCALE = 2.0 ** 40
    rotqlo = ((rq64 - rotq.astype(np.float64)) * LOSCALE).astype(BF)
    rotklo = ((rk64 - rotk.astype(np.float64)) * LOSCALE).astype(BF)
    masks = np.zeros((4, 128, 512), dtype=BF)
    jj = np.arange(128)[:, None]
    ii = np.arange(512)[None, :]
    for d in range(4):
        masks[d] = (jj <= ii - 128 * d).astype(BF)
    rmat = np.zeros((KD, KD), dtype=BF)
    for m in range(KD // 2):
        rmat[2 * m + 1, 2 * m] = -1.0
        rmat[2 * m, 2 * m + 1] = 1.0
    return rotq, rotk, rotqlo, rotklo, masks, rmat


def _r3(ap, k=16):
    return ap.rearrange("p (b k) -> p b k", k=k)


def build_nc(dump=False, upto=99):
    import os as _os
    import contextlib
    import itertools
    nc = bass.Bass("TRN2", target_bir_lowering=False)
    dt = lambda name, shape, ty, kind: nc.dram_tensor(name, shape, ty, kind=kind).ap()
    x_in = dt("x", [T, D], F32, "ExternalInput")
    wq_d = dt("wq", [D, D], F32, "ExternalInput")
    wk_d = dt("wk", [D, D], F32, "ExternalInput")
    wv_d = dt("wv", [D, VD], F32, "ExternalInput")
    wgt_d = dt("wgt", [D, VD], F32, "ExternalInput")
    wo_d = dt("wo", [VD, D], F32, "ExternalInput")
    wfg_d = dt("wfg", [IF, D], F32, "ExternalInput")
    wfu_d = dt("wfu", [IF, D], F32, "ExternalInput")
    wfd_d = dt("wfd", [D, IF], F32, "ExternalInput")
    rotq_d = dt("rotq", [NH, 2, KD, S], F32, "ExternalInput")
    rotk_d = dt("rotk", [NH, 2, KD, S], F32, "ExternalInput")
    rotqlo_d = dt("rotqlo", [NH, 2, KD, S], BF16, "ExternalInput")
    rotklo_d = dt("rotklo", [NH, 2, KD, S], BF16, "ExternalInput")
    masks_d = dt("masks", [4, 128, 512], BF16, "ExternalInput")
    rmat_d = dt("rmat", [KD, KD], BF16, "ExternalInput")
    out_d = dt("out", [T, D], F32, "ExternalOutput")
    wfgqT = dt("wfgqT", [D, IF], BF16, "Internal")
    wfuqT = dt("wfuqT", [D, IF], BF16, "Internal")
    wfdqT = dt("wfdqT", [IF, D], BF16, "Internal")
    dbg_d = [dt(f"dbg{i}", [T, D], F32, "ExternalOutput") for i in range(3)] if dump else None

    _uc = itertools.count()
    un = lambda n: f"{n}_{next(_uc)}"
    with contextlib.ExitStack() as st:
        sb = lambda name, shape, ty: st.enter_context(nc.sbuf_tensor(un(name), shape, ty))
        R = sb("R", [128, NT * D], BF16)
        ident = sb("ident", [128, 128], BF16)
        identi = sb("identi", [128, 128], mybir.dt.int32)
        rmat = sb("rmat", [128, 128], BF16)
        masks = sb("masks", [128, 4 * 512], BF16)
        ones1 = sb("ones1", [128, 1], BF16)
        msum = sb("msum", [128, NT_SEQ], F32)
        ssq = sb("ssq", [128, NT_SEQ], F32)
        negm = sb("negm", [128, NT_SEQ], F32)
        rstd = sb("rstd", [128, NT_SEQ], F32)
        stA = sb("stA", [128, NT_SEQ], F32)
        stB = sb("stB", [128, NT_SEQ], F32)
        stC = sb("stC", [128, NT_SEQ], F32)
        bm = sb("bm", [128, 192], F32)
        scq = sb("scq", [128, 192], F32)
        hscq = sb("hscq", [128, 192], F32)
        rscq = sb("rscq", [128, 192], F32)
        sc2 = sb("sc2", [128, 192], F32)
        QW = 1024
        qb = dict(
            xn=sb("q_xn", [128, 2 * QW], F32),
            a=sb("q_a", [128, QW], F32),
            sg=sb("q_sg", [128, QW], BF16),
            b=sb("q_b", [128, QW], BF16), c=sb("q_c", [128, QW], BF16),
        )
        xqtile = sb("xqtile", [128, D], BF16)
        t1a = xqtile
        T1B = {}
        SEMS = ["dma", "dmg", "ssq", "stats", "std", "rstd", "t1", "xq", "xqtr",
                "xqev", "gu", "guev", "hq", "hread", "hqtr", "hqev", "down", "radd",
                "aux", "at1", "atr", "atev", "qmm", "qev", "rot", "qd",
                "vg", "vgev", "smm", "sev", "omm", "yev", "csum", "phi", "gat",
                "gtr", "gtev", "dl", "dlev", "store", "vch", "sch",
                "xin", "cst1", "cst2", "wta", "wua", "wda", "wtb", "wub", "wdb",
                "tq", "tk", "lwq", "lwk", "lwv", "lwg", "lwo", "qxn", "qab",
                "tqa", "tqb", "tka", "tkb", "tcb", "sevs", "qpd"]
        sms = {n: st.enter_context(nc.semaphore("sem_" + n)) for n in SEMS}
        C = {n: 0 for n in SEMS}
        LIVE = {n: 0 for n in SEMS}
        CHV = {"sem": "vch", "val": 0}
        CHS = {"sem": "sch", "val": 0}

        def pub(ins, name, k=1):
            ins.then_inc(sms[name], k)
            LIVE[name] += k
            return ins

        def gpub(ins, name, k=1):
            return pub(ins, name, k)

        def vop(v, meth, *a, _pub=None, **kw):
            v.wait_ge(sms[CHV["sem"]], CHV["val"])
            ins = meth(*a, **kw)
            nm = _pub if _pub is not None else "vch"
            pub(ins, nm, 1)
            CHV["sem"], CHV["val"] = nm, LIVE[nm]
            return ins

        def sop(s, meth, *a, _pub=None, **kw):
            s.wait_ge(sms[CHS["sem"]], CHS["val"])
            ins = meth(*a, **kw)
            nm = _pub if _pub is not None else "sch"
            pub(ins, nm, 1)
            CHS["sem"], CHS["val"] = nm, LIVE[nm]
            return ins

        QCV = [0]
        QCS = [0]

        def quant_pre(v, src_ap, nf, rs_b, pub_first=None):
            qi = QCV[0]
            QCV[0] += 1
            par = qi % 2
            xn = qb["xn"][:, par * QW:par * QW + nf]
            if pub_first is None:
                vop(v, v.tensor_tensor, out=_r3(xn), in0=_r3(src_ap), in1=rs_b, op=Op.mult, _pub="qxn")
            else:
                vop(v, v.tensor_tensor, out=_r3(xn), in0=_r3(src_ap), in1=rs_b, op=Op.mult, _pub=pub_first)
                vop(v, v.tensor_copy, qb["c"][:, 0:1], xn[:, 0:1], _pub="qxn")
            return qi

        def quant_post(v, out_ap, nf, hs_b, qi, pub_last=None):
            b_, c_ = qb["b"], qb["c"]
            A = qb["a"][:, :nf]
            SG = qb["sg"][:, :nf]
            v.wait_ge(sms["qab"], qi + 1)
            vop(v, v.tensor_scalar, out=b_[:, :nf], in0=A, scalar1=0.25, scalar2=None, op0=Op.is_gt)
            for thr, w in ((0.75, None), (1.25, None), (1.75, None), (2.5, 2.0), (3.5, 2.0), (5.0, 4.0)):
                if w is None:
                    vop(v, v.tensor_scalar, out=c_[:, :nf], in0=A, scalar1=thr, scalar2=None, op0=Op.is_gt)
                else:
                    vop(v, v.tensor_scalar, out=c_[:, :nf], in0=A, scalar1=thr, scalar2=w, op0=Op.is_gt, op1=Op.mult)
                vop(v, v.tensor_tensor, out=b_[:, :nf], in0=b_[:, :nf], in1=c_[:, :nf], op=Op.add)
            vop(v, v.tensor_tensor, out=b_[:, :nf], in0=b_[:, :nf], in1=SG, op=Op.mult, _pub="qpd")
            return vop(v, v.tensor_tensor, out=_r3(out_ap), in0=_r3(b_[:, :nf]), in1=hs_b, op=Op.mult, _pub=pub_last)

        def quant_chain(v, src_ap, out_ap, nf, rs_b, hs_b, u1_buf=None, pub_first=None, pub_last=None):
            qi = quant_pre(v, src_ap, nf, rs_b, pub_first=pub_first)
            return quant_post(v, out_ap, nf, hs_b, qi, pub_last=pub_last)

        def quant_abs(s, nf):
            qi = QCS[0]
            QCS[0] += 1
            par = qi % 2
            s.wait_ge(sms["qxn"], qi + 1)
            if qi >= 1:
                s.wait_ge(sms["qpd"], qi)
            sop(s, s.activation, qb["a"][:, :nf], qb["xn"][:, par * QW:par * QW + nf], AF.Abs)
            sop(s, s.activation, qb["sg"][:, :nf], qb["xn"][:, par * QW:par * QW + nf], AF.Sign, _pub="qab")

        def quant_scale(v, bmn, nb, par=0):
            o = par * 96
            vop(v, v.tensor_scalar, out=scq[:, o:o + nb], in0=bmn, scalar1=1.0 / 6.0, scalar2=1e-12, op0=Op.mult, op1=Op.max)
            vop(v, v.tensor_scalar, out=hscq[:, o:o + nb], in0=scq[:, o:o + nb], scalar1=0.5, scalar2=None, op0=Op.mult)
            vop(v, v.reciprocal, rscq[:, o:o + nb], scq[:, o:o + nb])

        with nc.Block() as blk:
            @blk.gpsimd
            def _(g):
                gpub(g.iota(identi[:], [[1, 128]], base=0, channel_multiplier=-1), "aux")
                xi = x_in.rearrange("(t p) d -> p t d", p=128)
                Ri = R[:].rearrange("p (t d) -> p t d", d=D)
                g.dma_start(Ri, xi).then_inc(sms["xin"], 16)
            @blk.sync
            def _(sy):
                sy.dma_start(rmat[:], rmat_d).then_inc(sms["cst1"], 16)
                sy.dma_start(_r3(masks[:], k=512), masks_d.transpose([1, 0, 2])).then_inc(sms["cst2"], 16)
            @blk.vector
            def _(v):
                vop(v, v.memset, ones1[:], 1.0)
                v.wait_ge(sms["aux"], C["aux"] + 1)
                vop(v, v.tensor_scalar, out=ident[:], in0=identi[:], scalar1=0, scalar2=None, op0=Op.is_equal)
            @blk.scalar
            def _(s):
                s.wait_ge(sms["xin"], 16)
                s.wait_ge(sms["cst1"], 16)
                s.wait_ge(sms["cst2"], 16)
        C["xin"] += 16
        C["cst1"] += 16
        C["cst2"] += 16
        C["aux"] += 1

        prep = [(wfg_d, wfgqT, 22, D), (wfu_d, wfuqT, 22, D), (wfd_d, wfdqT, 8, IF)]
        if upto < 1:
            prep = []
        nprep = sum(p[2] for p in prep)
        with nc.Block() as blk:
            with (
                nc.psum_tensor(un("ptrW"), [128, IF], BF16, is_transpose_dest=True) as ptrW,
                nc.sbuf_tensor(un("wld"), [128, IF], F32) as wld,
                nc.sbuf_tensor(un("wqt"), [128, IF], BF16) as wqt,
                nc.sbuf_tensor(un("wtr"), [128, IF], BF16) as wtr,
            ):
                @blk.gpsimd
                def _(g):
                    n = 0
                    for src, dst, nchunk, width in prep:
                        for ch in range(nchunk):
                            if n >= 1:
                                g.wait_ge(sms["aux"], C["aux"] + n)
                            g.dma_start(wld[:, :width], src[ch * 128:(ch + 1) * 128, :]).then_inc(sms["dmg"], 16)
                            n += 1
                @blk.vector
                def _(v):
                    n = 0
                    for src, dst, nchunk, width in prep:
                        for ch in range(nchunk):
                            v.wait_ge(sms["dmg"], C["dmg"] + 16 * (n + 1))
                            if n >= 1:
                                v.wait_ge(sms["xqtr"], C["xqtr"] + n)
                            pieces = [(0, width)] if width <= QW else [(o, 704) for o in range(0, width, 704)]
                            nball = width // 16
                            vop(v, v.tensor_reduce, out=bm[:, :nball], in_=_r3(wld[:, :width]), axis=AX.X,
                                op=Op.max, apply_absolute_value=True)
                            quant_scale(v, bm[:, :nball], nball)
                            for pi, (off, w2) in enumerate(pieces):
                                nb = w2 // 16
                                rs_b = rscq[:, off // 16:off // 16 + nb].unsqueeze(2).broadcast_to([128, nb, 16])
                                hs_b = hscq[:, off // 16:off // 16 + nb].unsqueeze(2).broadcast_to([128, nb, 16])
                                quant_chain(v, wld[:, off:off + w2], wqt[:, off:off + w2], w2, rs_b, hs_b,
                                            u1_buf=qb["a"][:, :w2],
                                            pub_first="aux" if pi == len(pieces) - 1 else None,
                                            pub_last="xq" if pi == len(pieces) - 1 else None)
                            n += 1
                @blk.tensor
                def _(t_):
                    n = 0
                    for src, dst, nchunk, width in prep:
                        ntr = width // 128
                        for ch in range(nchunk):
                            t_.wait_ge(sms["xq"], C["xq"] + n + 1)
                            if n >= 1:
                                t_.wait_ge(sms["xqev"], C["xqev"] + n)
                            for k in range(ntr):
                                ins = t_.transpose(ptrW[:, k * 128:(k + 1) * 128], wqt[:, k * 128:(k + 1) * 128], ident[:])
                            pub(ins, "xqtr")
                            n += 1
                @blk.scalar
                def _(s):
                    n = 0
                    for src, dst, nchunk, width in prep:
                        ntr = width // 128
                        for ch in range(nchunk):
                            npieces = 1 if width <= QW else width // 704
                            for pi in range(npieces):
                                quant_abs(s, width if width <= QW else 704)
                            s.wait_ge(sms["xqtr"], C["xqtr"] + n + 1)
                            if n >= 1:
                                s.wait_ge(sms["dma"], C["dma"] + 16 * n)
                            sop(s, s.copy, wtr[:, :ntr * 128], ptrW[:, :ntr * 128], _pub="xqev")
                            n += 1
                @blk.sync
                def _(sy):
                    n = 0
                    for src, dst, nchunk, width in prep:
                        ntr = width // 128
                        for ch in range(nchunk):
                            sy.wait_ge(sms["xqev"], C["xqev"] + n + 1)
                            dsti = dst[0:width, ch * 128:(ch + 1) * 128].rearrange("(k p) c -> p k c", p=128)
                            sy.dma_start(dsti, wtr[:, :ntr * 128].rearrange("p (k c) -> p k c", c=128)).then_inc(sms["dma"], 16)
                            n += 1
                    if n:
                        sy.wait_ge(sms["dma"], C["dma"] + 16 * n)
        C["dmg"] += 16 * nprep
        C["xq"] += nprep
        C["aux"] += nprep
        C["xqtr"] += nprep
        C["xqev"] += nprep
        C["dma"] += 16 * nprep

        WG_MAX = 6
        FB = {}

        def ffn_stage_a(tbase):
            xqT = FB["xqT"]
            t1b = T1B["b"]
            with nc.Block() as blk:
                with nc.psum_tensor(un("ptrA"), [128, 2 * 8 * 128], BF16, is_transpose_dest=True) as ptrA:
                    @blk.vector
                    def _(v):
                        nco = SC_TILES
                        for t in range(SC_TILES):
                            vop(v, v.tensor_reduce, out=msum[:, t:t + 1], in_=R[:, (tbase + t) * D:(tbase + t + 1) * D],
                                axis=AX.X, op=Op.add)
                        v.wait_ge(sms["ssq"], C["ssq"] + 1)
                        vop(v, v.tensor_scalar, out=negm[:, :nco], in0=msum[:, :nco], scalar1=-1.0 / D, scalar2=None, op0=Op.mult)
                        vop(v, v.tensor_scalar, out=stA[:, :nco], in0=ssq[:, :nco], scalar1=1.0 / D, scalar2=None, op0=Op.mult)
                        vop(v, v.tensor_tensor, out=stB[:, :nco], in0=negm[:, :nco], in1=negm[:, :nco], op=Op.mult)
                        vop(v, v.tensor_tensor, out=stC[:, :nco], in0=stA[:, :nco], in1=stB[:, :nco], op=Op.subtract)
                        vop(v, v.tensor_scalar, out=stA[:, :nco], in0=stC[:, :nco], scalar1=EPS, scalar2=None,
                            op0=Op.add, _pub="stats")
                        v.wait_ge(sms["std"], C["std"] + 1)
                        vop(v, v.reciprocal, rstd[:, :nco], stB[:, :nco], _pub="rstd")
                        nb = D // 16
                        qis = []
                        pars = []

                        def a_post(j):
                            if j >= 1:
                                v.wait_ge(sms["xqtr"], C["xqtr"] + j)
                            o = pars[j] * 96
                            hs_b = hscq[:, o:o + nb].unsqueeze(2).broadcast_to([128, nb, 16])
                            quant_post(v, xqtile[:], D, hs_b, qis[j], pub_last="xq")

                        for t in range(SC_TILES):
                            v.wait_ge(sms["t1"], C["t1"] + t + 1)
                            par = QCV[0] % 2
                            o = par * 96
                            vop(v, v.tensor_reduce, out=bm[:, :nb], in_=_r3(t1b[:]), axis=AX.X,
                                op=Op.max, apply_absolute_value=True)
                            vop(v, v.tensor_scalar, out=sc2[:, o:o + nb], in0=bm[:, :nb], scalar1=rstd[:, t:t + 1],
                                scalar2=None, op0=Op.mult)
                            quant_scale(v, sc2[:, o:o + nb], nb, par=par)
                            vop(v, v.tensor_scalar, out=sc2[:, o:o + nb], in0=rscq[:, o:o + nb], scalar1=rstd[:, t:t + 1],
                                scalar2=None, op0=Op.mult)
                            rs_b = sc2[:, o:o + nb].unsqueeze(2).broadcast_to([128, nb, 16])
                            qis.append(quant_pre(v, t1b[:], D, rs_b))
                            pars.append(par)
                            if t >= 1:
                                a_post(t - 1)
                        a_post(SC_TILES - 1)
                    @blk.scalar
                    def _(s):
                        for t in range(SC_TILES):
                            sop(s, s.activation, qb["a"][:, :D], R[:, (tbase + t) * D:(tbase + t + 1) * D],
                                AF.Square, accum_out=ssq[:, t:t + 1],
                                _pub="ssq" if t == SC_TILES - 1 else None)
                        s.wait_ge(sms["stats"], C["stats"] + 1)
                        sop(s, s.activation, stB[:, :SC_TILES], stA[:, :SC_TILES], AF.Sqrt, _pub="std")
                        s.wait_ge(sms["rstd"], C["rstd"] + 1)
                        qbase = QCS[0]
                        for t in range(SC_TILES):
                            if t >= 1:
                                s.wait_ge(sms["qxn"], qbase + t)
                            sop(s, s.activation, t1b[:], R[:, (tbase + t) * D:(tbase + t + 1) * D], AF.Identity,
                                bias=negm[:, t:t + 1], scale=1.0, _pub="t1")
                            quant_abs(s, D)
                            if t >= 1:
                                s.wait_ge(sms["xqtr"], C["xqtr"] + t)
                                par = (t - 1) % 2
                                sop(s, s.copy,
                                    xqT[:].rearrange("p (c x) -> p c x", x=SC_TILES * 128)[:, :, (t - 1) * 128:t * 128],
                                    ptrA[:, par * 1024:(par + 1) * 1024].rearrange("p (c x) -> p c x", x=128)
                                    , _pub="xqev")
                        s.wait_ge(sms["xqtr"], C["xqtr"] + SC_TILES)
                        par = (SC_TILES - 1) % 2
                        sop(s, s.copy,
                            xqT[:].rearrange("p (c x) -> p c x", x=SC_TILES * 128)[:, :, (SC_TILES - 1) * 128:SC_TILES * 128],
                            ptrA[:, par * 1024:(par + 1) * 1024].rearrange("p (c x) -> p c x", x=128)
                            , _pub="xqev")
                    @blk.tensor
                    def _(t_):
                        for t in range(SC_TILES):
                            t_.wait_ge(sms["xq"], C["xq"] + t + 1)
                            if t >= 2:
                                t_.wait_ge(sms["xqev"], C["xqev"] + t - 1)
                            par = t % 2
                            for k in range(8):
                                ins = t_.transpose(ptrA[:, par * 1024 + k * 128:par * 1024 + (k + 1) * 128],
                                                   xqtile[:, k * 128:(k + 1) * 128], ident[:])
                            pub(ins, "xqtr")
            C["ssq"] += 1; C["stats"] += 1; C["std"] += 1; C["rstd"] += 1
            C["t1"] += SC_TILES; C["xq"] += SC_TILES; C["xqtr"] += SC_TILES; C["xqev"] += SC_TILES

        def ffn_stage_b(tbase):
            xqT, wgq, wuq, wdq = FB["xqT"], FB["wgq"], FB["wuq"], FB["wdq"]
            sgt, upt, hqt, hqT = FB["sgt"], FB["upt"], FB["hqt"], FB["hqT"]
            with nc.Block() as blk:
                with (
                    nc.psum_tensor(un("pgu"), [128, 4 * 512], F32) as pgu,
                    nc.psum_tensor(un("pd"), [128, 1024], F32) as pd,
                    nc.psum_tensor(un("ptrB"), [128, WG_MAX * 128], BF16, is_transpose_dest=True) as ptrB,
                ):
                    @blk.sync
                    def _(sy):
                        for g in range(4):
                            if g >= 2:
                                sy.wait_ge(sms["gu"], C["gu"] + (g - 1) * SC_TILES)
                                sy.wait_ge(sms["down"], C["down"] + (g - 1) * SC_TILES)
                            nch = FFN_GROUPS[g]
                            off = FFN_GOFF[g]
                            buf = g % 2
                            src_g = wfgqT[:, off * 128:(off + nch) * 128].rearrange("(c p) w -> p c w", p=128)
                            src_u = wfuqT[:, off * 128:(off + nch) * 128].rearrange("(c p) w -> p c w", p=128)
                            wsem = [("wta", "wua", "wda"), ("wtb", "wub", "wdb")][buf]
                            sy.dma_start(wgq[buf][:, :8 * nch * 128].rearrange("p (c w) -> p c w", w=nch * 128), src_g).then_inc(sms[wsem[0]], 16)
                            sy.dma_start(wuq[buf][:, :8 * nch * 128].rearrange("p (c w) -> p c w", w=nch * 128), src_u).then_inc(sms[wsem[1]], 16)
                            src_d = wfdqT[off * 128:(off + nch) * 128, :].rearrange("(c p) w -> p c w", p=128)
                            sy.dma_start(wdq[buf][:, :nch * 1024].rearrange("p (c w) -> p c w", w=1024), src_d).then_inc(sms[wsem[2]], 16)
                    NDN = 4 * SC_TILES
                    gof = lambda dn: dn // SC_TILES

                    def t_gu(t_, dn):
                        g = gof(dn); nch = FFN_GROUPS[g]; s1 = GSUB[nch][1]
                        t = dn % SC_TILES; buf = g % 2
                        if t == 0:
                            wsem = [("wta", "wua", "wda"), ("wtb", "wub", "wdb")][buf]
                            for _w in wsem:
                                t_.wait_ge(sms[_w], C[_w] + 16 * (g // 2 + 1))
                        if dn >= 1:
                            t_.wait_ge(sms["guev"], C["guev"] + dn)
                        for c in range(8):
                            stat = xqT[:, c * SC_TILES * 128 + t * 128: c * SC_TILES * 128 + (t + 1) * 128]
                            t_.matmul(pgu[:, 0:512], stat,
                                      wgq[buf][:, c * nch * 128: c * nch * 128 + 512],
                                      start=(c == 0), stop=(c == 7))
                            t_.matmul(pgu[:, 512:1024], stat,
                                      wuq[buf][:, c * nch * 128: c * nch * 128 + 512],
                                      start=(c == 0), stop=(c == 7))
                            t_.matmul(pgu[:, 1024:1024 + s1], stat,
                                      wgq[buf][:, c * nch * 128 + 512: c * nch * 128 + 512 + s1],
                                      start=(c == 0), stop=(c == 7))
                            ins = t_.matmul(pgu[:, 1536:1536 + s1], stat,
                                            wuq[buf][:, c * nch * 128 + 512: c * nch * 128 + 512 + s1],
                                            start=(c == 0), stop=(c == 7))
                        pub(ins, "gu")

                    def t_trdown(t_, dn):
                        g = gof(dn); nch = FFN_GROUPS[g]; buf = g % 2
                        t_.wait_ge(sms["hq"], C["hq"] + dn + 1)
                        if dn >= 1:
                            t_.wait_ge(sms["hqev"], C["hqev"] + dn)
                        for k in range(nch):
                            ins = t_.transpose(ptrB[:, k * 128:(k + 1) * 128], hqt[:, k * 128:(k + 1) * 128], ident[:])
                        pub(ins, "hqtr")
                        t_.wait_ge(sms["hqev"], C["hqev"] + dn + 1)
                        if dn >= 1:
                            t_.wait_ge(sms["radd"], C["radd"] + dn)
                        for c in range(nch):
                            for half in range(2):
                                ins = t_.matmul(pd[:, half * 512:(half + 1) * 512],
                                                hqT[:, c * 128:(c + 1) * 128],
                                                wdq[buf][:, c * 1024 + half * 512: c * 1024 + (half + 1) * 512],
                                                start=(c == 0), stop=(c == nch - 1))
                        pub(ins, "down")

                    @blk.tensor
                    def _(t_):
                        t_gu(t_, 0)
                        for dn in range(NDN):
                            if dn + 1 < NDN:
                                t_gu(t_, dn + 1)
                            t_trdown(t_, dn)

                    def s_evac(s, dn):
                        g = gof(dn); nch = FFN_GROUPS[g]; s1 = GSUB[nch][1]
                        s.wait_ge(sms["gu"], C["gu"] + dn + 1)
                        if dn >= 1:
                            s.wait_ge(sms["hread"], C["hread"] + dn)
                        sop(s, s.activation, sgt[:, 0:512], pgu[:, 0:512], AF.Silu)
                        sop(s, s.activation, sgt[:, 512:512 + s1], pgu[:, 1024:1024 + s1], AF.Silu)
                        sop(s, s.copy, upt[:, 0:512], pgu[:, 512:1024])
                        sop(s, s.copy, upt[:, 512:512 + s1], pgu[:, 1536:1536 + s1], _pub="guev")
                        quant_abs(s, nch * 128)

                    @blk.scalar
                    def _(s):
                        s_evac(s, 0)
                        for dn in range(NDN):
                            if dn + 1 < NDN:
                                s_evac(s, dn + 1)
                            nch = FFN_GROUPS[gof(dn)]
                            s.wait_ge(sms["hqtr"], C["hqtr"] + dn + 1)
                            sop(s, s.copy, hqT[:, :nch * 128], ptrB[:, :nch * 128], _pub="hqev")

                    @blk.vector
                    def _(v):
                        qis = []
                        pars = []

                        def b_post(j):
                            wj = FFN_GROUPS[gof(j)] * 128
                            nbj = wj // 16
                            if j >= 1:
                                v.wait_ge(sms["hqtr"], C["hqtr"] + j)
                            o = pars[j] * 96
                            hs_b = hscq[:, o:o + nbj].unsqueeze(2).broadcast_to([128, nbj, 16])
                            quant_post(v, hqt[:, :wj], wj, hs_b, qis[j], pub_last="hq")

                        def b_radd(j):
                            v.wait_ge(sms["down"], C["down"] + j + 1)
                            tile = tbase + j % SC_TILES
                            vop(v, v.scalar_tensor_tensor, out=R[:, tile * D:(tile + 1) * D],
                                in0=pd[:, :1024], scalar=0.5,
                                in1=R[:, tile * D:(tile + 1) * D], op0=Op.mult, op1=Op.add, _pub="radd")

                        for dn in range(NDN):
                            g = gof(dn); nch = FFN_GROUPS[g]
                            width = nch * 128
                            nb = width // 16
                            par = QCV[0] % 2
                            o = par * 96
                            hbuf = FB["hprod"][:, :width]
                            v.wait_ge(sms["guev"], C["guev"] + dn + 1)
                            vop(v, v.tensor_tensor, out=hbuf, in0=sgt[:, :width],
                                in1=upt[:, :width], op=Op.mult, _pub="hread")
                            vop(v, v.tensor_reduce, out=bm[:, :nb], in_=_r3(hbuf), axis=AX.X,
                                op=Op.max, apply_absolute_value=True)
                            quant_scale(v, bm[:, :nb], nb, par=par)
                            rs_b = rscq[:, o:o + nb].unsqueeze(2).broadcast_to([128, nb, 16])
                            qis.append(quant_pre(v, hbuf, width, rs_b))
                            pars.append(par)
                            if dn >= 1:
                                b_post(dn - 1)
                            if dn >= 2:
                                b_radd(dn - 2)
                        b_post(NDN - 1)
                        b_radd(NDN - 2)
                        b_radd(NDN - 1)
            for _w in ("wta", "wua", "wda", "wtb", "wub", "wdb"):
                C[_w] += 32
            C["gu"] += 4 * SC_TILES
            C["guev"] += 4 * SC_TILES
            C["hq"] += 4 * SC_TILES; C["hread"] += 4 * SC_TILES
            C["hqtr"] += 4 * SC_TILES; C["hqev"] += 4 * SC_TILES
            C["down"] += 4 * SC_TILES; C["radd"] += 4 * SC_TILES

        def emit_ffn():
            import contextlib as _ctx
            with _ctx.ExitStack() as fst:
                fsb = lambda name, shape, ty: fst.enter_context(nc.sbuf_tensor(un(name), shape, ty))
                FB["xqT"] = fsb("xqT", [128, 8 * SC_TILES * 128], BF16)
                T1B["b"] = fsb("t1b", [128, D], F32)
                FB["wgq"] = [fsb(f"wgq{i}", [128, 8 * WG_MAX * 128], BF16) for i in range(2)]
                FB["wuq"] = [fsb(f"wuq{i}", [128, 8 * WG_MAX * 128], BF16) for i in range(2)]
                FB["wdq"] = [fsb(f"wdq{i}", [128, WG_MAX * 1024], BF16) for i in range(2)]
                FB["sgt"] = fsb("sgt", [128, WG_MAX * 128], F32)
                FB["upt"] = fsb("upt", [128, WG_MAX * 128], F32)
                FB["hqt"] = fsb("hqt", [128, WG_MAX * 128], BF16)
                FB["hprod"] = fsb("hprod", [128, WG_MAX * 128], F32)
                FB["hqT"] = fsb("hqT", [128, WG_MAX * 128], BF16)
                for sc in range(int(_os.environ.get("FFNSC", N_SC))):
                    ffn_stage_a(sc * SC_TILES)
                    if _os.environ.get("FFNB", "1") == "1":
                        ffn_stage_b(sc * SC_TILES)

        AB = {}

        def attn_ln(sbase):
            xaT = AB["xaT"]
            with nc.Block() as blk:
                with nc.psum_tensor(un("ptrL"), [128, 2 * 8 * 128], BF16, is_transpose_dest=True) as ptrL:
                    @blk.vector
                    def _(v):
                        nco = NT_SEQ
                        for t in range(NT_SEQ):
                            vop(v, v.tensor_reduce, out=msum[:, t:t + 1], in_=R[:, (sbase + t) * D:(sbase + t + 1) * D],
                                axis=AX.X, op=Op.add)
                        v.wait_ge(sms["ssq"], C["ssq"] + 1)
                        vop(v, v.tensor_scalar, out=negm[:, :nco], in0=msum[:, :nco], scalar1=-1.0 / D, scalar2=None, op0=Op.mult)
                        vop(v, v.tensor_scalar, out=stA[:, :nco], in0=ssq[:, :nco], scalar1=1.0 / D, scalar2=None, op0=Op.mult)
                        vop(v, v.tensor_tensor, out=stB[:, :nco], in0=negm[:, :nco], in1=negm[:, :nco], op=Op.mult)
                        vop(v, v.tensor_tensor, out=stC[:, :nco], in0=stA[:, :nco], in1=stB[:, :nco], op=Op.subtract)
                        vop(v, v.tensor_scalar, out=stA[:, :nco], in0=stC[:, :nco], scalar1=EPS, scalar2=None,
                            op0=Op.add, _pub="stats")
                        v.wait_ge(sms["std"], C["std"] + 1)
                        vop(v, v.reciprocal, rstd[:, :nco], stB[:, :nco])
                        vop(v, v.tensor_tensor, out=stC[:, :nco], in0=negm[:, :nco], in1=rstd[:, :nco],
                            op=Op.mult, _pub="rstd")
                    @blk.scalar
                    def _(s):
                        for t in range(NT_SEQ):
                            sop(s, s.activation, qb["a"][:, :D], R[:, (sbase + t) * D:(sbase + t + 1) * D],
                                AF.Square, accum_out=ssq[:, t:t + 1],
                                _pub="ssq" if t == NT_SEQ - 1 else None)
                        s.wait_ge(sms["stats"], C["stats"] + 1)
                        sop(s, s.activation, stB[:, :NT_SEQ], stA[:, :NT_SEQ], AF.Sqrt, _pub="std")
                        s.wait_ge(sms["rstd"], C["rstd"] + 1)
                        for t in range(NT_SEQ):
                            if t >= 1:
                                s.wait_ge(sms["atr"], C["atr"] + t)
                            sop(s, s.activation, t1a[:], R[:, (sbase + t) * D:(sbase + t + 1) * D], AF.Identity,
                                bias=stC[:, t:t + 1], scale=rstd[:, t:t + 1], _pub="at1")
                            if t >= 1:
                                par = (t - 1) % 2
                                sop(s, s.copy,
                                    xaT[:].rearrange("p (c x) -> p c x", x=S)[:, :, (t - 1) * 128:t * 128],
                                    ptrL[:, par * 1024:(par + 1) * 1024].rearrange("p (c x) -> p c x", x=128)
                                    , _pub="atev")
                        s.wait_ge(sms["atr"], C["atr"] + NT_SEQ)
                        par = (NT_SEQ - 1) % 2
                        sop(s, s.copy,
                            xaT[:].rearrange("p (c x) -> p c x", x=S)[:, :, (NT_SEQ - 1) * 128:NT_SEQ * 128],
                            ptrL[:, par * 1024:(par + 1) * 1024].rearrange("p (c x) -> p c x", x=128)
                            , _pub="atev")
                    @blk.tensor
                    def _(t_):
                        for t in range(NT_SEQ):
                            t_.wait_ge(sms["at1"], C["at1"] + t + 1)
                            if t >= 2:
                                t_.wait_ge(sms["atev"], C["atev"] + t - 1)
                            par = t % 2
                            for k in range(8):
                                ins = t_.transpose(ptrL[:, par * 1024 + k * 128:par * 1024 + (k + 1) * 128],
                                                   t1a[:, k * 128:(k + 1) * 128], ident[:])
                            pub(ins, "atr")
            C["ssq"] += 1; C["stats"] += 1; C["std"] += 1; C["rstd"] += 1
            C["at1"] += NT_SEQ; C["atr"] += NT_SEQ; C["atev"] += NT_SEQ

        def emit_head(s_i, h, sbase):
            xaT, wqh, wkh, wvh, wgh, woh = AB["xaT"], AB["wqh"], AB["wkh"], AB["wvh"], AB["wgh"], AB["woh"]
            tabq, tabk, qsb, ksb, trot = AB["tabq"], AB["tabk"], AB["qsb"], AB["ksb"], AB["trot"]
            lostage = AB["lostage"]
            qD, kE = qsb, ksb
            vsb, gsb, ysb, sT, csb = AB["vsb"], AB["gsb"], AB["ysb"], AB["sT"], AB["csb"]
            cT, phi, yss, gat, gatT, dlb = AB["cT"], AB["phi"], AB["yss"], AB["gat"], AB["gatT"], AB["dlb"]
            with nc.Block() as blk:
                with (
                    nc.psum_tensor(un("pq"), [128, 2 * 512], F32) as pq,
                    nc.psum_tensor(un("pr"), [128, 2 * 512], F32) as pr,
                    nc.psum_tensor(un("pv2"), [128, 2 * 512], F32) as pv2,
                ):
                    @blk.sync
                    def _(sy):
                        sy.wait_ge(sms["qd"], C["qd"])
                        for idx in range(8):
                            qc, which = idx // 2, idx % 2
                            slot = qc % 2
                            if qc >= 2:
                                sy.wait_ge(sms["qd"], C["qd"] + 2 * (qc - 2) + which + 1)
                            tab_d = rotq_d if which == 0 else rotk_d
                            tab_s = tabq if which == 0 else tabk
                            semn = [("tqa", "tqb"), ("tka", "tkb")][which][slot]
                            sy.dma_start(
                                tab_s[:, slot * 1024:(slot + 1) * 1024].rearrange("p (a x) -> p a x", x=512),
                                tab_d[h][:, :, qc * 512:(qc + 1) * 512].transpose([1, 0, 2])
                            ).then_inc(sms[semn], 16)
                            lo_d = rotqlo_d if which == 0 else rotklo_d
                            n_lo = C["tq"] // 16 + idx
                            sy.wait_ge(sms["tcb"], n_lo)
                            sy.dma_start(
                                lostage[:].rearrange("p (a x) -> p a x", x=512),
                                lo_d[h][:, :, qc * 512:(qc + 1) * 512].transpose([1, 0, 2])
                            ).then_inc(sms["tq"], 16)
                    @blk.gpsimd
                    def _(g):
                        g.wait_ge(sms["qmm"], C["qmm"])
                        g.dma_start(wqh[:].rearrange("p (c w) -> p c w", w=128),
                                    wq_d[:, h * 128:(h + 1) * 128].rearrange("(c p) w -> p c w", p=128)).then_inc(sms["lwq"], 16)
                        g.dma_start(wkh[:].rearrange("p (c w) -> p c w", w=128),
                                    wk_d[:, h * 128:(h + 1) * 128].rearrange("(c p) w -> p c w", p=128)).then_inc(sms["lwk"], 16)
                        g.wait_ge(sms["vg"], C["vg"])
                        g.dma_start(wvh[:].rearrange("p (c w) -> p c w", w=HD),
                                    wv_d[:, h * HD:(h + 1) * HD].rearrange("(c p) w -> p c w", p=128)).then_inc(sms["lwv"], 16)
                        g.dma_start(wgh[:].rearrange("p (c w) -> p c w", w=HD),
                                    wgt_d[:, h * HD:(h + 1) * HD].rearrange("(c p) w -> p c w", p=128)).then_inc(sms["lwg"], 16)
                        g.wait_ge(sms["dl"], C["dl"])
                        g.dma_start(woh[:].rearrange("p (c w) -> p c w", w=D),
                                    wo_d[h * HD:(h + 1) * HD, :].rearrange("(c p) w -> p c w", p=128)).then_inc(sms["lwo"], 16)
                    @blk.tensor
                    def _(t_):
                        t_.wait_ge(sms["lwq"], C["lwq"] + 16)
                        t_.wait_ge(sms["lwk"], C["lwk"] + 16)
                        for qc in range(4):
                            par = qc % 2
                            if qc >= 1:
                                t_.wait_ge(sms["qev"], C["qev"] + 2 * qc)
                            for c in range(8):
                                ins = t_.matmul(pq[:, par * 512:(par + 1) * 512],
                                                wqh[:, c * 128:(c + 1) * 128],
                                                xaT[:, c * S + qc * 512:c * S + (qc + 1) * 512], start=(c == 0), stop=(c == 7))
                            pub(ins, "qmm")
                            for c in range(8):
                                ins = t_.matmul(pq[:, (1 - par) * 512:(2 - par) * 512],
                                                wkh[:, c * 128:(c + 1) * 128],
                                                xaT[:, c * S + qc * 512:c * S + (qc + 1) * 512], start=(c == 0), stop=(c == 7))
                            pub(ins, "qmm")
                        for idx in range(8):
                            qc, which = idx // 2, idx % 2
                            t_.wait_ge(sms["qev"], C["qev"] + 2 * qc + which + 1)
                            if idx >= 2:
                                t_.wait_ge(sms["qd"], C["qd"] + idx - 1)
                            par = idx % 2
                            src = qsb if which == 0 else ksb
                            pub(t_.matmul(pr[:, par * 512:(par + 1) * 512], rmat[:], src[:, qc * 512:(qc + 1) * 512],
                                          start=True, stop=True), "rot")
                        t_.wait_ge(sms["lwv"], C["lwv"] + 16)
                        t_.wait_ge(sms["lwg"], C["lwg"] + 16)
                        t_.wait_ge(sms["lwo"], C["lwo"] + 16)
                        t_.wait_ge(sms["qd"], C["qd"] + 8)
                        for t in range(NT_SEQ):
                            par = t % 2
                            if t >= 2:
                                t_.wait_ge(sms["vgev"], C["vgev"] + t - 1)
                            for c in range(8):
                                stat = xaT[:, c * S + t * 128:c * S + (t + 1) * 128]
                                t_.matmul(pv2[:, par * 512:par * 512 + 256], stat, wvh[:, c * HD:(c + 1) * HD],
                                          start=(c == 0), stop=(c == 7))
                            for c in range(8):
                                stat = xaT[:, c * S + t * 128:c * S + (t + 1) * 128]
                                ins = t_.matmul(pv2[:, par * 512 + 256:(par + 1) * 512], stat, wgh[:, c * HD:(c + 1) * HD],
                                                start=(c == 0), stop=(c == 7))
                            pub(ins, "vg")
                    @blk.scalar
                    def _(s):
                        for idx in range(8):
                            qc, which = idx // 2, idx % 2
                            par = qc % 2 if which == 0 else 1 - (qc % 2)
                            s.wait_ge(sms["qmm"], C["qmm"] + idx + 1)
                            dst = qsb if which == 0 else ksb
                            sop(s, s.copy, dst[:, qc * 512:(qc + 1) * 512], pq[:, par * 512:(par + 1) * 512]
                                , _pub="qev")
                        for t in range(NT_SEQ):
                            par = t % 2
                            s.wait_ge(sms["vg"], C["vg"] + t + 1)
                            sop(s, s.copy, vsb[:, t * HD:(t + 1) * HD], pv2[:, par * 512:par * 512 + 256])
                            sop(s, s.activation, gsb[:, t * HD:(t + 1) * HD], pv2[:, par * 512 + 256:(par + 1) * 512],
                                AF.Silu, _pub="vgev")
                    @blk.vector
                    def _(v):
                        for idx in range(8):
                            qc, which = idx // 2, idx % 2
                            slot = qc % 2
                            semn = [("tqa", "tqb"), ("tka", "tkb")][which][slot]
                            v.wait_ge(sms[semn], C[semn] + 16 * (qc // 2 + 1))
                            v.wait_ge(sms["tq"], C["tq"] + 16 * (idx + 1))
                            tab = tabq if which == 0 else tabk
                            vop(v, v.scalar_tensor_tensor, out=tab[:, slot * 1024:(slot + 1) * 1024],
                                in0=lostage[:], scalar=2.0 ** -40,
                                in1=tab[:, slot * 1024:(slot + 1) * 1024],
                                op0=Op.mult, op1=Op.add, _pub="tcb")
                            v.wait_ge(sms["rot"], C["rot"] + idx + 1)
                            v.wait_ge(sms["qev"], C["qev"] + 2 * qc + which + 1)
                            par = idx % 2
                            tab = tabq if which == 0 else tabk
                            src = qsb if which == 0 else ksb
                            dst = qD if which == 0 else kE
                            sl = slice(qc * 512, (qc + 1) * 512)
                            vop(v, v.tensor_tensor, out=trot[:], in0=pr[:, par * 512:(par + 1) * 512],
                                in1=tab[:, slot * 1024 + 512:slot * 1024 + 1024], op=Op.mult)
                            vop(v, v.tensor_tensor, out=dst[:, sl], in0=src[:, sl],
                                in1=tab[:, slot * 1024:slot * 1024 + 512], op=Op.mult)
                            vop(v, v.tensor_tensor, out=dst[:, sl], in0=dst[:, sl], in1=trot[:],
                                op=Op.add, _pub="qd")
                with (
                    nc.psum_tensor(un("ps"), [128, 2 * 512], F32) as ps,
                    nc.psum_tensor(un("po"), [128, 4 * 512], F32) as po,
                    nc.psum_tensor(un("pc"), [128, 512], F32) as pc,
                ):
                    @blk.tensor
                    def _(t_):
                        t_.wait_ge(sms["qd"], C["qd"] + 8)
                        t_.wait_ge(sms["vgev"], C["vgev"] + NT_SEQ)
                        sn = 0
                        kinds = []
                        for qc in range(4):
                            nkt = (qc + 1) * 4
                            if qc >= 1:
                                t_.wait_ge(sms["yev"], C["yev"] + qc)
                            for kt in range(nkt):
                                par = sn % 2
                                kinds.append(kt - qc * 4 >= 0)
                                if sn >= 2:
                                    t_.wait_ge(sms["sev"], C["sev"] + sum(1 for k2 in kinds[:sn - 1] if k2))
                                    t_.wait_ge(sms["sevs"], C["sevs"] + sum(1 for k2 in kinds[:sn - 1] if not k2))
                                pub(t_.matmul(ps[:, par * 512:(par + 1) * 512], kE[:, kt * 128:(kt + 1) * 128],
                                              qD[:, qc * 512:(qc + 1) * 512], start=True, stop=True), "smm")
                                t_.wait_ge(sms["sev"], C["sev"] + sum(1 for k2 in kinds[:sn + 1] if k2))
                                t_.wait_ge(sms["sevs"], C["sevs"] + sum(1 for k2 in kinds[:sn + 1] if not k2))
                                last_po = t_.matmul(pc[0:1, :512], ones1[:], sT[par][:], start=(kt == 0), stop=(kt == nkt - 1))
                                for isub in range(4):
                                    if kt <= qc * 4 + isub:
                                        last_po = t_.matmul(po[:, isub * 512:isub * 512 + 256],
                                                            sT[par][:, isub * 128:(isub + 1) * 128],
                                                            vsb[:, kt * HD:(kt + 1) * HD],
                                                            start=(kt == 0), stop=(kt == qc * 4 + isub))
                                sn += 1
                            pub(last_po, "omm")
                    @blk.vector
                    def _(v):
                        sn = 0
                        for qc in range(4):
                            nkt = (qc + 1) * 4
                            for kt in range(nkt):
                                par = sn % 2
                                v.wait_ge(sms["smm"], C["smm"] + sn + 1)
                                delta = kt - qc * 4
                                if delta >= 0:
                                    vop(v, v.tensor_tensor, out=sT[par][:], in0=ps[:, par * 512:(par + 1) * 512],
                                        in1=masks[:, delta * 512:(delta + 1) * 512], op=Op.mult, _pub="sev")
                                sn += 1
                    @blk.scalar
                    def _(s):
                        sn = 0
                        for qc in range(4):
                            nkt = (qc + 1) * 4
                            for kt in range(nkt):
                                par = sn % 2
                                delta = kt - qc * 4
                                if delta < 0:
                                    s.wait_ge(sms["smm"], C["smm"] + sn + 1)
                                    sop(s, s.copy, sT[par][:], ps[:, par * 512:(par + 1) * 512], _pub="sevs")
                                sn += 1
                            s.wait_ge(sms["omm"], C["omm"] + qc + 1)
                            sop(s, s.copy, csb[0:1, qc * 512:(qc + 1) * 512], pc[0:1, :512])
                            sop(s, s.copy, ysb[:, qc * 4 * HD:(qc + 1) * 4 * HD].rearrange("p (i w) -> p i w", w=256),
                                po[:].rearrange("p (i z w) -> p i (z w)", z=2, w=256)[:, :, 0:256], _pub="yev")
                with (
                    nc.psum_tensor(un("pt2"), [128, 512], BF16, is_transpose_dest=True) as pt2,
                    nc.psum_tensor(un("pct"), [128, 512], F32) as pct,
                    nc.psum_tensor(un("pdl"), [128, 2 * 1024], F32) as pdl,
                ):
                    @blk.tensor
                    def _(t_):
                        t_.wait_ge(sms["yev"], C["yev"] + 4)
                        for i in range(NT_SEQ):
                            ins = t_.matmul(pct[:, i:i + 1], csb[0:1, i * 128:(i + 1) * 128], ones1[0:1, 0:1],
                                            start=True, stop=True)
                        pub(ins, "csum")
                        for t in range(NT_SEQ):
                            t_.wait_ge(sms["gat"], C["gat"] + t + 1)
                            par = t % 2
                            if t >= 2:
                                t_.wait_ge(sms["gtev"], C["gtev"] + t - 1)
                            for k in range(2):
                                ins = t_.transpose(pt2[:, par * 256 + k * 128:par * 256 + (k + 1) * 128],
                                                   gat[:, par * HD + k * 128:par * HD + (k + 1) * 128], ident[:])
                            pub(ins, "gtr")
                            t_.wait_ge(sms["gtev"], C["gtev"] + t + 1)
                            if t >= 2:
                                t_.wait_ge(sms["dlev"], C["dlev"] + t - 1)
                            for k in range(2):
                                for half in range(2):
                                    ins = t_.matmul(pdl[:, par * 1024 + half * 512:par * 1024 + (half + 1) * 512],
                                                    gatT[:, par * 256 + k * 128:par * 256 + (k + 1) * 128],
                                                    woh[:, k * D + half * 512:k * D + (half + 1) * 512],
                                                    start=(k == 0), stop=(k == 1))
                            pub(ins, "dl")
                    @blk.vector
                    def _(v):
                        v.wait_ge(sms["csum"], C["csum"] + 1)
                        vop(v, v.tensor_copy, cT[:, :NT_SEQ], pct[:, :NT_SEQ])
                        vop(v, v.tensor_reduce, out=stA[:, :NT_SEQ], in_=cT[:, :NT_SEQ].unsqueeze(2), axis=AX.X,
                            op=Op.max, apply_absolute_value=True)
                        vop(v, v.tensor_scalar, out=stB[:, :NT_SEQ], in0=stA[:, :NT_SEQ], scalar1=1.0, scalar2=None, op0=Op.max)
                        vop(v, v.reciprocal, stA[:, :NT_SEQ], stB[:, :NT_SEQ])
                        vop(v, v.tensor_tensor, out=stB[:, :NT_SEQ], in0=stA[:, :NT_SEQ], in1=stA[:, :NT_SEQ], op=Op.mult)
                        v.wait_ge(sms["phi"], C["phi"] + 1)
                        vop(v, v.tensor_tensor, out=stC[:, :NT_SEQ], in0=stB[:, :NT_SEQ], in1=yss[:, :NT_SEQ], op=Op.mult)
                        vop(v, v.tensor_scalar, out=stB[:, :NT_SEQ], in0=stC[:, :NT_SEQ], scalar1=1.0 / HD, scalar2=EPS,
                            op0=Op.mult, op1=Op.add, _pub="aux")
                        v.wait_ge(sms["std"], C["std"] + 1)
                        vop(v, v.reciprocal, stC[:, :NT_SEQ], stB[:, :NT_SEQ])
                        vop(v, v.tensor_tensor, out=phi[:, :NT_SEQ], in0=stA[:, :NT_SEQ], in1=stC[:, :NT_SEQ], op=Op.mult)
                        for t in range(NT_SEQ):
                            par = t % 2
                            if t >= 2:
                                v.wait_ge(sms["gtr"], C["gtr"] + t - 1)
                            vop(v, v.scalar_tensor_tensor, out=gat[:, par * HD:(par + 1) * HD],
                                in0=ysb[:, t * HD:(t + 1) * HD], scalar=phi[:, t:t + 1],
                                in1=gsb[:, t * HD:(t + 1) * HD], op0=Op.mult, op1=Op.mult, _pub="gat")
                            if t >= 1:
                                v.wait_ge(sms["dlev"], C["dlev"] + t)
                                tile = sbase + t - 1
                                vop(v, v.tensor_tensor, out=R[:, tile * D:(tile + 1) * D],
                                    in0=R[:, tile * D:(tile + 1) * D], in1=dlb[:], op=Op.add, _pub="radd")
                        v.wait_ge(sms["dlev"], C["dlev"] + NT_SEQ)
                        tile = sbase + NT_SEQ - 1
                        vop(v, v.tensor_tensor, out=R[:, tile * D:(tile + 1) * D],
                            in0=R[:, tile * D:(tile + 1) * D], in1=dlb[:], op=Op.add, _pub="radd")
                    @blk.scalar
                    def _(s):
                        s.wait_ge(sms["yev"], C["yev"] + 4)
                        for t in range(NT_SEQ):
                            sop(s, s.activation, qb["b"][:, :HD], ysb[:, t * HD:(t + 1) * HD], AF.Square,
                                accum_out=yss[:, t:t + 1],
                                _pub="phi" if t == NT_SEQ - 1 else None)
                        s.wait_ge(sms["aux"], C["aux"] + 1)
                        sop(s, s.activation, stB[:, :NT_SEQ], stB[:, :NT_SEQ], AF.Sqrt, _pub="std")
                        for t in range(NT_SEQ):
                            par = t % 2
                            s.wait_ge(sms["gtr"], C["gtr"] + t + 1)
                            sop(s, s.copy, gatT[:, par * 256:(par + 1) * 256], pt2[:, par * 256:(par + 1) * 256]
                                , _pub="gtev")
                            s.wait_ge(sms["dl"], C["dl"] + t + 1)
                            if t >= 1:
                                s.wait_ge(sms["radd"], C["radd"] + t)
                            sop(s, s.copy, dlb[:], pdl[:, par * 1024:(par + 1) * 1024], _pub="dlev")
            for _w in ("tqa", "tqb", "tka", "tkb"):
                C[_w] += 32
            C["tq"] += 16 * 8
            C["tcb"] += 8
            for _w in ("lwq", "lwk", "lwv", "lwg", "lwo"):
                C[_w] += 16
            C["qmm"] += 8; C["qev"] += 8; C["rot"] += 8; C["qd"] += 8
            C["vg"] += NT_SEQ; C["vgev"] += NT_SEQ
            C["smm"] += 40; C["sev"] += 16; C["sevs"] += 24; C["omm"] += 4; C["yev"] += 4
            C["csum"] += 1; C["phi"] += 1; C["std"] += 1; C["aux"] += 1
            C["gat"] += NT_SEQ; C["gtr"] += NT_SEQ; C["gtev"] += NT_SEQ
            C["dl"] += NT_SEQ; C["dlev"] += NT_SEQ; C["radd"] += NT_SEQ

        def emit_attn():
            import contextlib as _ctx
            with _ctx.ExitStack() as ast_:
                asb = lambda name, shape, ty: ast_.enter_context(nc.sbuf_tensor(un(name), shape, ty))
                AB["xaT"] = asb("xaT", [128, 8 * S], BF16)
                AB["wqh"] = asb("wqh", [128, 8 * 128], BF16)
                AB["wkh"] = asb("wkh", [128, 8 * 128], BF16)
                AB["wvh"] = asb("wvh", [128, 8 * HD], BF16)
                AB["wgh"] = asb("wgh", [128, 8 * HD], BF16)
                AB["woh"] = asb("woh", [128, 2 * D], BF16)
                AB["tabq"] = asb("tabq", [128, 2 * 1024], F32)
                AB["tabk"] = asb("tabk", [128, 2 * 1024], F32)
                AB["qsb"] = asb("qsb", [128, S], BF16)
                AB["ksb"] = asb("ksb", [128, S], BF16)
                AB["trot"] = asb("trot", [128, 512], BF16)
                AB["lostage"] = asb("lostage", [128, 1024], BF16)
                AB["vsb"] = asb("vsb", [128, NT_SEQ * HD], BF16)
                AB["gsb"] = asb("gsb", [128, NT_SEQ * HD], BF16)
                AB["ysb"] = asb("ysb", [128, NT_SEQ * HD], BF16)
                AB["sT"] = [asb(f"sT{i}", [128, 512], BF16) for i in range(2)]
                AB["csb"] = asb("csb", [128, 2048], BF16)
                AB["cT"] = asb("cT", [128, NT_SEQ], F32)
                AB["phi"] = asb("phi", [128, NT_SEQ], F32)
                AB["yss"] = asb("yss", [128, NT_SEQ], F32)
                AB["gat"] = asb("gat", [128, 2 * HD], BF16)
                AB["gatT"] = asb("gatT", [128, 2 * 256], BF16)
                AB["dlb"] = asb("dlb", [128, D], BF16)
                for s_i in range(int(_os.environ.get("ATTNS", SEQ_PER_CORE))):
                    sbase = s_i * NT_SEQ
                    attn_ln(sbase)
                    for h in range(int(_os.environ.get("ATTNH", NH))):
                        emit_head(s_i, h, sbase)

        def emit_dump(i):
            with nc.Block() as blk:
                @blk.gpsimd
                def _(g):
                    g.dma_start(dbg_d[i].rearrange("(t p) d -> p t d", p=128),
                                R[:].rearrange("p (t d) -> p t d", d=D)).then_inc(sms["store"], 16)
                    g.wait_ge(sms["store"], C["store"] + 16)
            C["store"] += 16

        if upto >= 2:
            emit_ffn()
        if dump:
            emit_dump(0)
        if upto >= 3:
            emit_attn()
        if dump:
            emit_dump(1)
        if upto >= 4:
            emit_ffn()
        if dump:
            emit_dump(2)
        with nc.Block() as blk:
            @blk.gpsimd
            def _(g):
                g.dma_start(out_d.rearrange("(t p) d -> p t d", p=128),
                            R[:].rearrange("p (t d) -> p t d", d=D)).then_inc(sms["store"], 16)
                g.wait_ge(sms["store"], C["store"] + 16)
        C["store"] += 16
    return nc


def make_in_maps(L0_attn_x, L0_attn_wq, L0_attn_wk, L0_attn_wv, L0_attn_wg, L0_attn_wo,
                 L0_ffn_w_gate, L0_ffn_w_up, L0_ffn_w_down):
    rotq, rotk, rotqlo, rotklo, masks, rmat = _host_tables()
    x = np.ascontiguousarray(L0_attn_x, dtype=np.float32).reshape(B, S, D)
    in_maps = []
    for core in range(NCORES):
        xs = np.ascontiguousarray(x[core * SEQ_PER_CORE:(core + 1) * SEQ_PER_CORE].reshape(T, D))
        in_maps.append({
            "x": xs,
            "wq": np.ascontiguousarray(L0_attn_wq, dtype=np.float32),
            "wk": np.ascontiguousarray(L0_attn_wk, dtype=np.float32),
            "wv": np.ascontiguousarray(L0_attn_wv, dtype=np.float32),
            "wgt": np.ascontiguousarray(L0_attn_wg, dtype=np.float32),
            "wo": np.ascontiguousarray(L0_attn_wo, dtype=np.float32),
            "wfg": np.ascontiguousarray(L0_ffn_w_gate, dtype=np.float32),
            "wfu": np.ascontiguousarray(L0_ffn_w_up, dtype=np.float32),
            "wfd": np.ascontiguousarray(L0_ffn_w_down, dtype=np.float32),
            "rotq": rotq, "rotk": rotk, "rotqlo": rotqlo, "rotklo": rotklo,
            "masks": masks, "rmat": rmat,
        })
    return in_maps


def kernel(L0_attn_x, L0_attn_wq, L0_attn_wk, L0_attn_wv, L0_attn_wg, L0_attn_wo,
           L0_ffn_w_gate, L0_ffn_w_up, L0_ffn_w_down):
    nc = build_nc(dump=False)
    in_maps = make_in_maps(L0_attn_x, L0_attn_wq, L0_attn_wk, L0_attn_wv, L0_attn_wg,
                           L0_attn_wo, L0_ffn_w_gate, L0_ffn_w_up, L0_ffn_w_down)
    res = run_bass_kernel_spmd(nc, in_maps, core_ids=list(range(NCORES)))
    shards = [np.asarray(res.results[core]["out"]) for core in range(NCORES)]
    out = np.empty((B, S, D), dtype=shards[0].dtype)
    for core in range(NCORES):
        out[core * SEQ_PER_CORE:(core + 1) * SEQ_PER_CORE] = shards[core].reshape(SEQ_PER_CORE, S, D)
    return out


# version 09
# speedup vs baseline: 1.2907x; 1.2053x over previous
import numpy as np
import ml_dtypes
import concourse.bass as bass
import concourse.mybir as mybir
from concourse.bass_utils import run_bass_kernel_spmd
from concourse.alu_op_type import AluOpType as Op

F32 = mybir.dt.float32
BF16 = mybir.dt.bfloat16
I16 = mybir.dt.int16
AF = mybir.ActivationFunctionType
AX = mybir.AxisListType

NCORES = 8
B, S, D = 16, 2048, 1024
SEQ_PER_CORE = B // NCORES
T = SEQ_PER_CORE * S
NT = T // 128
NT_SEQ = S // 128
IF = 2816
NH, KD, HD, VD = 8, 128, 256, 2048
EPS = 1e-5
SC_TILES = 8
N_SC = NT // SC_TILES
FFN_GROUPS = [6, 6, 5, 5]
FFN_GOFF = [0, 6, 12, 17]
GSUB = {6: [512, 256], 5: [512, 128]}
BF = np.dtype(ml_dtypes.bfloat16)


def _host_tables():
    lin = np.linspace(0.0, 1.0, KD // 2)
    angle = np.repeat(1.0 / 10000.0 ** lin, 2)
    ar32 = np.arange(NH, dtype=np.float32)
    decay = np.log1p(-(np.float32(2.0) ** (np.float32(-5.0) - ar32)).astype(np.float32)
                     ).astype(np.float32).astype(np.float64)
    pos = np.arange(S, dtype=np.float64)
    sin = np.sin(pos[:, None] * angle[None, :])
    cos = np.cos(pos[:, None] * angle[None, :])
    e32 = np.exp(decay[:, None] * np.arange(S)[None, :]).astype(np.float32).astype(np.float64)
    nm = 1.0 / np.sqrt(np.cumsum(e32, axis=1))
    Dp = np.exp(pos[None, :] * decay[:, None]) * nm
    Ep = np.exp(-pos[None, :] * decay[:, None]) * (KD ** -0.5)
    rotq = np.empty((NH, 2, KD, S), dtype=np.float32)
    rotk = np.empty((NH, 2, KD, S), dtype=np.float32)
    rq64 = np.empty((NH, 2, KD, S), dtype=np.float64)
    rk64 = np.empty((NH, 2, KD, S), dtype=np.float64)
    for h in range(NH):
        rq64[h, 0] = cos.T * Dp[h][None, :]
        rq64[h, 1] = sin.T * Dp[h][None, :]
        rk64[h, 0] = cos.T * Ep[h][None, :]
        rk64[h, 1] = sin.T * Ep[h][None, :]
    rotq[:] = rq64.astype(np.float32)
    rotk[:] = rk64.astype(np.float32)
    LOSCALE = 2.0 ** 40
    rotqlo = ((rq64 - rotq.astype(np.float64)) * LOSCALE).astype(BF)
    rotklo = ((rk64 - rotk.astype(np.float64)) * LOSCALE).astype(BF)
    masks = np.zeros((4, 128, 512), dtype=BF)
    jj = np.arange(128)[:, None]
    ii = np.arange(512)[None, :]
    for d in range(4):
        masks[d] = (jj <= ii - 128 * d).astype(BF)
    rmat = np.zeros((KD, KD), dtype=BF)
    for m in range(KD // 2):
        rmat[2 * m + 1, 2 * m] = -1.0
        rmat[2 * m, 2 * m + 1] = 1.0
    return rotq, rotk, rotqlo, rotklo, masks, rmat


def _r3(ap, k=16):
    return ap.rearrange("p (b k) -> p b k", k=k)


def build_nc(dump=False, upto=99):
    import os as _os
    import contextlib
    import itertools
    nc = bass.Bass("TRN2", target_bir_lowering=False)
    dt = lambda name, shape, ty, kind: nc.dram_tensor(name, shape, ty, kind=kind).ap()
    x_in = dt("x", [T, D], F32, "ExternalInput")
    wq_d = dt("wq", [D, D], F32, "ExternalInput")
    wk_d = dt("wk", [D, D], F32, "ExternalInput")
    wv_d = dt("wv", [D, VD], F32, "ExternalInput")
    wgt_d = dt("wgt", [D, VD], F32, "ExternalInput")
    wo_d = dt("wo", [VD, D], F32, "ExternalInput")
    wfg_d = dt("wfg", [IF, D], F32, "ExternalInput")
    wfu_d = dt("wfu", [IF, D], F32, "ExternalInput")
    wfd_d = dt("wfd", [D, IF], F32, "ExternalInput")
    rotq_d = dt("rotq", [NH, 2, KD, S], F32, "ExternalInput")
    rotk_d = dt("rotk", [NH, 2, KD, S], F32, "ExternalInput")
    rotqlo_d = dt("rotqlo", [NH, 2, KD, S], BF16, "ExternalInput")
    rotklo_d = dt("rotklo", [NH, 2, KD, S], BF16, "ExternalInput")
    masks_d = dt("masks", [4, 128, 512], BF16, "ExternalInput")
    rmat_d = dt("rmat", [KD, KD], BF16, "ExternalInput")
    out_d = dt("out", [T, D], F32, "ExternalOutput")
    wfgqT = dt("wfgqT", [D, IF], BF16, "Internal")
    wfuqT = dt("wfuqT", [D, IF], BF16, "Internal")
    wfdqT = dt("wfdqT", [IF, D], BF16, "Internal")
    dbg_d = [dt(f"dbg{i}", [T, D], F32, "ExternalOutput") for i in range(3)] if dump else None

    _uc = itertools.count()
    un = lambda n: f"{n}_{next(_uc)}"
    with contextlib.ExitStack() as st:
        sb = lambda name, shape, ty: st.enter_context(nc.sbuf_tensor(un(name), shape, ty))
        R = sb("R", [128, NT * D], BF16)
        ident = sb("ident", [128, 128], BF16)
        identi = sb("identi", [128, 128], mybir.dt.int32)
        rmat = sb("rmat", [128, 128], BF16)
        masks = sb("masks", [128, 4 * 512], BF16)
        ones1 = sb("ones1", [128, 1], BF16)
        msum = sb("msum", [128, NT_SEQ], F32)
        ssq = sb("ssq", [128, NT_SEQ], F32)
        negm = sb("negm", [128, NT_SEQ], F32)
        rstd = sb("rstd", [128, NT_SEQ], F32)
        stA = sb("stA", [128, NT_SEQ], F32)
        stB = sb("stB", [128, NT_SEQ], F32)
        stC = sb("stC", [128, NT_SEQ], F32)
        bm = sb("bm", [128, 192], F32)
        scq = sb("scq", [128, 192], F32)
        rscq = sb("rscq", [128, 192], F32)
        sc2 = sb("sc2", [128, 192], F32)
        QW = 1024
        qb = dict(
            xn=sb("q_xn", [128, QW], F32),
            a=sb("q_a", [128, QW], F32),
            sg=sb("q_sg", [128, QW], BF16),
            b=sb("q_b", [128, QW], BF16), c=sb("q_c", [128, QW], BF16),
        )
        xqtile = sb("xqtile", [128, D], BF16)
        t1a = xqtile
        T1B = {}
        SEMS = ["dma", "dmg", "ssq", "stats", "std", "rstd", "t1", "xq", "xqtr",
                "xqev", "gu", "guev", "hq", "hread", "hqtr", "hqev", "down", "radd",
                "aux", "at1", "atr", "atev", "qmm", "qev", "rot", "qd",
                "vg", "vgev", "smm", "sev", "omm", "yev", "csum", "phi", "gat",
                "gtr", "gtev", "dl", "dlev", "store", "vch", "sch",
                "xin", "cst1", "cst2", "wta", "wua", "wda", "wtb", "wub", "wdb",
                "tq", "tk", "lwq", "lwk", "lwv", "lwg", "lwo", "qxn", "qab",
                "tqa", "tqb", "tka", "tkb", "tcb", "sevs", "qpd"]
        sms = {n: st.enter_context(nc.semaphore("sem_" + n)) for n in SEMS}
        C = {n: 0 for n in SEMS}
        LIVE = {n: 0 for n in SEMS}
        CHV = {"sem": "vch", "val": 0}
        CHS = {"sem": "sch", "val": 0}

        def pub(ins, name, k=1):
            ins.then_inc(sms[name], k)
            LIVE[name] += k
            return ins

        def gpub(ins, name, k=1):
            return pub(ins, name, k)

        def vop(v, meth, *a, _pub=None, **kw):
            v.wait_ge(sms[CHV["sem"]], CHV["val"])
            ins = meth(*a, **kw)
            nm = _pub if _pub is not None else "vch"
            pub(ins, nm, 1)
            CHV["sem"], CHV["val"] = nm, LIVE[nm]
            return ins

        def sop(s, meth, *a, _pub=None, **kw):
            s.wait_ge(sms[CHS["sem"]], CHS["val"])
            ins = meth(*a, **kw)
            nm = _pub if _pub is not None else "sch"
            pub(ins, nm, 1)
            CHS["sem"], CHS["val"] = nm, LIVE[nm]
            return ins

        QCV = [0]
        QCS = [0]

        def quant_chain(v, src_ap, out_ap, nf, rs_b, sc_b, pub_first=None, pub_last=None):
            xn = qb["xn"][:, :nf]
            A = qb["a"][:, :nf]
            SG = qb["sg"][:, :nf]
            b_, c_ = qb["b"], qb["c"]
            vop(v, v.tensor_tensor, out=_r3(xn), in0=_r3(src_ap), in1=rs_b, op=Op.mult)
            vop(v, v.tensor_scalar, out=SG, in0=src_ap, scalar1=0.0, scalar2=0.5,
                op0=Op.is_gt, op1=Op.subtract, _pub=pub_first)
            vop(v, v.tensor_scalar, out=A, in0=xn, scalar1=-1.0, scalar2=None, op0=Op.mult)
            vop(v, v.tensor_tensor, out=A, in0=xn, in1=A, op=Op.max)
            vop(v, v.tensor_scalar, out=b_[:, :nf], in0=A, scalar1=0.25, scalar2=None, op0=Op.is_gt)
            for thr, w in ((0.75, None), (1.25, None), (1.75, None), (2.5, 2.0), (3.5, 2.0), (5.0, 4.0)):
                if w is None:
                    vop(v, v.tensor_scalar, out=c_[:, :nf], in0=A, scalar1=thr, scalar2=None, op0=Op.is_gt)
                else:
                    vop(v, v.tensor_scalar, out=c_[:, :nf], in0=A, scalar1=thr, scalar2=w, op0=Op.is_gt, op1=Op.mult)
                vop(v, v.tensor_tensor, out=b_[:, :nf], in0=b_[:, :nf], in1=c_[:, :nf], op=Op.add)
            vop(v, v.tensor_tensor, out=b_[:, :nf], in0=b_[:, :nf], in1=SG, op=Op.mult)
            return vop(v, v.tensor_tensor, out=_r3(out_ap), in0=_r3(b_[:, :nf]), in1=sc_b, op=Op.mult, _pub=pub_last)

        def quant_scale(v, bmn, nb):
            vop(v, v.tensor_scalar, out=scq[:, :nb], in0=bmn, scalar1=1.0 / 6.0, scalar2=1e-12, op0=Op.mult, op1=Op.max)
            vop(v, v.reciprocal, rscq[:, :nb], scq[:, :nb])

        with nc.Block() as blk:
            @blk.gpsimd
            def _(g):
                gpub(g.iota(identi[:], [[1, 128]], base=0, channel_multiplier=-1), "aux")
                xi = x_in.rearrange("(t p) d -> p t d", p=128)
                Ri = R[:].rearrange("p (t d) -> p t d", d=D)
                g.dma_start(Ri, xi).then_inc(sms["xin"], 16)
            @blk.sync
            def _(sy):
                sy.dma_start(rmat[:], rmat_d).then_inc(sms["cst1"], 16)
                sy.dma_start(_r3(masks[:], k=512), masks_d.transpose([1, 0, 2])).then_inc(sms["cst2"], 16)
            @blk.vector
            def _(v):
                vop(v, v.memset, ones1[:], 1.0)
                v.wait_ge(sms["aux"], C["aux"] + 1)
                vop(v, v.tensor_scalar, out=ident[:], in0=identi[:], scalar1=0, scalar2=None, op0=Op.is_equal)
            @blk.scalar
            def _(s):
                s.wait_ge(sms["xin"], 16)
                s.wait_ge(sms["cst1"], 16)
                s.wait_ge(sms["cst2"], 16)
        C["xin"] += 16
        C["cst1"] += 16
        C["cst2"] += 16
        C["aux"] += 1

        prep = [(wfg_d, wfgqT, 22, D), (wfu_d, wfuqT, 22, D), (wfd_d, wfdqT, 8, IF)]
        if upto < 1:
            prep = []
        nprep = sum(p[2] for p in prep)
        with nc.Block() as blk:
            with (
                nc.psum_tensor(un("ptrW"), [128, IF], BF16, is_transpose_dest=True) as ptrW,
                nc.sbuf_tensor(un("wld"), [128, IF], F32) as wld,
                nc.sbuf_tensor(un("wqt"), [128, IF], BF16) as wqt,
                nc.sbuf_tensor(un("wtr"), [128, IF], BF16) as wtr,
            ):
                @blk.gpsimd
                def _(g):
                    n = 0
                    for src, dst, nchunk, width in prep:
                        for ch in range(nchunk):
                            if n >= 1:
                                g.wait_ge(sms["aux"], C["aux"] + n)
                            g.dma_start(wld[:, :width], src[ch * 128:(ch + 1) * 128, :]).then_inc(sms["dmg"], 16)
                            n += 1
                @blk.vector
                def _(v):
                    n = 0
                    for src, dst, nchunk, width in prep:
                        for ch in range(nchunk):
                            v.wait_ge(sms["dmg"], C["dmg"] + 16 * (n + 1))
                            if n >= 1:
                                v.wait_ge(sms["xqtr"], C["xqtr"] + n)
                            pieces = [(0, width)] if width <= QW else [(o, 704) for o in range(0, width, 704)]
                            nball = width // 16
                            vop(v, v.tensor_reduce, out=bm[:, :nball], in_=_r3(wld[:, :width]), axis=AX.X,
                                op=Op.max, apply_absolute_value=True)
                            quant_scale(v, bm[:, :nball], nball)
                            for pi, (off, w2) in enumerate(pieces):
                                nb = w2 // 16
                                rs_b = rscq[:, off // 16:off // 16 + nb].unsqueeze(2).broadcast_to([128, nb, 16])
                                sc_b = scq[:, off // 16:off // 16 + nb].unsqueeze(2).broadcast_to([128, nb, 16])
                                quant_chain(v, wld[:, off:off + w2], wqt[:, off:off + w2], w2, rs_b, sc_b,
                                            pub_first="aux" if pi == len(pieces) - 1 else None,
                                            pub_last="xq" if pi == len(pieces) - 1 else None)
                            n += 1
                @blk.tensor
                def _(t_):
                    n = 0
                    for src, dst, nchunk, width in prep:
                        ntr = width // 128
                        for ch in range(nchunk):
                            t_.wait_ge(sms["xq"], C["xq"] + n + 1)
                            if n >= 1:
                                t_.wait_ge(sms["xqev"], C["xqev"] + n)
                            for k in range(ntr):
                                ins = t_.transpose(ptrW[:, k * 128:(k + 1) * 128], wqt[:, k * 128:(k + 1) * 128], ident[:])
                            pub(ins, "xqtr")
                            n += 1
                @blk.scalar
                def _(s):
                    n = 0
                    for src, dst, nchunk, width in prep:
                        ntr = width // 128
                        for ch in range(nchunk):
                            s.wait_ge(sms["xqtr"], C["xqtr"] + n + 1)
                            if n >= 1:
                                s.wait_ge(sms["dma"], C["dma"] + 16 * n)
                            sop(s, s.copy, wtr[:, :ntr * 128], ptrW[:, :ntr * 128], _pub="xqev")
                            n += 1
                @blk.sync
                def _(sy):
                    n = 0
                    for src, dst, nchunk, width in prep:
                        ntr = width // 128
                        for ch in range(nchunk):
                            sy.wait_ge(sms["xqev"], C["xqev"] + n + 1)
                            dsti = dst[0:width, ch * 128:(ch + 1) * 128].rearrange("(k p) c -> p k c", p=128)
                            sy.dma_start(dsti, wtr[:, :ntr * 128].rearrange("p (k c) -> p k c", c=128)).then_inc(sms["dma"], 16)
                            n += 1
                    if n:
                        sy.wait_ge(sms["dma"], C["dma"] + 16 * n)
        C["dmg"] += 16 * nprep
        C["xq"] += nprep
        C["aux"] += nprep
        C["xqtr"] += nprep
        C["xqev"] += nprep
        C["dma"] += 16 * nprep

        WG_MAX = 6
        FB = {}

        def ffn_stage_a(tbase):
            xqT = FB["xqT"]
            t1b = T1B["b"]
            with nc.Block() as blk:
                with nc.psum_tensor(un("ptrA"), [128, 2 * 8 * 128], BF16, is_transpose_dest=True) as ptrA:
                    @blk.vector
                    def _(v):
                        nco = SC_TILES
                        for t in range(SC_TILES):
                            vop(v, v.tensor_reduce, out=msum[:, t:t + 1], in_=R[:, (tbase + t) * D:(tbase + t + 1) * D],
                                axis=AX.X, op=Op.add)
                        v.wait_ge(sms["ssq"], C["ssq"] + 1)
                        vop(v, v.tensor_scalar, out=negm[:, :nco], in0=msum[:, :nco], scalar1=-1.0 / D, scalar2=None, op0=Op.mult)
                        vop(v, v.tensor_scalar, out=stA[:, :nco], in0=ssq[:, :nco], scalar1=1.0 / D, scalar2=None, op0=Op.mult)
                        vop(v, v.tensor_tensor, out=stB[:, :nco], in0=negm[:, :nco], in1=negm[:, :nco], op=Op.mult)
                        vop(v, v.tensor_tensor, out=stC[:, :nco], in0=stA[:, :nco], in1=stB[:, :nco], op=Op.subtract)
                        vop(v, v.tensor_scalar, out=stA[:, :nco], in0=stC[:, :nco], scalar1=EPS, scalar2=None,
                            op0=Op.add, _pub="stats")
                        v.wait_ge(sms["std"], C["std"] + 1)
                        vop(v, v.reciprocal, rstd[:, :nco], stB[:, :nco], _pub="rstd")
                        nb = D // 16
                        for t in range(SC_TILES):
                            v.wait_ge(sms["t1"], C["t1"] + t + 1)
                            if t >= 1:
                                v.wait_ge(sms["xqtr"], C["xqtr"] + t)
                            vop(v, v.tensor_reduce, out=bm[:, :nb], in_=_r3(t1b[:]), axis=AX.X,
                                op=Op.max, apply_absolute_value=True)
                            vop(v, v.tensor_scalar, out=sc2[:, :nb], in0=bm[:, :nb], scalar1=rstd[:, t:t + 1],
                                scalar2=None, op0=Op.mult)
                            quant_scale(v, sc2[:, :nb], nb)
                            vop(v, v.tensor_scalar, out=sc2[:, :nb], in0=rscq[:, :nb], scalar1=rstd[:, t:t + 1],
                                scalar2=None, op0=Op.mult)
                            rs_b = sc2[:, :nb].unsqueeze(2).broadcast_to([128, nb, 16])
                            sc_b = scq[:, :nb].unsqueeze(2).broadcast_to([128, nb, 16])
                            quant_chain(v, t1b[:], xqtile[:], D, rs_b, sc_b, pub_last="xq")
                    @blk.scalar
                    def _(s):
                        for t in range(SC_TILES):
                            sop(s, s.activation, qb["a"][:, :D], R[:, (tbase + t) * D:(tbase + t + 1) * D],
                                AF.Square, accum_out=ssq[:, t:t + 1],
                                _pub="ssq" if t == SC_TILES - 1 else None)
                        s.wait_ge(sms["stats"], C["stats"] + 1)
                        sop(s, s.activation, stB[:, :SC_TILES], stA[:, :SC_TILES], AF.Sqrt, _pub="std")
                        s.wait_ge(sms["rstd"], C["rstd"] + 1)
                        for t in range(SC_TILES):
                            if t >= 1:
                                s.wait_ge(sms["xq"], C["xq"] + t)
                            sop(s, s.activation, t1b[:], R[:, (tbase + t) * D:(tbase + t + 1) * D], AF.Identity,
                                bias=negm[:, t:t + 1], scale=1.0, _pub="t1")
                            if t >= 1:
                                s.wait_ge(sms["xqtr"], C["xqtr"] + t)
                                par = (t - 1) % 2
                                sop(s, s.copy,
                                    xqT[:].rearrange("p (c x) -> p c x", x=SC_TILES * 128)[:, :, (t - 1) * 128:t * 128],
                                    ptrA[:, par * 1024:(par + 1) * 1024].rearrange("p (c x) -> p c x", x=128)
                                    , _pub="xqev")
                        s.wait_ge(sms["xqtr"], C["xqtr"] + SC_TILES)
                        par = (SC_TILES - 1) % 2
                        sop(s, s.copy,
                            xqT[:].rearrange("p (c x) -> p c x", x=SC_TILES * 128)[:, :, (SC_TILES - 1) * 128:SC_TILES * 128],
                            ptrA[:, par * 1024:(par + 1) * 1024].rearrange("p (c x) -> p c x", x=128)
                            , _pub="xqev")
                    @blk.tensor
                    def _(t_):
                        for t in range(SC_TILES):
                            t_.wait_ge(sms["xq"], C["xq"] + t + 1)
                            if t >= 2:
                                t_.wait_ge(sms["xqev"], C["xqev"] + t - 1)
                            par = t % 2
                            for k in range(8):
                                ins = t_.transpose(ptrA[:, par * 1024 + k * 128:par * 1024 + (k + 1) * 128],
                                                   xqtile[:, k * 128:(k + 1) * 128], ident[:])
                            pub(ins, "xqtr")
            C["ssq"] += 1; C["stats"] += 1; C["std"] += 1; C["rstd"] += 1
            C["t1"] += SC_TILES; C["xq"] += SC_TILES; C["xqtr"] += SC_TILES; C["xqev"] += SC_TILES

        def ffn_stage_b(tbase):
            xqT, wgq, wuq, wdq = FB["xqT"], FB["wgq"], FB["wuq"], FB["wdq"]
            sgt, upt, hqt, hqT = FB["sgt"], FB["upt"], FB["hqt"], FB["hqT"]
            with nc.Block() as blk:
                with (
                    nc.psum_tensor(un("pgu"), [128, 4 * 512], F32) as pgu,
                    nc.psum_tensor(un("pd"), [128, 1024], F32) as pd,
                    nc.psum_tensor(un("ptrB"), [128, WG_MAX * 128], BF16, is_transpose_dest=True) as ptrB,
                ):
                    @blk.sync
                    def _(sy):
                        for g in range(4):
                            if g >= 2:
                                sy.wait_ge(sms["gu"], C["gu"] + (g - 1) * SC_TILES)
                                sy.wait_ge(sms["down"], C["down"] + (g - 1) * SC_TILES)
                            nch = FFN_GROUPS[g]
                            off = FFN_GOFF[g]
                            buf = g % 2
                            src_g = wfgqT[:, off * 128:(off + nch) * 128].rearrange("(c p) w -> p c w", p=128)
                            src_u = wfuqT[:, off * 128:(off + nch) * 128].rearrange("(c p) w -> p c w", p=128)
                            wsem = [("wta", "wua", "wda"), ("wtb", "wub", "wdb")][buf]
                            sy.dma_start(wgq[buf][:, :8 * nch * 128].rearrange("p (c w) -> p c w", w=nch * 128), src_g).then_inc(sms[wsem[0]], 16)
                            sy.dma_start(wuq[buf][:, :8 * nch * 128].rearrange("p (c w) -> p c w", w=nch * 128), src_u).then_inc(sms[wsem[1]], 16)
                            src_d = wfdqT[off * 128:(off + nch) * 128, :].rearrange("(c p) w -> p c w", p=128)
                            sy.dma_start(wdq[buf][:, :nch * 1024].rearrange("p (c w) -> p c w", w=1024), src_d).then_inc(sms[wsem[2]], 16)
                    NDN = 4 * SC_TILES
                    gof = lambda dn: dn // SC_TILES

                    def t_gu(t_, dn):
                        g = gof(dn); nch = FFN_GROUPS[g]; s1 = GSUB[nch][1]
                        t = dn % SC_TILES; buf = g % 2
                        if t == 0:
                            wsem = [("wta", "wua", "wda"), ("wtb", "wub", "wdb")][buf]
                            for _w in wsem:
                                t_.wait_ge(sms[_w], C[_w] + 16 * (g // 2 + 1))
                        if dn >= 1:
                            t_.wait_ge(sms["guev"], C["guev"] + dn)
                        for c in range(8):
                            stat = xqT[:, c * SC_TILES * 128 + t * 128: c * SC_TILES * 128 + (t + 1) * 128]
                            t_.matmul(pgu[:, 0:512], stat,
                                      wgq[buf][:, c * nch * 128: c * nch * 128 + 512],
                                      start=(c == 0), stop=(c == 7))
                            t_.matmul(pgu[:, 512:1024], stat,
                                      wuq[buf][:, c * nch * 128: c * nch * 128 + 512],
                                      start=(c == 0), stop=(c == 7))
                            t_.matmul(pgu[:, 1024:1024 + s1], stat,
                                      wgq[buf][:, c * nch * 128 + 512: c * nch * 128 + 512 + s1],
                                      start=(c == 0), stop=(c == 7))
                            ins = t_.matmul(pgu[:, 1536:1536 + s1], stat,
                                            wuq[buf][:, c * nch * 128 + 512: c * nch * 128 + 512 + s1],
                                            start=(c == 0), stop=(c == 7))
                        pub(ins, "gu")

                    def t_trdown(t_, dn):
                        g = gof(dn); nch = FFN_GROUPS[g]; buf = g % 2
                        t_.wait_ge(sms["hq"], C["hq"] + dn + 1)
                        if dn >= 1:
                            t_.wait_ge(sms["hqev"], C["hqev"] + dn)
                        for k in range(nch):
                            ins = t_.transpose(ptrB[:, k * 128:(k + 1) * 128], hqt[:, k * 128:(k + 1) * 128], ident[:])
                        pub(ins, "hqtr")
                        t_.wait_ge(sms["hqev"], C["hqev"] + dn + 1)
                        if dn >= 1:
                            t_.wait_ge(sms["radd"], C["radd"] + dn)
                        for c in range(nch):
                            for half in range(2):
                                ins = t_.matmul(pd[:, half * 512:(half + 1) * 512],
                                                hqT[:, c * 128:(c + 1) * 128],
                                                wdq[buf][:, c * 1024 + half * 512: c * 1024 + (half + 1) * 512],
                                                start=(c == 0), stop=(c == nch - 1))
                        pub(ins, "down")

                    @blk.tensor
                    def _(t_):
                        t_gu(t_, 0)
                        for dn in range(NDN):
                            if dn + 1 < NDN:
                                t_gu(t_, dn + 1)
                            t_trdown(t_, dn)

                    def s_evac(s, dn):
                        g = gof(dn); nch = FFN_GROUPS[g]; s1 = GSUB[nch][1]
                        s.wait_ge(sms["gu"], C["gu"] + dn + 1)
                        if dn >= 1:
                            s.wait_ge(sms["hread"], C["hread"] + dn)
                        sop(s, s.activation, sgt[:, 0:512], pgu[:, 0:512], AF.Silu)
                        sop(s, s.activation, sgt[:, 512:512 + s1], pgu[:, 1024:1024 + s1], AF.Silu)
                        sop(s, s.copy, upt[:, 0:512], pgu[:, 512:1024])
                        sop(s, s.copy, upt[:, 512:512 + s1], pgu[:, 1536:1536 + s1], _pub="guev")

                    @blk.scalar
                    def _(s):
                        s_evac(s, 0)
                        for dn in range(NDN):
                            if dn + 1 < NDN:
                                s_evac(s, dn + 1)
                            nch = FFN_GROUPS[gof(dn)]
                            s.wait_ge(sms["hqtr"], C["hqtr"] + dn + 1)
                            sop(s, s.copy, hqT[:, :nch * 128], ptrB[:, :nch * 128], _pub="hqev")

                    @blk.vector
                    def _(v):
                        for dn in range(NDN):
                            g = gof(dn); nch = FFN_GROUPS[g]
                            width = nch * 128
                            nb = width // 16
                            hbuf = FB["hprod"][:, :width]
                            v.wait_ge(sms["guev"], C["guev"] + dn + 1)
                            vop(v, v.tensor_tensor, out=hbuf, in0=sgt[:, :width],
                                in1=upt[:, :width], op=Op.mult, _pub="hread")
                            if dn >= 1:
                                v.wait_ge(sms["hqtr"], C["hqtr"] + dn)
                            vop(v, v.tensor_reduce, out=bm[:, :nb], in_=_r3(hbuf), axis=AX.X,
                                op=Op.max, apply_absolute_value=True)
                            quant_scale(v, bm[:, :nb], nb)
                            rs_b = rscq[:, :nb].unsqueeze(2).broadcast_to([128, nb, 16])
                            sc_b = scq[:, :nb].unsqueeze(2).broadcast_to([128, nb, 16])
                            quant_chain(v, hbuf, hqt[:, :width], width, rs_b, sc_b, pub_last="hq")
                            if dn >= 1:
                                v.wait_ge(sms["down"], C["down"] + dn)
                                tile = tbase + (dn - 1) % SC_TILES
                                vop(v, v.scalar_tensor_tensor, out=R[:, tile * D:(tile + 1) * D],
                                    in0=pd[:, :1024], scalar=0.5,
                                    in1=R[:, tile * D:(tile + 1) * D], op0=Op.mult, op1=Op.add, _pub="radd")
                        v.wait_ge(sms["down"], C["down"] + NDN)
                        tile = tbase + (NDN - 1) % SC_TILES
                        vop(v, v.scalar_tensor_tensor, out=R[:, tile * D:(tile + 1) * D],
                            in0=pd[:, :1024], scalar=0.5,
                            in1=R[:, tile * D:(tile + 1) * D], op0=Op.mult, op1=Op.add, _pub="radd")
            for _w in ("wta", "wua", "wda", "wtb", "wub", "wdb"):
                C[_w] += 32
            C["gu"] += 4 * SC_TILES
            C["guev"] += 4 * SC_TILES
            C["hq"] += 4 * SC_TILES; C["hread"] += 4 * SC_TILES
            C["hqtr"] += 4 * SC_TILES; C["hqev"] += 4 * SC_TILES
            C["down"] += 4 * SC_TILES; C["radd"] += 4 * SC_TILES

        def emit_ffn():
            import contextlib as _ctx
            with _ctx.ExitStack() as fst:
                fsb = lambda name, shape, ty: fst.enter_context(nc.sbuf_tensor(un(name), shape, ty))
                FB["xqT"] = fsb("xqT", [128, 8 * SC_TILES * 128], BF16)
                T1B["b"] = fsb("t1b", [128, D], F32)
                FB["wgq"] = [fsb(f"wgq{i}", [128, 8 * WG_MAX * 128], BF16) for i in range(2)]
                FB["wuq"] = [fsb(f"wuq{i}", [128, 8 * WG_MAX * 128], BF16) for i in range(2)]
                FB["wdq"] = [fsb(f"wdq{i}", [128, WG_MAX * 1024], BF16) for i in range(2)]
                FB["sgt"] = fsb("sgt", [128, WG_MAX * 128], F32)
                FB["upt"] = fsb("upt", [128, WG_MAX * 128], F32)
                FB["hqt"] = fsb("hqt", [128, WG_MAX * 128], BF16)
                FB["hprod"] = fsb("hprod", [128, WG_MAX * 128], F32)
                FB["hqT"] = fsb("hqT", [128, WG_MAX * 128], BF16)
                for sc in range(int(_os.environ.get("FFNSC", N_SC))):
                    ffn_stage_a(sc * SC_TILES)
                    if _os.environ.get("FFNB", "1") == "1":
                        ffn_stage_b(sc * SC_TILES)

        AB = {}

        def attn_ln(sbase):
            xaT = AB["xaT"]
            with nc.Block() as blk:
                with nc.psum_tensor(un("ptrL"), [128, 2 * 8 * 128], BF16, is_transpose_dest=True) as ptrL:
                    @blk.vector
                    def _(v):
                        nco = NT_SEQ
                        for t in range(NT_SEQ):
                            vop(v, v.tensor_reduce, out=msum[:, t:t + 1], in_=R[:, (sbase + t) * D:(sbase + t + 1) * D],
                                axis=AX.X, op=Op.add)
                        v.wait_ge(sms["ssq"], C["ssq"] + 1)
                        vop(v, v.tensor_scalar, out=negm[:, :nco], in0=msum[:, :nco], scalar1=-1.0 / D, scalar2=None, op0=Op.mult)
                        vop(v, v.tensor_scalar, out=stA[:, :nco], in0=ssq[:, :nco], scalar1=1.0 / D, scalar2=None, op0=Op.mult)
                        vop(v, v.tensor_tensor, out=stB[:, :nco], in0=negm[:, :nco], in1=negm[:, :nco], op=Op.mult)
                        vop(v, v.tensor_tensor, out=stC[:, :nco], in0=stA[:, :nco], in1=stB[:, :nco], op=Op.subtract)
                        vop(v, v.tensor_scalar, out=stA[:, :nco], in0=stC[:, :nco], scalar1=EPS, scalar2=None,
                            op0=Op.add, _pub="stats")
                        v.wait_ge(sms["std"], C["std"] + 1)
                        vop(v, v.reciprocal, rstd[:, :nco], stB[:, :nco])
                        vop(v, v.tensor_tensor, out=stC[:, :nco], in0=negm[:, :nco], in1=rstd[:, :nco],
                            op=Op.mult, _pub="rstd")
                    @blk.scalar
                    def _(s):
                        for t in range(NT_SEQ):
                            sop(s, s.activation, qb["a"][:, :D], R[:, (sbase + t) * D:(sbase + t + 1) * D],
                                AF.Square, accum_out=ssq[:, t:t + 1],
                                _pub="ssq" if t == NT_SEQ - 1 else None)
                        s.wait_ge(sms["stats"], C["stats"] + 1)
                        sop(s, s.activation, stB[:, :NT_SEQ], stA[:, :NT_SEQ], AF.Sqrt, _pub="std")
                        s.wait_ge(sms["rstd"], C["rstd"] + 1)
                        for t in range(NT_SEQ):
                            if t >= 1:
                                s.wait_ge(sms["atr"], C["atr"] + t)
                            sop(s, s.activation, t1a[:], R[:, (sbase + t) * D:(sbase + t + 1) * D], AF.Identity,
                                bias=stC[:, t:t + 1], scale=rstd[:, t:t + 1], _pub="at1")
                            if t >= 1:
                                par = (t - 1) % 2
                                sop(s, s.copy,
                                    xaT[:].rearrange("p (c x) -> p c x", x=S)[:, :, (t - 1) * 128:t * 128],
                                    ptrL[:, par * 1024:(par + 1) * 1024].rearrange("p (c x) -> p c x", x=128)
                                    , _pub="atev")
                        s.wait_ge(sms["atr"], C["atr"] + NT_SEQ)
                        par = (NT_SEQ - 1) % 2
                        sop(s, s.copy,
                            xaT[:].rearrange("p (c x) -> p c x", x=S)[:, :, (NT_SEQ - 1) * 128:NT_SEQ * 128],
                            ptrL[:, par * 1024:(par + 1) * 1024].rearrange("p (c x) -> p c x", x=128)
                            , _pub="atev")
                    @blk.tensor
                    def _(t_):
                        for t in range(NT_SEQ):
                            t_.wait_ge(sms["at1"], C["at1"] + t + 1)
                            if t >= 2:
                                t_.wait_ge(sms["atev"], C["atev"] + t - 1)
                            par = t % 2
                            for k in range(8):
                                ins = t_.transpose(ptrL[:, par * 1024 + k * 128:par * 1024 + (k + 1) * 128],
                                                   t1a[:, k * 128:(k + 1) * 128], ident[:])
                            pub(ins, "atr")
            C["ssq"] += 1; C["stats"] += 1; C["std"] += 1; C["rstd"] += 1
            C["at1"] += NT_SEQ; C["atr"] += NT_SEQ; C["atev"] += NT_SEQ

        def emit_head(s_i, h, sbase):
            xaT, wqh, wkh, wvh, wgh, woh = AB["xaT"], AB["wqh"], AB["wkh"], AB["wvh"], AB["wgh"], AB["woh"]
            tabq, tabk, qsb, ksb, trot = AB["tabq"], AB["tabk"], AB["qsb"], AB["ksb"], AB["trot"]
            lostage = AB["lostage"]
            qD, kE = qsb, ksb
            vsb, gsb, ysb, sT, csb = AB["vsb"], AB["gsb"], AB["ysb"], AB["sT"], AB["csb"]
            cT, phi, yss, gat, gatT, dlb = AB["cT"], AB["phi"], AB["yss"], AB["gat"], AB["gatT"], AB["dlb"]
            with nc.Block() as blk:
                with (
                    nc.psum_tensor(un("pq"), [128, 2 * 512], F32) as pq,
                    nc.psum_tensor(un("pr"), [128, 2 * 512], F32) as pr,
                    nc.psum_tensor(un("pv2"), [128, 2 * 512], F32) as pv2,
                ):
                    @blk.sync
                    def _(sy):
                        sy.wait_ge(sms["qd"], C["qd"])
                        for idx in range(8):
                            qc, which = idx // 2, idx % 2
                            slot = qc % 2
                            if qc >= 2:
                                sy.wait_ge(sms["qd"], C["qd"] + 2 * (qc - 2) + which + 1)
                            tab_d = rotq_d if which == 0 else rotk_d
                            tab_s = tabq if which == 0 else tabk
                            semn = [("tqa", "tqb"), ("tka", "tkb")][which][slot]
                            sy.dma_start(
                                tab_s[:, slot * 1024:(slot + 1) * 1024].rearrange("p (a x) -> p a x", x=512),
                                tab_d[h][:, :, qc * 512:(qc + 1) * 512].transpose([1, 0, 2])
                            ).then_inc(sms[semn], 16)
                            lo_d = rotqlo_d if which == 0 else rotklo_d
                            n_lo = C["tq"] // 16 + idx
                            sy.wait_ge(sms["tcb"], n_lo)
                            sy.dma_start(
                                lostage[:].rearrange("p (a x) -> p a x", x=512),
                                lo_d[h][:, :, qc * 512:(qc + 1) * 512].transpose([1, 0, 2])
                            ).then_inc(sms["tq"], 16)
                    @blk.gpsimd
                    def _(g):
                        g.wait_ge(sms["qmm"], C["qmm"])
                        g.dma_start(wqh[:].rearrange("p (c w) -> p c w", w=128),
                                    wq_d[:, h * 128:(h + 1) * 128].rearrange("(c p) w -> p c w", p=128)).then_inc(sms["lwq"], 16)
                        g.dma_start(wkh[:].rearrange("p (c w) -> p c w", w=128),
                                    wk_d[:, h * 128:(h + 1) * 128].rearrange("(c p) w -> p c w", p=128)).then_inc(sms["lwk"], 16)
                        g.wait_ge(sms["vg"], C["vg"])
                        g.dma_start(wvh[:].rearrange("p (c w) -> p c w", w=HD),
                                    wv_d[:, h * HD:(h + 1) * HD].rearrange("(c p) w -> p c w", p=128)).then_inc(sms["lwv"], 16)
                        g.dma_start(wgh[:].rearrange("p (c w) -> p c w", w=HD),
                                    wgt_d[:, h * HD:(h + 1) * HD].rearrange("(c p) w -> p c w", p=128)).then_inc(sms["lwg"], 16)
                        g.wait_ge(sms["dl"], C["dl"])
                        g.dma_start(woh[:].rearrange("p (c w) -> p c w", w=D),
                                    wo_d[h * HD:(h + 1) * HD, :].rearrange("(c p) w -> p c w", p=128)).then_inc(sms["lwo"], 16)
                    @blk.tensor
                    def _(t_):
                        t_.wait_ge(sms["lwq"], C["lwq"] + 16)
                        t_.wait_ge(sms["lwk"], C["lwk"] + 16)
                        for qc in range(4):
                            par = qc % 2
                            if qc >= 1:
                                t_.wait_ge(sms["qev"], C["qev"] + 2 * qc)
                            for c in range(8):
                                ins = t_.matmul(pq[:, par * 512:(par + 1) * 512],
                                                wqh[:, c * 128:(c + 1) * 128],
                                                xaT[:, c * S + qc * 512:c * S + (qc + 1) * 512], start=(c == 0), stop=(c == 7))
                            pub(ins, "qmm")
                            for c in range(8):
                                ins = t_.matmul(pq[:, (1 - par) * 512:(2 - par) * 512],
                                                wkh[:, c * 128:(c + 1) * 128],
                                                xaT[:, c * S + qc * 512:c * S + (qc + 1) * 512], start=(c == 0), stop=(c == 7))
                            pub(ins, "qmm")
                        for idx in range(8):
                            qc, which = idx // 2, idx % 2
                            t_.wait_ge(sms["qev"], C["qev"] + 2 * qc + which + 1)
                            if idx >= 2:
                                t_.wait_ge(sms["qd"], C["qd"] + idx - 1)
                            par = idx % 2
                            src = qsb if which == 0 else ksb
                            pub(t_.matmul(pr[:, par * 512:(par + 1) * 512], rmat[:], src[:, qc * 512:(qc + 1) * 512],
                                          start=True, stop=True), "rot")
                        t_.wait_ge(sms["lwv"], C["lwv"] + 16)
                        t_.wait_ge(sms["lwg"], C["lwg"] + 16)
                        t_.wait_ge(sms["lwo"], C["lwo"] + 16)
                        t_.wait_ge(sms["qd"], C["qd"] + 8)
                        for t in range(NT_SEQ):
                            par = t % 2
                            if t >= 2:
                                t_.wait_ge(sms["vgev"], C["vgev"] + t - 1)
                            for c in range(8):
                                stat = xaT[:, c * S + t * 128:c * S + (t + 1) * 128]
                                t_.matmul(pv2[:, par * 512:par * 512 + 256], stat, wvh[:, c * HD:(c + 1) * HD],
                                          start=(c == 0), stop=(c == 7))
                            for c in range(8):
                                stat = xaT[:, c * S + t * 128:c * S + (t + 1) * 128]
                                ins = t_.matmul(pv2[:, par * 512 + 256:(par + 1) * 512], stat, wgh[:, c * HD:(c + 1) * HD],
                                                start=(c == 0), stop=(c == 7))
                            pub(ins, "vg")
                    @blk.scalar
                    def _(s):
                        for idx in range(8):
                            qc, which = idx // 2, idx % 2
                            par = qc % 2 if which == 0 else 1 - (qc % 2)
                            s.wait_ge(sms["qmm"], C["qmm"] + idx + 1)
                            dst = qsb if which == 0 else ksb
                            sop(s, s.copy, dst[:, qc * 512:(qc + 1) * 512], pq[:, par * 512:(par + 1) * 512]
                                , _pub="qev")
                        for t in range(NT_SEQ):
                            par = t % 2
                            s.wait_ge(sms["vg"], C["vg"] + t + 1)
                            sop(s, s.copy, vsb[:, t * HD:(t + 1) * HD], pv2[:, par * 512:par * 512 + 256])
                            sop(s, s.activation, gsb[:, t * HD:(t + 1) * HD], pv2[:, par * 512 + 256:(par + 1) * 512],
                                AF.Silu, _pub="vgev")
                    @blk.vector
                    def _(v):
                        for idx in range(8):
                            qc, which = idx // 2, idx % 2
                            slot = qc % 2
                            semn = [("tqa", "tqb"), ("tka", "tkb")][which][slot]
                            v.wait_ge(sms[semn], C[semn] + 16 * (qc // 2 + 1))
                            v.wait_ge(sms["tq"], C["tq"] + 16 * (idx + 1))
                            tab = tabq if which == 0 else tabk
                            vop(v, v.scalar_tensor_tensor, out=tab[:, slot * 1024:(slot + 1) * 1024],
                                in0=lostage[:], scalar=2.0 ** -40,
                                in1=tab[:, slot * 1024:(slot + 1) * 1024],
                                op0=Op.mult, op1=Op.add, _pub="tcb")
                            v.wait_ge(sms["rot"], C["rot"] + idx + 1)
                            v.wait_ge(sms["qev"], C["qev"] + 2 * qc + which + 1)
                            par = idx % 2
                            tab = tabq if which == 0 else tabk
                            src = qsb if which == 0 else ksb
                            dst = qD if which == 0 else kE
                            sl = slice(qc * 512, (qc + 1) * 512)
                            vop(v, v.tensor_tensor, out=trot[:], in0=pr[:, par * 512:(par + 1) * 512],
                                in1=tab[:, slot * 1024 + 512:slot * 1024 + 1024], op=Op.mult)
                            vop(v, v.tensor_tensor, out=dst[:, sl], in0=src[:, sl],
                                in1=tab[:, slot * 1024:slot * 1024 + 512], op=Op.mult)
                            vop(v, v.tensor_tensor, out=dst[:, sl], in0=dst[:, sl], in1=trot[:],
                                op=Op.add, _pub="qd")
                with (
                    nc.psum_tensor(un("ps"), [128, 2 * 512], F32) as ps,
                    nc.psum_tensor(un("po"), [128, 4 * 512], F32) as po,
                    nc.psum_tensor(un("pc"), [128, 512], F32) as pc,
                ):
                    @blk.tensor
                    def _(t_):
                        t_.wait_ge(sms["qd"], C["qd"] + 8)
                        t_.wait_ge(sms["vgev"], C["vgev"] + NT_SEQ)
                        sn = 0
                        kinds = []
                        for qc in range(4):
                            nkt = (qc + 1) * 4
                            if qc >= 1:
                                t_.wait_ge(sms["yev"], C["yev"] + qc)
                            for kt in range(nkt):
                                par = sn % 2
                                kinds.append(kt - qc * 4 >= 0)
                                if sn >= 2:
                                    t_.wait_ge(sms["sev"], C["sev"] + sum(1 for k2 in kinds[:sn - 1] if k2))
                                    t_.wait_ge(sms["sevs"], C["sevs"] + sum(1 for k2 in kinds[:sn - 1] if not k2))
                                pub(t_.matmul(ps[:, par * 512:(par + 1) * 512], kE[:, kt * 128:(kt + 1) * 128],
                                              qD[:, qc * 512:(qc + 1) * 512], start=True, stop=True), "smm")
                                t_.wait_ge(sms["sev"], C["sev"] + sum(1 for k2 in kinds[:sn + 1] if k2))
                                t_.wait_ge(sms["sevs"], C["sevs"] + sum(1 for k2 in kinds[:sn + 1] if not k2))
                                last_po = t_.matmul(pc[0:1, :512], ones1[:], sT[par][:], start=(kt == 0), stop=(kt == nkt - 1))
                                for isub in range(4):
                                    if kt <= qc * 4 + isub:
                                        last_po = t_.matmul(po[:, isub * 512:isub * 512 + 256],
                                                            sT[par][:, isub * 128:(isub + 1) * 128],
                                                            vsb[:, kt * HD:(kt + 1) * HD],
                                                            start=(kt == 0), stop=(kt == qc * 4 + isub))
                                sn += 1
                            pub(last_po, "omm")
                    @blk.vector
                    def _(v):
                        sn = 0
                        for qc in range(4):
                            nkt = (qc + 1) * 4
                            for kt in range(nkt):
                                par = sn % 2
                                v.wait_ge(sms["smm"], C["smm"] + sn + 1)
                                delta = kt - qc * 4
                                if delta >= 0:
                                    vop(v, v.tensor_tensor, out=sT[par][:], in0=ps[:, par * 512:(par + 1) * 512],
                                        in1=masks[:, delta * 512:(delta + 1) * 512], op=Op.mult, _pub="sev")
                                sn += 1
                    @blk.scalar
                    def _(s):
                        sn = 0
                        for qc in range(4):
                            nkt = (qc + 1) * 4
                            for kt in range(nkt):
                                par = sn % 2
                                delta = kt - qc * 4
                                if delta < 0:
                                    s.wait_ge(sms["smm"], C["smm"] + sn + 1)
                                    sop(s, s.copy, sT[par][:], ps[:, par * 512:(par + 1) * 512], _pub="sevs")
                                sn += 1
                            s.wait_ge(sms["omm"], C["omm"] + qc + 1)
                            sop(s, s.copy, csb[0:1, qc * 512:(qc + 1) * 512], pc[0:1, :512])
                            sop(s, s.copy, ysb[:, qc * 4 * HD:(qc + 1) * 4 * HD].rearrange("p (i w) -> p i w", w=256),
                                po[:].rearrange("p (i z w) -> p i (z w)", z=2, w=256)[:, :, 0:256], _pub="yev")
                with (
                    nc.psum_tensor(un("pt2"), [128, 512], BF16, is_transpose_dest=True) as pt2,
                    nc.psum_tensor(un("pct"), [128, 512], F32) as pct,
                    nc.psum_tensor(un("pdl"), [128, 2 * 1024], F32) as pdl,
                ):
                    @blk.tensor
                    def _(t_):
                        t_.wait_ge(sms["yev"], C["yev"] + 4)
                        for i in range(NT_SEQ):
                            ins = t_.matmul(pct[:, i:i + 1], csb[0:1, i * 128:(i + 1) * 128], ones1[0:1, 0:1],
                                            start=True, stop=True)
                        pub(ins, "csum")
                        for t in range(NT_SEQ):
                            t_.wait_ge(sms["gat"], C["gat"] + t + 1)
                            par = t % 2
                            if t >= 2:
                                t_.wait_ge(sms["gtev"], C["gtev"] + t - 1)
                            for k in range(2):
                                ins = t_.transpose(pt2[:, par * 256 + k * 128:par * 256 + (k + 1) * 128],
                                                   gat[:, par * HD + k * 128:par * HD + (k + 1) * 128], ident[:])
                            pub(ins, "gtr")
                            t_.wait_ge(sms["gtev"], C["gtev"] + t + 1)
                            if t >= 2:
                                t_.wait_ge(sms["dlev"], C["dlev"] + t - 1)
                            for k in range(2):
                                for half in range(2):
                                    ins = t_.matmul(pdl[:, par * 1024 + half * 512:par * 1024 + (half + 1) * 512],
                                                    gatT[:, par * 256 + k * 128:par * 256 + (k + 1) * 128],
                                                    woh[:, k * D + half * 512:k * D + (half + 1) * 512],
                                                    start=(k == 0), stop=(k == 1))
                            pub(ins, "dl")
                    @blk.vector
                    def _(v):
                        v.wait_ge(sms["csum"], C["csum"] + 1)
                        vop(v, v.tensor_copy, cT[:, :NT_SEQ], pct[:, :NT_SEQ])
                        vop(v, v.tensor_reduce, out=stA[:, :NT_SEQ], in_=cT[:, :NT_SEQ].unsqueeze(2), axis=AX.X,
                            op=Op.max, apply_absolute_value=True)
                        vop(v, v.tensor_scalar, out=stB[:, :NT_SEQ], in0=stA[:, :NT_SEQ], scalar1=1.0, scalar2=None, op0=Op.max)
                        vop(v, v.reciprocal, stA[:, :NT_SEQ], stB[:, :NT_SEQ])
                        vop(v, v.tensor_tensor, out=stB[:, :NT_SEQ], in0=stA[:, :NT_SEQ], in1=stA[:, :NT_SEQ], op=Op.mult)
                        v.wait_ge(sms["phi"], C["phi"] + 1)
                        vop(v, v.tensor_tensor, out=stC[:, :NT_SEQ], in0=stB[:, :NT_SEQ], in1=yss[:, :NT_SEQ], op=Op.mult)
                        vop(v, v.tensor_scalar, out=stB[:, :NT_SEQ], in0=stC[:, :NT_SEQ], scalar1=1.0 / HD, scalar2=EPS,
                            op0=Op.mult, op1=Op.add, _pub="aux")
                        v.wait_ge(sms["std"], C["std"] + 1)
                        vop(v, v.reciprocal, stC[:, :NT_SEQ], stB[:, :NT_SEQ])
                        vop(v, v.tensor_tensor, out=phi[:, :NT_SEQ], in0=stA[:, :NT_SEQ], in1=stC[:, :NT_SEQ], op=Op.mult)
                        for t in range(NT_SEQ):
                            par = t % 2
                            if t >= 2:
                                v.wait_ge(sms["gtr"], C["gtr"] + t - 1)
                            vop(v, v.scalar_tensor_tensor, out=gat[:, par * HD:(par + 1) * HD],
                                in0=ysb[:, t * HD:(t + 1) * HD], scalar=phi[:, t:t + 1],
                                in1=gsb[:, t * HD:(t + 1) * HD], op0=Op.mult, op1=Op.mult, _pub="gat")
                            if t >= 1:
                                v.wait_ge(sms["dlev"], C["dlev"] + t)
                                tile = sbase + t - 1
                                vop(v, v.tensor_tensor, out=R[:, tile * D:(tile + 1) * D],
                                    in0=R[:, tile * D:(tile + 1) * D], in1=dlb[:], op=Op.add, _pub="radd")
                        v.wait_ge(sms["dlev"], C["dlev"] + NT_SEQ)
                        tile = sbase + NT_SEQ - 1
                        vop(v, v.tensor_tensor, out=R[:, tile * D:(tile + 1) * D],
                            in0=R[:, tile * D:(tile + 1) * D], in1=dlb[:], op=Op.add, _pub="radd")
                    @blk.scalar
                    def _(s):
                        s.wait_ge(sms["yev"], C["yev"] + 4)
                        for t in range(NT_SEQ):
                            sop(s, s.activation, qb["b"][:, :HD], ysb[:, t * HD:(t + 1) * HD], AF.Square,
                                accum_out=yss[:, t:t + 1],
                                _pub="phi" if t == NT_SEQ - 1 else None)
                        s.wait_ge(sms["aux"], C["aux"] + 1)
                        sop(s, s.activation, stB[:, :NT_SEQ], stB[:, :NT_SEQ], AF.Sqrt, _pub="std")
                        for t in range(NT_SEQ):
                            par = t % 2
                            s.wait_ge(sms["gtr"], C["gtr"] + t + 1)
                            sop(s, s.copy, gatT[:, par * 256:(par + 1) * 256], pt2[:, par * 256:(par + 1) * 256]
                                , _pub="gtev")
                            s.wait_ge(sms["dl"], C["dl"] + t + 1)
                            if t >= 1:
                                s.wait_ge(sms["radd"], C["radd"] + t)
                            sop(s, s.copy, dlb[:], pdl[:, par * 1024:(par + 1) * 1024], _pub="dlev")
            for _w in ("tqa", "tqb", "tka", "tkb"):
                C[_w] += 32
            C["tq"] += 16 * 8
            C["tcb"] += 8
            for _w in ("lwq", "lwk", "lwv", "lwg", "lwo"):
                C[_w] += 16
            C["qmm"] += 8; C["qev"] += 8; C["rot"] += 8; C["qd"] += 8
            C["vg"] += NT_SEQ; C["vgev"] += NT_SEQ
            C["smm"] += 40; C["sev"] += 16; C["sevs"] += 24; C["omm"] += 4; C["yev"] += 4
            C["csum"] += 1; C["phi"] += 1; C["std"] += 1; C["aux"] += 1
            C["gat"] += NT_SEQ; C["gtr"] += NT_SEQ; C["gtev"] += NT_SEQ
            C["dl"] += NT_SEQ; C["dlev"] += NT_SEQ; C["radd"] += NT_SEQ

        def emit_attn():
            import contextlib as _ctx
            with _ctx.ExitStack() as ast_:
                asb = lambda name, shape, ty: ast_.enter_context(nc.sbuf_tensor(un(name), shape, ty))
                AB["xaT"] = asb("xaT", [128, 8 * S], BF16)
                AB["wqh"] = asb("wqh", [128, 8 * 128], BF16)
                AB["wkh"] = asb("wkh", [128, 8 * 128], BF16)
                AB["wvh"] = asb("wvh", [128, 8 * HD], BF16)
                AB["wgh"] = asb("wgh", [128, 8 * HD], BF16)
                AB["woh"] = asb("woh", [128, 2 * D], BF16)
                AB["tabq"] = asb("tabq", [128, 2 * 1024], F32)
                AB["tabk"] = asb("tabk", [128, 2 * 1024], F32)
                AB["qsb"] = asb("qsb", [128, S], BF16)
                AB["ksb"] = asb("ksb", [128, S], BF16)
                AB["trot"] = asb("trot", [128, 512], BF16)
                AB["lostage"] = asb("lostage", [128, 1024], BF16)
                AB["vsb"] = asb("vsb", [128, NT_SEQ * HD], BF16)
                AB["gsb"] = asb("gsb", [128, NT_SEQ * HD], BF16)
                AB["ysb"] = asb("ysb", [128, NT_SEQ * HD], BF16)
                AB["sT"] = [asb(f"sT{i}", [128, 512], BF16) for i in range(2)]
                AB["csb"] = asb("csb", [128, 2048], BF16)
                AB["cT"] = asb("cT", [128, NT_SEQ], F32)
                AB["phi"] = asb("phi", [128, NT_SEQ], F32)
                AB["yss"] = asb("yss", [128, NT_SEQ], F32)
                AB["gat"] = asb("gat", [128, 2 * HD], BF16)
                AB["gatT"] = asb("gatT", [128, 2 * 256], BF16)
                AB["dlb"] = asb("dlb", [128, D], BF16)
                for s_i in range(int(_os.environ.get("ATTNS", SEQ_PER_CORE))):
                    sbase = s_i * NT_SEQ
                    attn_ln(sbase)
                    for h in range(int(_os.environ.get("ATTNH", NH))):
                        emit_head(s_i, h, sbase)

        def emit_dump(i):
            with nc.Block() as blk:
                @blk.gpsimd
                def _(g):
                    g.dma_start(dbg_d[i].rearrange("(t p) d -> p t d", p=128),
                                R[:].rearrange("p (t d) -> p t d", d=D)).then_inc(sms["store"], 16)
                    g.wait_ge(sms["store"], C["store"] + 16)
            C["store"] += 16

        if upto >= 2:
            emit_ffn()
        if dump:
            emit_dump(0)
        if upto >= 3:
            emit_attn()
        if dump:
            emit_dump(1)
        if upto >= 4:
            emit_ffn()
        if dump:
            emit_dump(2)
        with nc.Block() as blk:
            @blk.gpsimd
            def _(g):
                g.dma_start(out_d.rearrange("(t p) d -> p t d", p=128),
                            R[:].rearrange("p (t d) -> p t d", d=D)).then_inc(sms["store"], 16)
                g.wait_ge(sms["store"], C["store"] + 16)
        C["store"] += 16
    return nc


def make_in_maps(L0_attn_x, L0_attn_wq, L0_attn_wk, L0_attn_wv, L0_attn_wg, L0_attn_wo,
                 L0_ffn_w_gate, L0_ffn_w_up, L0_ffn_w_down):
    rotq, rotk, rotqlo, rotklo, masks, rmat = _host_tables()
    x = np.ascontiguousarray(L0_attn_x, dtype=np.float32).reshape(B, S, D)
    in_maps = []
    for core in range(NCORES):
        xs = np.ascontiguousarray(x[core * SEQ_PER_CORE:(core + 1) * SEQ_PER_CORE].reshape(T, D))
        in_maps.append({
            "x": xs,
            "wq": np.ascontiguousarray(L0_attn_wq, dtype=np.float32),
            "wk": np.ascontiguousarray(L0_attn_wk, dtype=np.float32),
            "wv": np.ascontiguousarray(L0_attn_wv, dtype=np.float32),
            "wgt": np.ascontiguousarray(L0_attn_wg, dtype=np.float32),
            "wo": np.ascontiguousarray(L0_attn_wo, dtype=np.float32),
            "wfg": np.ascontiguousarray(L0_ffn_w_gate, dtype=np.float32),
            "wfu": np.ascontiguousarray(L0_ffn_w_up, dtype=np.float32),
            "wfd": np.ascontiguousarray(L0_ffn_w_down, dtype=np.float32),
            "rotq": rotq, "rotk": rotk, "rotqlo": rotqlo, "rotklo": rotklo,
            "masks": masks, "rmat": rmat,
        })
    return in_maps


def kernel(L0_attn_x, L0_attn_wq, L0_attn_wk, L0_attn_wv, L0_attn_wg, L0_attn_wo,
           L0_ffn_w_gate, L0_ffn_w_up, L0_ffn_w_down):
    nc = build_nc(dump=False)
    in_maps = make_in_maps(L0_attn_x, L0_attn_wq, L0_attn_wk, L0_attn_wv, L0_attn_wg,
                           L0_attn_wo, L0_ffn_w_gate, L0_ffn_w_up, L0_ffn_w_down)
    res = run_bass_kernel_spmd(nc, in_maps, core_ids=list(range(NCORES)))
    shards = [np.asarray(res.results[core]["out"]) for core in range(NCORES)]
    out = np.empty((B, S, D), dtype=shards[0].dtype)
    for core in range(NCORES):
        out[core * SEQ_PER_CORE:(core + 1) * SEQ_PER_CORE] = shards[core].reshape(SEQ_PER_CORE, S, D)
    return out


# version 10
# speedup vs baseline: 1.4091x; 1.0917x over previous
import numpy as np
import ml_dtypes
import concourse.bass as bass
import concourse.mybir as mybir
from concourse.bass_utils import run_bass_kernel_spmd
from concourse.alu_op_type import AluOpType as Op

F32 = mybir.dt.float32
BF16 = mybir.dt.bfloat16
I16 = mybir.dt.int16
AF = mybir.ActivationFunctionType
AX = mybir.AxisListType

NCORES = 8
B, S, D = 16, 2048, 1024
SEQ_PER_CORE = B // NCORES
T = SEQ_PER_CORE * S
NT = T // 128
NT_SEQ = S // 128
IF = 2816
NH, KD, HD, VD = 8, 128, 256, 2048
EPS = 1e-5
SC_TILES = 8
N_SC = NT // SC_TILES
FFN_GROUPS = [6, 6, 5, 5]
FFN_GOFF = [0, 6, 12, 17]
GSUB = {6: [512, 256], 5: [512, 128]}
BF = np.dtype(ml_dtypes.bfloat16)


def _host_tables():
    lin = np.linspace(0.0, 1.0, KD // 2)
    angle = np.repeat(1.0 / 10000.0 ** lin, 2)
    ar32 = np.arange(NH, dtype=np.float32)
    decay = np.log1p(-(np.float32(2.0) ** (np.float32(-5.0) - ar32)).astype(np.float32)
                     ).astype(np.float32).astype(np.float64)
    pos = np.arange(S, dtype=np.float64)
    sin = np.sin(pos[:, None] * angle[None, :])
    cos = np.cos(pos[:, None] * angle[None, :])
    e32 = np.exp(decay[:, None] * np.arange(S)[None, :]).astype(np.float32).astype(np.float64)
    nm = 1.0 / np.sqrt(np.cumsum(e32, axis=1))
    Dp = np.exp(pos[None, :] * decay[:, None]) * nm
    Ep = np.exp(-pos[None, :] * decay[:, None]) * (KD ** -0.5)
    rotq = np.empty((NH, 2, KD, S), dtype=np.float32)
    rotk = np.empty((NH, 2, KD, S), dtype=np.float32)
    rq64 = np.empty((NH, 2, KD, S), dtype=np.float64)
    rk64 = np.empty((NH, 2, KD, S), dtype=np.float64)
    for h in range(NH):
        rq64[h, 0] = cos.T * Dp[h][None, :]
        rq64[h, 1] = sin.T * Dp[h][None, :]
        rk64[h, 0] = cos.T * Ep[h][None, :]
        rk64[h, 1] = sin.T * Ep[h][None, :]
    rotq[:] = rq64.astype(np.float32)
    rotk[:] = rk64.astype(np.float32)
    LOSCALE = 2.0 ** 40
    rotqlo = ((rq64 - rotq.astype(np.float64)) * LOSCALE).astype(BF)
    rotklo = ((rk64 - rotk.astype(np.float64)) * LOSCALE).astype(BF)
    masks = np.zeros((4, 128, 512), dtype=BF)
    jj = np.arange(128)[:, None]
    ii = np.arange(512)[None, :]
    for d in range(4):
        masks[d] = (jj <= ii - 128 * d).astype(BF)
    rmat = np.zeros((KD, KD), dtype=BF)
    for m in range(KD // 2):
        rmat[2 * m + 1, 2 * m] = -1.0
        rmat[2 * m, 2 * m + 1] = 1.0
    return rotq, rotk, rotqlo, rotklo, masks, rmat


def _r3(ap, k=16):
    return ap.rearrange("p (b k) -> p b k", k=k)


def build_nc(dump=False, upto=99):
    import os as _os
    import contextlib
    import itertools
    nc = bass.Bass("TRN2", target_bir_lowering=False)
    dt = lambda name, shape, ty, kind: nc.dram_tensor(name, shape, ty, kind=kind).ap()
    x_in = dt("x", [T, D], F32, "ExternalInput")
    wq_d = dt("wq", [D, D], F32, "ExternalInput")
    wk_d = dt("wk", [D, D], F32, "ExternalInput")
    wv_d = dt("wv", [D, VD], F32, "ExternalInput")
    wgt_d = dt("wgt", [D, VD], F32, "ExternalInput")
    wo_d = dt("wo", [VD, D], F32, "ExternalInput")
    wfg_d = dt("wfg", [IF, D], F32, "ExternalInput")
    wfu_d = dt("wfu", [IF, D], F32, "ExternalInput")
    wfd_d = dt("wfd", [D, IF], F32, "ExternalInput")
    rotq_d = dt("rotq", [NH, 2, KD, S], F32, "ExternalInput")
    rotk_d = dt("rotk", [NH, 2, KD, S], F32, "ExternalInput")
    rotqlo_d = dt("rotqlo", [NH, 2, KD, S], BF16, "ExternalInput")
    rotklo_d = dt("rotklo", [NH, 2, KD, S], BF16, "ExternalInput")
    masks_d = dt("masks", [4, 128, 512], BF16, "ExternalInput")
    rmat_d = dt("rmat", [KD, KD], BF16, "ExternalInput")
    out_d = dt("out", [T, D], F32, "ExternalOutput")
    wfgqT = dt("wfgqT", [D, IF], BF16, "Internal")
    wfuqT = dt("wfuqT", [D, IF], BF16, "Internal")
    wfdqT = dt("wfdqT", [IF, D], BF16, "Internal")
    dbg_d = [dt(f"dbg{i}", [T, D], F32, "ExternalOutput") for i in range(3)] if dump else None

    _uc = itertools.count()
    un = lambda n: f"{n}_{next(_uc)}"
    with contextlib.ExitStack() as st:
        sb = lambda name, shape, ty: st.enter_context(nc.sbuf_tensor(un(name), shape, ty))
        R = sb("R", [128, NT * D], BF16)
        ident = sb("ident", [128, 128], BF16)
        identi = sb("identi", [128, 128], mybir.dt.int32)
        rmat = sb("rmat", [128, 128], BF16)
        masks = sb("masks", [128, 4 * 512], BF16)
        ones1 = sb("ones1", [128, 1], BF16)
        msum = sb("msum", [128, NT_SEQ], F32)
        ssq = sb("ssq", [128, NT_SEQ], F32)
        negm = sb("negm", [128, NT_SEQ], F32)
        rstd = sb("rstd", [128, NT_SEQ], F32)
        stA = sb("stA", [128, NT_SEQ], F32)
        stB = sb("stB", [128, NT_SEQ], F32)
        stC = sb("stC", [128, NT_SEQ], F32)
        bm = sb("bm", [128, 192], F32)
        scq = sb("scq", [128, 192], F32)
        rscq = sb("rscq", [128, 192], F32)
        sc2 = sb("sc2", [128, 192], F32)
        QW = 1024
        qb = dict(
            xn=sb("q_xn", [128, QW], F32),
            a=sb("q_a", [128, QW], F32),
            sg=sb("q_sg", [128, QW], BF16),
            b=sb("q_b", [128, QW], BF16), c=sb("q_c", [128, QW], BF16),
        )
        xqtile = sb("xqtile", [128, D], BF16)
        t1a = xqtile
        T1B = {}
        SEMS = ["dma", "dmg", "ssq", "stats", "std", "rstd", "t1", "xq", "xqtr",
                "xqev", "gu", "guev", "hq", "hread", "hqtr", "hqev", "down", "radd",
                "aux", "at1", "atr", "atev", "qmm", "qev", "rot", "qd",
                "vg", "vgev", "smm", "sev", "omm", "yev", "csum", "phi", "gat",
                "gtr", "gtev", "dl", "dlev", "store", "vch", "sch",
                "xin", "cst1", "cst2", "wta", "wua", "wda", "wtb", "wub", "wdb",
                "tq", "tk", "lwq", "lwk", "lwv", "lwg", "lwo", "qxn", "qab",
                "tqa", "tqb", "tka", "tkb", "tcb", "sevs", "qpd"]
        sms = {n: st.enter_context(nc.semaphore("sem_" + n)) for n in SEMS}
        C = {n: 0 for n in SEMS}
        LIVE = {n: 0 for n in SEMS}
        CHV = {"sem": "vch", "val": 0}
        CHS = {"sem": "sch", "val": 0}

        def pub(ins, name, k=1):
            ins.then_inc(sms[name], k)
            LIVE[name] += k
            return ins

        def gpub(ins, name, k=1):
            return pub(ins, name, k)

        def vop(v, meth, *a, _pub=None, **kw):
            v.wait_ge(sms[CHV["sem"]], CHV["val"])
            ins = meth(*a, **kw)
            nm = _pub if _pub is not None else "vch"
            pub(ins, nm, 1)
            CHV["sem"], CHV["val"] = nm, LIVE[nm]
            return ins

        def sop(s, meth, *a, _pub=None, **kw):
            s.wait_ge(sms[CHS["sem"]], CHS["val"])
            ins = meth(*a, **kw)
            nm = _pub if _pub is not None else "sch"
            pub(ins, nm, 1)
            CHS["sem"], CHS["val"] = nm, LIVE[nm]
            return ins

        QCV = [0]
        QCS = [0]

        def quant_chain(v, src_ap, out_ap, nf, rs_b, sc_b, pub_first=None, pub_last=None):
            xn = qb["xn"][:, :nf]
            A = qb["a"][:, :nf]
            SG = qb["sg"][:, :nf]
            b_, c_ = qb["b"], qb["c"]
            vop(v, v.tensor_tensor, out=_r3(xn), in0=_r3(src_ap), in1=rs_b, op=Op.mult)
            vop(v, v.tensor_scalar, out=SG, in0=src_ap, scalar1=0.0, scalar2=0.5,
                op0=Op.is_gt, op1=Op.subtract, _pub=pub_first)
            vop(v, v.tensor_scalar, out=A, in0=xn, scalar1=-1.0, scalar2=None, op0=Op.mult)
            vop(v, v.tensor_tensor, out=A, in0=xn, in1=A, op=Op.max)
            vop(v, v.tensor_scalar, out=b_[:, :nf], in0=A, scalar1=0.25, scalar2=None, op0=Op.is_gt)
            for thr, w in ((0.75, None), (1.25, None), (1.75, None), (2.5, 2.0), (3.5, 2.0), (5.0, 4.0)):
                if w is None:
                    vop(v, v.tensor_scalar, out=c_[:, :nf], in0=A, scalar1=thr, scalar2=None, op0=Op.is_gt)
                else:
                    vop(v, v.tensor_scalar, out=c_[:, :nf], in0=A, scalar1=thr, scalar2=w, op0=Op.is_gt, op1=Op.mult)
                vop(v, v.tensor_tensor, out=b_[:, :nf], in0=b_[:, :nf], in1=c_[:, :nf], op=Op.add)
            vop(v, v.tensor_tensor, out=b_[:, :nf], in0=b_[:, :nf], in1=SG, op=Op.mult)
            return vop(v, v.tensor_tensor, out=_r3(out_ap), in0=_r3(b_[:, :nf]), in1=sc_b, op=Op.mult, _pub=pub_last)

        def quant_scale(v, bmn, nb):
            vop(v, v.tensor_scalar, out=scq[:, :nb], in0=bmn, scalar1=1.0 / 6.0, scalar2=1e-12, op0=Op.mult, op1=Op.max)
            vop(v, v.reciprocal, rscq[:, :nb], scq[:, :nb])

        with nc.Block() as blk:
            @blk.gpsimd
            def _(g):
                gpub(g.iota(identi[:], [[1, 128]], base=0, channel_multiplier=-1), "aux")
                xi = x_in.rearrange("(t p) d -> p t d", p=128)
                Ri = R[:].rearrange("p (t d) -> p t d", d=D)
                g.dma_start(Ri, xi).then_inc(sms["xin"], 16)
            @blk.sync
            def _(sy):
                sy.dma_start(rmat[:], rmat_d).then_inc(sms["cst1"], 16)
                sy.dma_start(_r3(masks[:], k=512), masks_d.transpose([1, 0, 2])).then_inc(sms["cst2"], 16)
            @blk.vector
            def _(v):
                vop(v, v.memset, ones1[:], 1.0)
                v.wait_ge(sms["aux"], C["aux"] + 1)
                vop(v, v.tensor_scalar, out=ident[:], in0=identi[:], scalar1=0, scalar2=None, op0=Op.is_equal)
            @blk.scalar
            def _(s):
                s.wait_ge(sms["xin"], 16)
                s.wait_ge(sms["cst1"], 16)
                s.wait_ge(sms["cst2"], 16)
        C["xin"] += 16
        C["cst1"] += 16
        C["cst2"] += 16
        C["aux"] += 1

        prep = [(wfg_d, wfgqT, 22, D), (wfu_d, wfuqT, 22, D), (wfd_d, wfdqT, 8, IF)]
        if upto < 1:
            prep = []
        nprep = sum(p[2] for p in prep)
        with nc.Block() as blk:
            with (
                nc.psum_tensor(un("ptrW"), [128, IF], BF16, is_transpose_dest=True) as ptrW,
                nc.sbuf_tensor(un("wld"), [128, IF], F32) as wld,
                nc.sbuf_tensor(un("wqt"), [128, IF], BF16) as wqt,
                nc.sbuf_tensor(un("wtr"), [128, IF], BF16) as wtr,
            ):
                @blk.gpsimd
                def _(g):
                    n = 0
                    for src, dst, nchunk, width in prep:
                        for ch in range(nchunk):
                            if n >= 1:
                                g.wait_ge(sms["aux"], C["aux"] + n)
                            g.dma_start(wld[:, :width], src[ch * 128:(ch + 1) * 128, :]).then_inc(sms["dmg"], 16)
                            n += 1
                @blk.vector
                def _(v):
                    n = 0
                    for src, dst, nchunk, width in prep:
                        for ch in range(nchunk):
                            v.wait_ge(sms["dmg"], C["dmg"] + 16 * (n + 1))
                            if n >= 1:
                                v.wait_ge(sms["xqtr"], C["xqtr"] + n)
                            pieces = [(0, width)] if width <= QW else [(o, 704) for o in range(0, width, 704)]
                            nball = width // 16
                            vop(v, v.tensor_reduce, out=bm[:, :nball], in_=_r3(wld[:, :width]), axis=AX.X,
                                op=Op.max, apply_absolute_value=True)
                            quant_scale(v, bm[:, :nball], nball)
                            for pi, (off, w2) in enumerate(pieces):
                                nb = w2 // 16
                                rs_b = rscq[:, off // 16:off // 16 + nb].unsqueeze(2).broadcast_to([128, nb, 16])
                                sc_b = scq[:, off // 16:off // 16 + nb].unsqueeze(2).broadcast_to([128, nb, 16])
                                quant_chain(v, wld[:, off:off + w2], wqt[:, off:off + w2], w2, rs_b, sc_b,
                                            pub_first="aux" if pi == len(pieces) - 1 else None,
                                            pub_last="xq" if pi == len(pieces) - 1 else None)
                            n += 1
                @blk.tensor
                def _(t_):
                    n = 0
                    for src, dst, nchunk, width in prep:
                        ntr = width // 128
                        for ch in range(nchunk):
                            t_.wait_ge(sms["xq"], C["xq"] + n + 1)
                            if n >= 1:
                                t_.wait_ge(sms["xqev"], C["xqev"] + n)
                            for k in range(ntr):
                                ins = t_.transpose(ptrW[:, k * 128:(k + 1) * 128], wqt[:, k * 128:(k + 1) * 128], ident[:])
                            pub(ins, "xqtr")
                            n += 1
                @blk.scalar
                def _(s):
                    n = 0
                    for src, dst, nchunk, width in prep:
                        ntr = width // 128
                        for ch in range(nchunk):
                            s.wait_ge(sms["xqtr"], C["xqtr"] + n + 1)
                            if n >= 1:
                                s.wait_ge(sms["dma"], C["dma"] + 16 * n)
                            sop(s, s.copy, wtr[:, :ntr * 128], ptrW[:, :ntr * 128], _pub="xqev")
                            n += 1
                @blk.sync
                def _(sy):
                    n = 0
                    for src, dst, nchunk, width in prep:
                        ntr = width // 128
                        for ch in range(nchunk):
                            sy.wait_ge(sms["xqev"], C["xqev"] + n + 1)
                            dsti = dst[0:width, ch * 128:(ch + 1) * 128].rearrange("(k p) c -> p k c", p=128)
                            sy.dma_start(dsti, wtr[:, :ntr * 128].rearrange("p (k c) -> p k c", c=128)).then_inc(sms["dma"], 16)
                            n += 1
                    if n:
                        sy.wait_ge(sms["dma"], C["dma"] + 16 * n)
        C["dmg"] += 16 * nprep
        C["xq"] += nprep
        C["aux"] += nprep
        C["xqtr"] += nprep
        C["xqev"] += nprep
        C["dma"] += 16 * nprep

        WG_MAX = 6
        FB = {}

        def ffn_stage_a(tbase):
            xqT = FB["xqT"]
            t1b = T1B["b"]
            with nc.Block() as blk:
                with nc.psum_tensor(un("ptrA"), [128, 2 * 8 * 128], BF16, is_transpose_dest=True) as ptrA:
                    @blk.vector
                    def _(v):
                        nco = SC_TILES
                        for t in range(SC_TILES):
                            vop(v, v.tensor_reduce, out=msum[:, t:t + 1], in_=R[:, (tbase + t) * D:(tbase + t + 1) * D],
                                axis=AX.X, op=Op.add)
                        v.wait_ge(sms["ssq"], C["ssq"] + 1)
                        vop(v, v.tensor_scalar, out=negm[:, :nco], in0=msum[:, :nco], scalar1=-1.0 / D, scalar2=None, op0=Op.mult)
                        vop(v, v.tensor_scalar, out=stA[:, :nco], in0=ssq[:, :nco], scalar1=1.0 / D, scalar2=None, op0=Op.mult)
                        vop(v, v.tensor_tensor, out=stB[:, :nco], in0=negm[:, :nco], in1=negm[:, :nco], op=Op.mult)
                        vop(v, v.tensor_tensor, out=stC[:, :nco], in0=stA[:, :nco], in1=stB[:, :nco], op=Op.subtract)
                        vop(v, v.tensor_scalar, out=stA[:, :nco], in0=stC[:, :nco], scalar1=EPS, scalar2=None,
                            op0=Op.add, _pub="stats")
                        v.wait_ge(sms["std"], C["std"] + 1)
                        vop(v, v.reciprocal, rstd[:, :nco], stB[:, :nco], _pub="rstd")
                        nb = D // 16
                        for t in range(SC_TILES):
                            v.wait_ge(sms["t1"], C["t1"] + t + 1)
                            if t >= 1:
                                v.wait_ge(sms["xqtr"], C["xqtr"] + t)
                            vop(v, v.tensor_reduce, out=bm[:, :nb], in_=_r3(t1b[:]), axis=AX.X,
                                op=Op.max, apply_absolute_value=True)
                            vop(v, v.tensor_scalar, out=sc2[:, :nb], in0=bm[:, :nb], scalar1=rstd[:, t:t + 1],
                                scalar2=None, op0=Op.mult)
                            quant_scale(v, sc2[:, :nb], nb)
                            vop(v, v.tensor_scalar, out=sc2[:, :nb], in0=rscq[:, :nb], scalar1=rstd[:, t:t + 1],
                                scalar2=None, op0=Op.mult)
                            rs_b = sc2[:, :nb].unsqueeze(2).broadcast_to([128, nb, 16])
                            sc_b = scq[:, :nb].unsqueeze(2).broadcast_to([128, nb, 16])
                            quant_chain(v, t1b[:], xqtile[:], D, rs_b, sc_b, pub_last="xq")
                    @blk.scalar
                    def _(s):
                        for t in range(SC_TILES):
                            sop(s, s.activation, qb["a"][:, :D], R[:, (tbase + t) * D:(tbase + t + 1) * D],
                                AF.Square, accum_out=ssq[:, t:t + 1],
                                _pub="ssq" if t == SC_TILES - 1 else None)
                        s.wait_ge(sms["stats"], C["stats"] + 1)
                        sop(s, s.activation, stB[:, :SC_TILES], stA[:, :SC_TILES], AF.Sqrt, _pub="std")
                        s.wait_ge(sms["rstd"], C["rstd"] + 1)
                        for t in range(SC_TILES):
                            if t >= 1:
                                s.wait_ge(sms["xq"], C["xq"] + t)
                            sop(s, s.activation, t1b[:], R[:, (tbase + t) * D:(tbase + t + 1) * D], AF.Identity,
                                bias=negm[:, t:t + 1], scale=1.0, _pub="t1")
                            if t >= 1:
                                s.wait_ge(sms["xqtr"], C["xqtr"] + t)
                                par = (t - 1) % 2
                                sop(s, s.copy,
                                    xqT[:].rearrange("p (c x) -> p c x", x=SC_TILES * 128)[:, :, (t - 1) * 128:t * 128],
                                    ptrA[:, par * 1024:(par + 1) * 1024].rearrange("p (c x) -> p c x", x=128)
                                    , _pub="xqev")
                        s.wait_ge(sms["xqtr"], C["xqtr"] + SC_TILES)
                        par = (SC_TILES - 1) % 2
                        sop(s, s.copy,
                            xqT[:].rearrange("p (c x) -> p c x", x=SC_TILES * 128)[:, :, (SC_TILES - 1) * 128:SC_TILES * 128],
                            ptrA[:, par * 1024:(par + 1) * 1024].rearrange("p (c x) -> p c x", x=128)
                            , _pub="xqev")
                    @blk.tensor
                    def _(t_):
                        for t in range(SC_TILES):
                            t_.wait_ge(sms["xq"], C["xq"] + t + 1)
                            if t >= 2:
                                t_.wait_ge(sms["xqev"], C["xqev"] + t - 1)
                            par = t % 2
                            for k in range(8):
                                ins = t_.transpose(ptrA[:, par * 1024 + k * 128:par * 1024 + (k + 1) * 128],
                                                   xqtile[:, k * 128:(k + 1) * 128], ident[:])
                            pub(ins, "xqtr")
            C["ssq"] += 1; C["stats"] += 1; C["std"] += 1; C["rstd"] += 1
            C["t1"] += SC_TILES; C["xq"] += SC_TILES; C["xqtr"] += SC_TILES; C["xqev"] += SC_TILES

        def ffn_stage_b(tbase):
            xqT, wgq, wuq, wdq = FB["xqT"], FB["wgq"], FB["wuq"], FB["wdq"]
            sgt, upt, hqt, hqT = FB["sgt"], FB["upt"], FB["hqt"], FB["hqT"]
            with nc.Block() as blk:
                with (
                    nc.psum_tensor(un("pgu"), [128, 4 * 512], F32) as pgu,
                    nc.psum_tensor(un("pd"), [128, 1024], F32) as pd,
                    nc.psum_tensor(un("ptrB"), [128, WG_MAX * 128], BF16, is_transpose_dest=True) as ptrB,
                ):
                    @blk.sync
                    def _(sy):
                        for g in range(4):
                            if g >= 2:
                                sy.wait_ge(sms["gu"], C["gu"] + (g - 1) * SC_TILES)
                                sy.wait_ge(sms["down"], C["down"] + (g - 1) * SC_TILES)
                            nch = FFN_GROUPS[g]
                            off = FFN_GOFF[g]
                            buf = g % 2
                            src_g = wfgqT[:, off * 128:(off + nch) * 128].rearrange("(c p) w -> p c w", p=128)
                            src_u = wfuqT[:, off * 128:(off + nch) * 128].rearrange("(c p) w -> p c w", p=128)
                            wsem = [("wta", "wua", "wda"), ("wtb", "wub", "wdb")][buf]
                            sy.dma_start(wgq[buf][:, :8 * nch * 128].rearrange("p (c w) -> p c w", w=nch * 128), src_g).then_inc(sms[wsem[0]], 16)
                            sy.dma_start(wuq[buf][:, :8 * nch * 128].rearrange("p (c w) -> p c w", w=nch * 128), src_u).then_inc(sms[wsem[1]], 16)
                            src_d = wfdqT[off * 128:(off + nch) * 128, :].rearrange("(c p) w -> p c w", p=128)
                            sy.dma_start(wdq[buf][:, :nch * 1024].rearrange("p (c w) -> p c w", w=1024), src_d).then_inc(sms[wsem[2]], 16)
                    NDN = 4 * SC_TILES
                    gof = lambda dn: dn // SC_TILES

                    def t_gu(t_, dn):
                        g = gof(dn); nch = FFN_GROUPS[g]; s1 = GSUB[nch][1]
                        t = dn % SC_TILES; buf = g % 2
                        if t == 0:
                            wsem = [("wta", "wua", "wda"), ("wtb", "wub", "wdb")][buf]
                            for _w in wsem:
                                t_.wait_ge(sms[_w], C[_w] + 16 * (g // 2 + 1))
                        if dn >= 1:
                            t_.wait_ge(sms["guev"], C["guev"] + dn)
                        for c in range(8):
                            stat = xqT[:, c * SC_TILES * 128 + t * 128: c * SC_TILES * 128 + (t + 1) * 128]
                            t_.matmul(pgu[:, 0:512], stat,
                                      wgq[buf][:, c * nch * 128: c * nch * 128 + 512],
                                      start=(c == 0), stop=(c == 7))
                            t_.matmul(pgu[:, 512:1024], stat,
                                      wuq[buf][:, c * nch * 128: c * nch * 128 + 512],
                                      start=(c == 0), stop=(c == 7))
                            t_.matmul(pgu[:, 1024:1024 + s1], stat,
                                      wgq[buf][:, c * nch * 128 + 512: c * nch * 128 + 512 + s1],
                                      start=(c == 0), stop=(c == 7))
                            ins = t_.matmul(pgu[:, 1536:1536 + s1], stat,
                                            wuq[buf][:, c * nch * 128 + 512: c * nch * 128 + 512 + s1],
                                            start=(c == 0), stop=(c == 7))
                        pub(ins, "gu")

                    def t_trdown(t_, dn):
                        g = gof(dn); nch = FFN_GROUPS[g]; buf = g % 2
                        t_.wait_ge(sms["hq"], C["hq"] + dn + 1)
                        if dn >= 1:
                            t_.wait_ge(sms["hqev"], C["hqev"] + dn)
                        for k in range(nch):
                            ins = t_.transpose(ptrB[:, k * 128:(k + 1) * 128], hqt[:, k * 128:(k + 1) * 128], ident[:])
                        pub(ins, "hqtr")
                        t_.wait_ge(sms["hqev"], C["hqev"] + dn + 1)
                        if dn >= 1:
                            t_.wait_ge(sms["radd"], C["radd"] + dn)
                        for c in range(nch):
                            for half in range(2):
                                ins = t_.matmul(pd[:, half * 512:(half + 1) * 512],
                                                hqT[:, c * 128:(c + 1) * 128],
                                                wdq[buf][:, c * 1024 + half * 512: c * 1024 + (half + 1) * 512],
                                                start=(c == 0), stop=(c == nch - 1))
                        pub(ins, "down")

                    @blk.tensor
                    def _(t_):
                        t_gu(t_, 0)
                        for dn in range(NDN):
                            if dn + 1 < NDN:
                                t_gu(t_, dn + 1)
                            t_trdown(t_, dn)

                    def s_evac(s, dn):
                        g = gof(dn); nch = FFN_GROUPS[g]; s1 = GSUB[nch][1]
                        s.wait_ge(sms["gu"], C["gu"] + dn + 1)
                        if dn >= 1:
                            s.wait_ge(sms["hread"], C["hread"] + dn)
                        sop(s, s.activation, sgt[:, 0:512], pgu[:, 0:512], AF.Silu)
                        sop(s, s.activation, sgt[:, 512:512 + s1], pgu[:, 1024:1024 + s1], AF.Silu)
                        sop(s, s.copy, upt[:, 0:512], pgu[:, 512:1024])
                        sop(s, s.copy, upt[:, 512:512 + s1], pgu[:, 1536:1536 + s1], _pub="guev")

                    @blk.scalar
                    def _(s):
                        s_evac(s, 0)
                        for dn in range(NDN):
                            if dn + 1 < NDN:
                                s_evac(s, dn + 1)
                            nch = FFN_GROUPS[gof(dn)]
                            s.wait_ge(sms["hqtr"], C["hqtr"] + dn + 1)
                            sop(s, s.copy, hqT[:, :nch * 128], ptrB[:, :nch * 128], _pub="hqev")

                    @blk.vector
                    def _(v):
                        for dn in range(NDN):
                            g = gof(dn); nch = FFN_GROUPS[g]
                            width = nch * 128
                            nb = width // 16
                            hbuf = FB["hprod"][:, :width]
                            v.wait_ge(sms["guev"], C["guev"] + dn + 1)
                            vop(v, v.tensor_tensor, out=hbuf, in0=sgt[:, :width],
                                in1=upt[:, :width], op=Op.mult, _pub="hread")
                            if dn >= 1:
                                v.wait_ge(sms["hqtr"], C["hqtr"] + dn)
                            vop(v, v.tensor_reduce, out=bm[:, :nb], in_=_r3(hbuf), axis=AX.X,
                                op=Op.max, apply_absolute_value=True)
                            quant_scale(v, bm[:, :nb], nb)
                            rs_b = rscq[:, :nb].unsqueeze(2).broadcast_to([128, nb, 16])
                            sc_b = scq[:, :nb].unsqueeze(2).broadcast_to([128, nb, 16])
                            quant_chain(v, hbuf, hqt[:, :width], width, rs_b, sc_b, pub_last="hq")
                            if dn >= 1:
                                v.wait_ge(sms["down"], C["down"] + dn)
                                tile = tbase + (dn - 1) % SC_TILES
                                vop(v, v.scalar_tensor_tensor, out=R[:, tile * D:(tile + 1) * D],
                                    in0=pd[:, :1024], scalar=0.5,
                                    in1=R[:, tile * D:(tile + 1) * D], op0=Op.mult, op1=Op.add, _pub="radd")
                        v.wait_ge(sms["down"], C["down"] + NDN)
                        tile = tbase + (NDN - 1) % SC_TILES
                        vop(v, v.scalar_tensor_tensor, out=R[:, tile * D:(tile + 1) * D],
                            in0=pd[:, :1024], scalar=0.5,
                            in1=R[:, tile * D:(tile + 1) * D], op0=Op.mult, op1=Op.add, _pub="radd")
            for _w in ("wta", "wua", "wda", "wtb", "wub", "wdb"):
                C[_w] += 32
            C["gu"] += 4 * SC_TILES
            C["guev"] += 4 * SC_TILES
            C["hq"] += 4 * SC_TILES; C["hread"] += 4 * SC_TILES
            C["hqtr"] += 4 * SC_TILES; C["hqev"] += 4 * SC_TILES
            C["down"] += 4 * SC_TILES; C["radd"] += 4 * SC_TILES

        def emit_ffn():
            import contextlib as _ctx
            with _ctx.ExitStack() as fst:
                fsb = lambda name, shape, ty: fst.enter_context(nc.sbuf_tensor(un(name), shape, ty))
                FB["xqT"] = fsb("xqT", [128, 8 * SC_TILES * 128], BF16)
                T1B["b"] = fsb("t1b", [128, D], F32)
                FB["wgq"] = [fsb(f"wgq{i}", [128, 8 * WG_MAX * 128], BF16) for i in range(2)]
                FB["wuq"] = [fsb(f"wuq{i}", [128, 8 * WG_MAX * 128], BF16) for i in range(2)]
                FB["wdq"] = [fsb(f"wdq{i}", [128, WG_MAX * 1024], BF16) for i in range(2)]
                FB["sgt"] = fsb("sgt", [128, WG_MAX * 128], F32)
                FB["upt"] = fsb("upt", [128, WG_MAX * 128], F32)
                FB["hqt"] = fsb("hqt", [128, WG_MAX * 128], BF16)
                FB["hprod"] = fsb("hprod", [128, WG_MAX * 128], F32)
                FB["hqT"] = fsb("hqT", [128, WG_MAX * 128], BF16)
                for sc in range(int(_os.environ.get("FFNSC", N_SC))):
                    ffn_stage_a(sc * SC_TILES)
                    if _os.environ.get("FFNB", "1") == "1":
                        ffn_stage_b(sc * SC_TILES)

        AB = {}

        def attn_ln(sbase):
            xaT = AB["xaT"]
            with nc.Block() as blk:
                with nc.psum_tensor(un("ptrL"), [128, 2 * 8 * 128], BF16, is_transpose_dest=True) as ptrL:
                    @blk.vector
                    def _(v):
                        nco = NT_SEQ
                        for t in range(NT_SEQ):
                            vop(v, v.tensor_reduce, out=msum[:, t:t + 1], in_=R[:, (sbase + t) * D:(sbase + t + 1) * D],
                                axis=AX.X, op=Op.add)
                        v.wait_ge(sms["ssq"], C["ssq"] + 1)
                        vop(v, v.tensor_scalar, out=negm[:, :nco], in0=msum[:, :nco], scalar1=-1.0 / D, scalar2=None, op0=Op.mult)
                        vop(v, v.tensor_scalar, out=stA[:, :nco], in0=ssq[:, :nco], scalar1=1.0 / D, scalar2=None, op0=Op.mult)
                        vop(v, v.tensor_tensor, out=stB[:, :nco], in0=negm[:, :nco], in1=negm[:, :nco], op=Op.mult)
                        vop(v, v.tensor_tensor, out=stC[:, :nco], in0=stA[:, :nco], in1=stB[:, :nco], op=Op.subtract)
                        vop(v, v.tensor_scalar, out=stA[:, :nco], in0=stC[:, :nco], scalar1=EPS, scalar2=None,
                            op0=Op.add, _pub="stats")
                        v.wait_ge(sms["std"], C["std"] + 1)
                        vop(v, v.reciprocal, rstd[:, :nco], stB[:, :nco])
                        vop(v, v.tensor_tensor, out=stC[:, :nco], in0=negm[:, :nco], in1=rstd[:, :nco],
                            op=Op.mult, _pub="rstd")
                    @blk.scalar
                    def _(s):
                        for t in range(NT_SEQ):
                            sop(s, s.activation, qb["a"][:, :D], R[:, (sbase + t) * D:(sbase + t + 1) * D],
                                AF.Square, accum_out=ssq[:, t:t + 1],
                                _pub="ssq" if t == NT_SEQ - 1 else None)
                        s.wait_ge(sms["stats"], C["stats"] + 1)
                        sop(s, s.activation, stB[:, :NT_SEQ], stA[:, :NT_SEQ], AF.Sqrt, _pub="std")
                        s.wait_ge(sms["rstd"], C["rstd"] + 1)
                        for t in range(NT_SEQ):
                            if t >= 1:
                                s.wait_ge(sms["atr"], C["atr"] + t)
                            sop(s, s.activation, t1a[:], R[:, (sbase + t) * D:(sbase + t + 1) * D], AF.Identity,
                                bias=stC[:, t:t + 1], scale=rstd[:, t:t + 1], _pub="at1")
                            if t >= 1:
                                par = (t - 1) % 2
                                sop(s, s.copy,
                                    xaT[:].rearrange("p (c x) -> p c x", x=S)[:, :, (t - 1) * 128:t * 128],
                                    ptrL[:, par * 1024:(par + 1) * 1024].rearrange("p (c x) -> p c x", x=128)
                                    , _pub="atev")
                        s.wait_ge(sms["atr"], C["atr"] + NT_SEQ)
                        par = (NT_SEQ - 1) % 2
                        sop(s, s.copy,
                            xaT[:].rearrange("p (c x) -> p c x", x=S)[:, :, (NT_SEQ - 1) * 128:NT_SEQ * 128],
                            ptrL[:, par * 1024:(par + 1) * 1024].rearrange("p (c x) -> p c x", x=128)
                            , _pub="atev")
                    @blk.tensor
                    def _(t_):
                        for t in range(NT_SEQ):
                            t_.wait_ge(sms["at1"], C["at1"] + t + 1)
                            if t >= 2:
                                t_.wait_ge(sms["atev"], C["atev"] + t - 1)
                            par = t % 2
                            for k in range(8):
                                ins = t_.transpose(ptrL[:, par * 1024 + k * 128:par * 1024 + (k + 1) * 128],
                                                   t1a[:, k * 128:(k + 1) * 128], ident[:])
                            pub(ins, "atr")
            C["ssq"] += 1; C["stats"] += 1; C["std"] += 1; C["rstd"] += 1
            C["at1"] += NT_SEQ; C["atr"] += NT_SEQ; C["atev"] += NT_SEQ

        def emit_head(s_i, h, sbase):
            xaT, wqh, wkh, wvh, wgh, woh = AB["xaT"], AB["wqh"], AB["wkh"], AB["wvh"], AB["wgh"], AB["woh"]
            tabq, tabk, qsb, ksb, trot = AB["tabq"], AB["tabk"], AB["qsb"], AB["ksb"], AB["trot"]
            lostage = AB["lostage"]
            qD, kE = qsb, ksb
            vsb, gsb, ysb, sT, csb = AB["vsb"], AB["gsb"], AB["ysb"], AB["sT"], AB["csb"]
            cT, phi, yss, gat, gatT, dlb = AB["cT"], AB["phi"], AB["yss"], AB["gat"], AB["gatT"], AB["dlb"]
            with nc.Block() as blk:
                with (
                    nc.psum_tensor(un("pq"), [128, 2 * 512], F32) as pq,
                    nc.psum_tensor(un("pr"), [128, 2 * 512], F32) as pr,
                    nc.psum_tensor(un("pv2"), [128, 2 * 512], F32) as pv2,
                ):
                    @blk.sync
                    def _(sy):
                        sy.wait_ge(sms["qd"], C["qd"])
                        for idx in range(8):
                            qc, which = idx // 2, idx % 2
                            slot = qc % 2
                            if qc >= 2:
                                sy.wait_ge(sms["qd"], C["qd"] + 2 * (qc - 2) + which + 1)
                            tab_d = rotq_d if which == 0 else rotk_d
                            tab_s = tabq if which == 0 else tabk
                            semn = [("tqa", "tqb"), ("tka", "tkb")][which][slot]
                            sy.dma_start(
                                tab_s[:, slot * 1024:(slot + 1) * 1024].rearrange("p (a x) -> p a x", x=512),
                                tab_d[h][:, :, qc * 512:(qc + 1) * 512].transpose([1, 0, 2])
                            ).then_inc(sms[semn], 16)
                            lo_d = rotqlo_d if which == 0 else rotklo_d
                            n_lo = C["tq"] // 16 + idx
                            sy.wait_ge(sms["tcb"], n_lo)
                            sy.dma_start(
                                lostage[:].rearrange("p (a x) -> p a x", x=512),
                                lo_d[h][:, :, qc * 512:(qc + 1) * 512].transpose([1, 0, 2])
                            ).then_inc(sms["tq"], 16)
                    @blk.gpsimd
                    def _(g):
                        g.wait_ge(sms["qmm"], C["qmm"])
                        g.dma_start(wqh[:].rearrange("p (c w) -> p c w", w=128),
                                    wq_d[:, h * 128:(h + 1) * 128].rearrange("(c p) w -> p c w", p=128)).then_inc(sms["lwq"], 16)
                        g.dma_start(wkh[:].rearrange("p (c w) -> p c w", w=128),
                                    wk_d[:, h * 128:(h + 1) * 128].rearrange("(c p) w -> p c w", p=128)).then_inc(sms["lwk"], 16)
                        g.wait_ge(sms["vg"], C["vg"])
                        g.dma_start(wvh[:].rearrange("p (c w) -> p c w", w=HD),
                                    wv_d[:, h * HD:(h + 1) * HD].rearrange("(c p) w -> p c w", p=128)).then_inc(sms["lwv"], 16)
                        g.dma_start(wgh[:].rearrange("p (c w) -> p c w", w=HD),
                                    wgt_d[:, h * HD:(h + 1) * HD].rearrange("(c p) w -> p c w", p=128)).then_inc(sms["lwg"], 16)
                        g.wait_ge(sms["dl"], C["dl"])
                        g.dma_start(woh[:].rearrange("p (c w) -> p c w", w=D),
                                    wo_d[h * HD:(h + 1) * HD, :].rearrange("(c p) w -> p c w", p=128)).then_inc(sms["lwo"], 16)
                    @blk.tensor
                    def _(t_):
                        t_.wait_ge(sms["lwq"], C["lwq"] + 16)
                        t_.wait_ge(sms["lwk"], C["lwk"] + 16)
                        for qc in range(4):
                            par = qc % 2
                            if qc >= 1:
                                t_.wait_ge(sms["qev"], C["qev"] + 2 * qc)
                            for c in range(8):
                                ins = t_.matmul(pq[:, par * 512:(par + 1) * 512],
                                                wqh[:, c * 128:(c + 1) * 128],
                                                xaT[:, c * S + qc * 512:c * S + (qc + 1) * 512], start=(c == 0), stop=(c == 7))
                            pub(ins, "qmm")
                            for c in range(8):
                                ins = t_.matmul(pq[:, (1 - par) * 512:(2 - par) * 512],
                                                wkh[:, c * 128:(c + 1) * 128],
                                                xaT[:, c * S + qc * 512:c * S + (qc + 1) * 512], start=(c == 0), stop=(c == 7))
                            pub(ins, "qmm")
                        for idx in range(8):
                            qc, which = idx // 2, idx % 2
                            t_.wait_ge(sms["qev"], C["qev"] + 2 * qc + which + 1)
                            if idx >= 2:
                                t_.wait_ge(sms["qd"], C["qd"] + idx - 1)
                            par = idx % 2
                            src = qsb if which == 0 else ksb
                            pub(t_.matmul(pr[:, par * 512:(par + 1) * 512], rmat[:], src[:, qc * 512:(qc + 1) * 512],
                                          start=True, stop=True), "rot")
                        t_.wait_ge(sms["lwv"], C["lwv"] + 16)
                        t_.wait_ge(sms["lwg"], C["lwg"] + 16)
                        t_.wait_ge(sms["lwo"], C["lwo"] + 16)
                        t_.wait_ge(sms["qd"], C["qd"] + 8)
                        for t in range(NT_SEQ):
                            par = t % 2
                            if t >= 2:
                                t_.wait_ge(sms["vgev"], C["vgev"] + t - 1)
                            for c in range(8):
                                stat = xaT[:, c * S + t * 128:c * S + (t + 1) * 128]
                                t_.matmul(pv2[:, par * 512:par * 512 + 256], stat, wvh[:, c * HD:(c + 1) * HD],
                                          start=(c == 0), stop=(c == 7))
                            for c in range(8):
                                stat = xaT[:, c * S + t * 128:c * S + (t + 1) * 128]
                                ins = t_.matmul(pv2[:, par * 512 + 256:(par + 1) * 512], stat, wgh[:, c * HD:(c + 1) * HD],
                                                start=(c == 0), stop=(c == 7))
                            pub(ins, "vg")
                    @blk.scalar
                    def _(s):
                        for idx in range(8):
                            qc, which = idx // 2, idx % 2
                            par = qc % 2 if which == 0 else 1 - (qc % 2)
                            s.wait_ge(sms["qmm"], C["qmm"] + idx + 1)
                            dst = qsb if which == 0 else ksb
                            sop(s, s.copy, dst[:, qc * 512:(qc + 1) * 512], pq[:, par * 512:(par + 1) * 512]
                                , _pub="qev")
                        for t in range(NT_SEQ):
                            par = t % 2
                            s.wait_ge(sms["vg"], C["vg"] + t + 1)
                            sop(s, s.copy, vsb[:, t * HD:(t + 1) * HD], pv2[:, par * 512:par * 512 + 256])
                            sop(s, s.activation, gsb[:, t * HD:(t + 1) * HD], pv2[:, par * 512 + 256:(par + 1) * 512],
                                AF.Silu, _pub="vgev")
                    @blk.vector
                    def _(v):
                        for idx in range(8):
                            qc, which = idx // 2, idx % 2
                            slot = qc % 2
                            semn = [("tqa", "tqb"), ("tka", "tkb")][which][slot]
                            v.wait_ge(sms[semn], C[semn] + 16 * (qc // 2 + 1))
                            v.wait_ge(sms["tq"], C["tq"] + 16 * (idx + 1))
                            tab = tabq if which == 0 else tabk
                            vop(v, v.scalar_tensor_tensor, out=tab[:, slot * 1024:(slot + 1) * 1024],
                                in0=lostage[:], scalar=2.0 ** -40,
                                in1=tab[:, slot * 1024:(slot + 1) * 1024],
                                op0=Op.mult, op1=Op.add, _pub="tcb")
                            v.wait_ge(sms["rot"], C["rot"] + idx + 1)
                            v.wait_ge(sms["qev"], C["qev"] + 2 * qc + which + 1)
                            par = idx % 2
                            tab = tabq if which == 0 else tabk
                            src = qsb if which == 0 else ksb
                            dst = qD if which == 0 else kE
                            sl = slice(qc * 512, (qc + 1) * 512)
                            vop(v, v.tensor_tensor, out=trot[:], in0=pr[:, par * 512:(par + 1) * 512],
                                in1=tab[:, slot * 1024 + 512:slot * 1024 + 1024], op=Op.mult)
                            vop(v, v.tensor_tensor, out=dst[:, sl], in0=src[:, sl],
                                in1=tab[:, slot * 1024:slot * 1024 + 512], op=Op.mult)
                            vop(v, v.tensor_tensor, out=dst[:, sl], in0=dst[:, sl], in1=trot[:],
                                op=Op.add, _pub="qd")
                order = [(qc, kt) for qc in range(4) for kt in range((qc + 1) * 4)]
                skinds = [kt - qc * 4 >= 0 for (qc, kt) in order]
                nT = lambda k: sum(1 for k2 in skinds[:k] if k2)
                nF = lambda k: sum(1 for k2 in skinds[:k] if not k2)
                with (
                    nc.psum_tensor(un("ps"), [128, 3 * 512], F32) as ps,
                    nc.psum_tensor(un("po"), [128, 4 * 512], F32) as po,
                    nc.psum_tensor(un("pc"), [128, 512], F32) as pc,
                ):
                    @blk.tensor
                    def _(t_):
                        t_.wait_ge(sms["qd"], C["qd"] + 8)
                        t_.wait_ge(sms["vgev"], C["vgev"] + NT_SEQ)

                        def smm_emit(j):
                            qc, kt = order[j]
                            sl = j % 3
                            if j >= 3:
                                t_.wait_ge(sms["sev"], C["sev"] + nT(j - 2))
                                t_.wait_ge(sms["sevs"], C["sevs"] + nF(j - 2))
                            pub(t_.matmul(ps[:, sl * 512:(sl + 1) * 512], kE[:, kt * 128:(kt + 1) * 128],
                                          qD[:, qc * 512:(qc + 1) * 512], start=True, stop=True), "smm")

                        def out_emit(j):
                            qc, kt = order[j]
                            sl = j % 3
                            nkt = (qc + 1) * 4
                            if qc >= 1 and kt == 0:
                                t_.wait_ge(sms["yev"], C["yev"] + qc)
                            t_.wait_ge(sms["sev"], C["sev"] + nT(j + 1))
                            t_.wait_ge(sms["sevs"], C["sevs"] + nF(j + 1))
                            last_po = t_.matmul(pc[0:1, :512], ones1[:], sT[sl][:], start=(kt == 0), stop=(kt == nkt - 1))
                            for isub in range(4):
                                if kt <= qc * 4 + isub:
                                    last_po = t_.matmul(po[:, isub * 512:isub * 512 + 256],
                                                        sT[sl][:, isub * 128:(isub + 1) * 128],
                                                        vsb[:, kt * HD:(kt + 1) * HD],
                                                        start=(kt == 0), stop=(kt == qc * 4 + isub))
                            if kt == nkt - 1:
                                pub(last_po, "omm")

                        smm_emit(0)
                        smm_emit(1)
                        for j in range(len(order)):
                            if j + 2 < len(order):
                                smm_emit(j + 2)
                            out_emit(j)
                    @blk.vector
                    def _(v):
                        for j, (qc, kt) in enumerate(order):
                            sl = j % 3
                            delta = kt - qc * 4
                            if delta >= 0:
                                v.wait_ge(sms["smm"], C["smm"] + j + 1)
                                vop(v, v.tensor_tensor, out=sT[sl][:], in0=ps[:, sl * 512:(sl + 1) * 512],
                                    in1=masks[:, delta * 512:(delta + 1) * 512], op=Op.mult, _pub="sev")
                    @blk.scalar
                    def _(s):
                        for j, (qc, kt) in enumerate(order):
                            sl = j % 3
                            delta = kt - qc * 4
                            if delta < 0:
                                s.wait_ge(sms["smm"], C["smm"] + j + 1)
                                sop(s, s.copy, sT[sl][:], ps[:, sl * 512:(sl + 1) * 512], _pub="sevs")
                            if kt == (qc + 1) * 4 - 1:
                                s.wait_ge(sms["omm"], C["omm"] + qc + 1)
                                sop(s, s.copy, csb[0:1, qc * 512:(qc + 1) * 512], pc[0:1, :512])
                                sop(s, s.copy, ysb[:, qc * 4 * HD:(qc + 1) * 4 * HD].rearrange("p (i w) -> p i w", w=256),
                                    po[:].rearrange("p (i z w) -> p i (z w)", z=2, w=256)[:, :, 0:256], _pub="yev")
                with (
                    nc.psum_tensor(un("pt2"), [128, 512], BF16, is_transpose_dest=True) as pt2,
                    nc.psum_tensor(un("pct"), [128, 512], F32) as pct,
                    nc.psum_tensor(un("pdl"), [128, 2 * 1024], F32) as pdl,
                ):
                    @blk.tensor
                    def _(t_):
                        t_.wait_ge(sms["yev"], C["yev"] + 4)
                        for i in range(NT_SEQ):
                            ins = t_.matmul(pct[:, i:i + 1], csb[0:1, i * 128:(i + 1) * 128], ones1[0:1, 0:1],
                                            start=True, stop=True)
                        pub(ins, "csum")
                        for t in range(NT_SEQ):
                            t_.wait_ge(sms["gat"], C["gat"] + t + 1)
                            par = t % 2
                            if t >= 2:
                                t_.wait_ge(sms["gtev"], C["gtev"] + t - 1)
                            for k in range(2):
                                ins = t_.transpose(pt2[:, par * 256 + k * 128:par * 256 + (k + 1) * 128],
                                                   gat[:, par * HD + k * 128:par * HD + (k + 1) * 128], ident[:])
                            pub(ins, "gtr")
                            t_.wait_ge(sms["gtev"], C["gtev"] + t + 1)
                            if t >= 2:
                                t_.wait_ge(sms["dlev"], C["dlev"] + t - 1)
                            for k in range(2):
                                for half in range(2):
                                    ins = t_.matmul(pdl[:, par * 1024 + half * 512:par * 1024 + (half + 1) * 512],
                                                    gatT[:, par * 256 + k * 128:par * 256 + (k + 1) * 128],
                                                    woh[:, k * D + half * 512:k * D + (half + 1) * 512],
                                                    start=(k == 0), stop=(k == 1))
                            pub(ins, "dl")
                    @blk.vector
                    def _(v):
                        v.wait_ge(sms["csum"], C["csum"] + 1)
                        vop(v, v.tensor_copy, cT[:, :NT_SEQ], pct[:, :NT_SEQ])
                        vop(v, v.tensor_reduce, out=stA[:, :NT_SEQ], in_=cT[:, :NT_SEQ].unsqueeze(2), axis=AX.X,
                            op=Op.max, apply_absolute_value=True)
                        vop(v, v.tensor_scalar, out=stB[:, :NT_SEQ], in0=stA[:, :NT_SEQ], scalar1=1.0, scalar2=None, op0=Op.max)
                        vop(v, v.reciprocal, stA[:, :NT_SEQ], stB[:, :NT_SEQ])
                        vop(v, v.tensor_tensor, out=stB[:, :NT_SEQ], in0=stA[:, :NT_SEQ], in1=stA[:, :NT_SEQ], op=Op.mult)
                        v.wait_ge(sms["phi"], C["phi"] + 1)
                        vop(v, v.tensor_tensor, out=stC[:, :NT_SEQ], in0=stB[:, :NT_SEQ], in1=yss[:, :NT_SEQ], op=Op.mult)
                        vop(v, v.tensor_scalar, out=stB[:, :NT_SEQ], in0=stC[:, :NT_SEQ], scalar1=1.0 / HD, scalar2=EPS,
                            op0=Op.mult, op1=Op.add, _pub="aux")
                        v.wait_ge(sms["std"], C["std"] + 1)
                        vop(v, v.reciprocal, stC[:, :NT_SEQ], stB[:, :NT_SEQ])
                        vop(v, v.tensor_tensor, out=phi[:, :NT_SEQ], in0=stA[:, :NT_SEQ], in1=stC[:, :NT_SEQ], op=Op.mult)
                        for t in range(NT_SEQ):
                            par = t % 2
                            if t >= 2:
                                v.wait_ge(sms["gtr"], C["gtr"] + t - 1)
                            vop(v, v.scalar_tensor_tensor, out=gat[:, par * HD:(par + 1) * HD],
                                in0=ysb[:, t * HD:(t + 1) * HD], scalar=phi[:, t:t + 1],
                                in1=gsb[:, t * HD:(t + 1) * HD], op0=Op.mult, op1=Op.mult, _pub="gat")
                            if t >= 1:
                                v.wait_ge(sms["dlev"], C["dlev"] + t)
                                tile = sbase + t - 1
                                vop(v, v.tensor_tensor, out=R[:, tile * D:(tile + 1) * D],
                                    in0=R[:, tile * D:(tile + 1) * D], in1=dlb[:], op=Op.add, _pub="radd")
                        v.wait_ge(sms["dlev"], C["dlev"] + NT_SEQ)
                        tile = sbase + NT_SEQ - 1
                        vop(v, v.tensor_tensor, out=R[:, tile * D:(tile + 1) * D],
                            in0=R[:, tile * D:(tile + 1) * D], in1=dlb[:], op=Op.add, _pub="radd")
                    @blk.scalar
                    def _(s):
                        s.wait_ge(sms["yev"], C["yev"] + 4)
                        for t in range(NT_SEQ):
                            sop(s, s.activation, qb["b"][:, :HD], ysb[:, t * HD:(t + 1) * HD], AF.Square,
                                accum_out=yss[:, t:t + 1],
                                _pub="phi" if t == NT_SEQ - 1 else None)
                        s.wait_ge(sms["aux"], C["aux"] + 1)
                        sop(s, s.activation, stB[:, :NT_SEQ], stB[:, :NT_SEQ], AF.Sqrt, _pub="std")
                        for t in range(NT_SEQ):
                            par = t % 2
                            s.wait_ge(sms["gtr"], C["gtr"] + t + 1)
                            sop(s, s.copy, gatT[:, par * 256:(par + 1) * 256], pt2[:, par * 256:(par + 1) * 256]
                                , _pub="gtev")
                            s.wait_ge(sms["dl"], C["dl"] + t + 1)
                            if t >= 1:
                                s.wait_ge(sms["radd"], C["radd"] + t)
                            sop(s, s.copy, dlb[:], pdl[:, par * 1024:(par + 1) * 1024], _pub="dlev")
            for _w in ("tqa", "tqb", "tka", "tkb"):
                C[_w] += 32
            C["tq"] += 16 * 8
            C["tcb"] += 8
            for _w in ("lwq", "lwk", "lwv", "lwg", "lwo"):
                C[_w] += 16
            C["qmm"] += 8; C["qev"] += 8; C["rot"] += 8; C["qd"] += 8
            C["vg"] += NT_SEQ; C["vgev"] += NT_SEQ
            C["smm"] += 40; C["sev"] += 16; C["sevs"] += 24; C["omm"] += 4; C["yev"] += 4
            C["csum"] += 1; C["phi"] += 1; C["std"] += 1; C["aux"] += 1
            C["gat"] += NT_SEQ; C["gtr"] += NT_SEQ; C["gtev"] += NT_SEQ
            C["dl"] += NT_SEQ; C["dlev"] += NT_SEQ; C["radd"] += NT_SEQ

        def emit_attn():
            import contextlib as _ctx
            with _ctx.ExitStack() as ast_:
                asb = lambda name, shape, ty: ast_.enter_context(nc.sbuf_tensor(un(name), shape, ty))
                AB["xaT"] = asb("xaT", [128, 8 * S], BF16)
                AB["wqh"] = asb("wqh", [128, 8 * 128], BF16)
                AB["wkh"] = asb("wkh", [128, 8 * 128], BF16)
                AB["wvh"] = asb("wvh", [128, 8 * HD], BF16)
                AB["wgh"] = asb("wgh", [128, 8 * HD], BF16)
                AB["woh"] = asb("woh", [128, 2 * D], BF16)
                AB["tabq"] = asb("tabq", [128, 2 * 1024], F32)
                AB["tabk"] = asb("tabk", [128, 2 * 1024], F32)
                AB["qsb"] = asb("qsb", [128, S], BF16)
                AB["ksb"] = asb("ksb", [128, S], BF16)
                AB["trot"] = asb("trot", [128, 512], BF16)
                AB["lostage"] = asb("lostage", [128, 1024], BF16)
                AB["vsb"] = asb("vsb", [128, NT_SEQ * HD], BF16)
                AB["gsb"] = asb("gsb", [128, NT_SEQ * HD], BF16)
                AB["ysb"] = asb("ysb", [128, NT_SEQ * HD], BF16)
                AB["sT"] = [asb(f"sT{i}", [128, 512], BF16) for i in range(3)]
                AB["csb"] = asb("csb", [128, 2048], BF16)
                AB["cT"] = asb("cT", [128, NT_SEQ], F32)
                AB["phi"] = asb("phi", [128, NT_SEQ], F32)
                AB["yss"] = asb("yss", [128, NT_SEQ], F32)
                AB["gat"] = asb("gat", [128, 2 * HD], BF16)
                AB["gatT"] = asb("gatT", [128, 2 * 256], BF16)
                AB["dlb"] = asb("dlb", [128, D], BF16)
                for s_i in range(int(_os.environ.get("ATTNS", SEQ_PER_CORE))):
                    sbase = s_i * NT_SEQ
                    attn_ln(sbase)
                    for h in range(int(_os.environ.get("ATTNH", NH))):
                        emit_head(s_i, h, sbase)

        def emit_dump(i):
            with nc.Block() as blk:
                @blk.gpsimd
                def _(g):
                    g.dma_start(dbg_d[i].rearrange("(t p) d -> p t d", p=128),
                                R[:].rearrange("p (t d) -> p t d", d=D)).then_inc(sms["store"], 16)
                    g.wait_ge(sms["store"], C["store"] + 16)
            C["store"] += 16

        if upto >= 2:
            emit_ffn()
        if dump:
            emit_dump(0)
        if upto >= 3:
            emit_attn()
        if dump:
            emit_dump(1)
        if upto >= 4:
            emit_ffn()
        if dump:
            emit_dump(2)
        with nc.Block() as blk:
            @blk.gpsimd
            def _(g):
                g.dma_start(out_d.rearrange("(t p) d -> p t d", p=128),
                            R[:].rearrange("p (t d) -> p t d", d=D)).then_inc(sms["store"], 16)
                g.wait_ge(sms["store"], C["store"] + 16)
        C["store"] += 16
    return nc


def make_in_maps(L0_attn_x, L0_attn_wq, L0_attn_wk, L0_attn_wv, L0_attn_wg, L0_attn_wo,
                 L0_ffn_w_gate, L0_ffn_w_up, L0_ffn_w_down):
    rotq, rotk, rotqlo, rotklo, masks, rmat = _host_tables()
    x = np.ascontiguousarray(L0_attn_x, dtype=np.float32).reshape(B, S, D)
    in_maps = []
    for core in range(NCORES):
        xs = np.ascontiguousarray(x[core * SEQ_PER_CORE:(core + 1) * SEQ_PER_CORE].reshape(T, D))
        in_maps.append({
            "x": xs,
            "wq": np.ascontiguousarray(L0_attn_wq, dtype=np.float32),
            "wk": np.ascontiguousarray(L0_attn_wk, dtype=np.float32),
            "wv": np.ascontiguousarray(L0_attn_wv, dtype=np.float32),
            "wgt": np.ascontiguousarray(L0_attn_wg, dtype=np.float32),
            "wo": np.ascontiguousarray(L0_attn_wo, dtype=np.float32),
            "wfg": np.ascontiguousarray(L0_ffn_w_gate, dtype=np.float32),
            "wfu": np.ascontiguousarray(L0_ffn_w_up, dtype=np.float32),
            "wfd": np.ascontiguousarray(L0_ffn_w_down, dtype=np.float32),
            "rotq": rotq, "rotk": rotk, "rotqlo": rotqlo, "rotklo": rotklo,
            "masks": masks, "rmat": rmat,
        })
    return in_maps


def kernel(L0_attn_x, L0_attn_wq, L0_attn_wk, L0_attn_wv, L0_attn_wg, L0_attn_wo,
           L0_ffn_w_gate, L0_ffn_w_up, L0_ffn_w_down):
    nc = build_nc(dump=False)
    in_maps = make_in_maps(L0_attn_x, L0_attn_wq, L0_attn_wk, L0_attn_wv, L0_attn_wg,
                           L0_attn_wo, L0_ffn_w_gate, L0_ffn_w_up, L0_ffn_w_down)
    res = run_bass_kernel_spmd(nc, in_maps, core_ids=list(range(NCORES)))
    shards = [np.asarray(res.results[core]["out"]) for core in range(NCORES)]
    out = np.empty((B, S, D), dtype=shards[0].dtype)
    for core in range(NCORES):
        out[core * SEQ_PER_CORE:(core + 1) * SEQ_PER_CORE] = shards[core].reshape(SEQ_PER_CORE, S, D)
    return out


# version 11
# speedup vs baseline: 1.4295x; 1.0145x over previous
import numpy as np
import ml_dtypes
import concourse.bass as bass
import concourse.mybir as mybir
from concourse.bass_utils import run_bass_kernel_spmd
from concourse.alu_op_type import AluOpType as Op

F32 = mybir.dt.float32
BF16 = mybir.dt.bfloat16
I16 = mybir.dt.int16
AF = mybir.ActivationFunctionType
AX = mybir.AxisListType

NCORES = 8
B, S, D = 16, 2048, 1024
SEQ_PER_CORE = B // NCORES
T = SEQ_PER_CORE * S
NT = T // 128
NT_SEQ = S // 128
IF = 2816
NH, KD, HD, VD = 8, 128, 256, 2048
EPS = 1e-5
SC_TILES = 8
N_SC = NT // SC_TILES
FFN_GROUPS = [6, 6, 5, 5]
FFN_GOFF = [0, 6, 12, 17]
GSUB = {6: [512, 256], 5: [512, 128]}
BF = np.dtype(ml_dtypes.bfloat16)


def _host_tables():
    lin = np.linspace(0.0, 1.0, KD // 2)
    angle = np.repeat(1.0 / 10000.0 ** lin, 2)
    ar32 = np.arange(NH, dtype=np.float32)
    decay = np.log1p(-(np.float32(2.0) ** (np.float32(-5.0) - ar32)).astype(np.float32)
                     ).astype(np.float32).astype(np.float64)
    pos = np.arange(S, dtype=np.float64)
    sin = np.sin(pos[:, None] * angle[None, :])
    cos = np.cos(pos[:, None] * angle[None, :])
    e32 = np.exp(decay[:, None] * np.arange(S)[None, :]).astype(np.float32).astype(np.float64)
    nm = 1.0 / np.sqrt(np.cumsum(e32, axis=1))
    Dp = np.exp(pos[None, :] * decay[:, None]) * nm
    Ep = np.exp(-pos[None, :] * decay[:, None]) * (KD ** -0.5)
    rotq = np.empty((NH, 2, KD, S), dtype=np.float32)
    rotk = np.empty((NH, 2, KD, S), dtype=np.float32)
    rq64 = np.empty((NH, 2, KD, S), dtype=np.float64)
    rk64 = np.empty((NH, 2, KD, S), dtype=np.float64)
    for h in range(NH):
        rq64[h, 0] = cos.T * Dp[h][None, :]
        rq64[h, 1] = sin.T * Dp[h][None, :]
        rk64[h, 0] = cos.T * Ep[h][None, :]
        rk64[h, 1] = sin.T * Ep[h][None, :]
    rotq[:] = rq64.astype(np.float32)
    rotk[:] = rk64.astype(np.float32)
    LOSCALE = 2.0 ** 40
    rotqlo = ((rq64 - rotq.astype(np.float64)) * LOSCALE).astype(BF)
    rotklo = ((rk64 - rotk.astype(np.float64)) * LOSCALE).astype(BF)
    masks = np.zeros((4, 128, 512), dtype=BF)
    jj = np.arange(128)[:, None]
    ii = np.arange(512)[None, :]
    for d in range(4):
        masks[d] = (jj <= ii - 128 * d).astype(BF)
    rmat = np.zeros((KD, KD), dtype=BF)
    for m in range(KD // 2):
        rmat[2 * m + 1, 2 * m] = -1.0
        rmat[2 * m, 2 * m + 1] = 1.0
    return rotq, rotk, rotqlo, rotklo, masks, rmat


def _r3(ap, k=16):
    return ap.rearrange("p (b k) -> p b k", k=k)


def build_nc(dump=False, upto=99):
    import os as _os
    import contextlib
    import itertools
    nc = bass.Bass("TRN2", target_bir_lowering=False)
    dt = lambda name, shape, ty, kind: nc.dram_tensor(name, shape, ty, kind=kind).ap()
    x_in = dt("x", [T, D], F32, "ExternalInput")
    wq_d = dt("wq", [D, D], F32, "ExternalInput")
    wk_d = dt("wk", [D, D], F32, "ExternalInput")
    wv_d = dt("wv", [D, VD], F32, "ExternalInput")
    wgt_d = dt("wgt", [D, VD], F32, "ExternalInput")
    wo_d = dt("wo", [VD, D], F32, "ExternalInput")
    wfg_d = dt("wfg", [IF, D], F32, "ExternalInput")
    wfu_d = dt("wfu", [IF, D], F32, "ExternalInput")
    wfd_d = dt("wfd", [D, IF], F32, "ExternalInput")
    rotq_d = dt("rotq", [NH, 2, KD, S], F32, "ExternalInput")
    rotk_d = dt("rotk", [NH, 2, KD, S], F32, "ExternalInput")
    rotqlo_d = dt("rotqlo", [NH, 2, KD, S], BF16, "ExternalInput")
    rotklo_d = dt("rotklo", [NH, 2, KD, S], BF16, "ExternalInput")
    masks_d = dt("masks", [4, 128, 512], BF16, "ExternalInput")
    rmat_d = dt("rmat", [KD, KD], BF16, "ExternalInput")
    out_d = dt("out", [T, D], F32, "ExternalOutput")
    wfgqT = dt("wfgqT", [D, IF], BF16, "Internal")
    wfuqT = dt("wfuqT", [D, IF], BF16, "Internal")
    wfdqT = dt("wfdqT", [IF, D], BF16, "Internal")
    dbg_d = [dt(f"dbg{i}", [T, D], F32, "ExternalOutput") for i in range(3)] if dump else None

    _uc = itertools.count()
    un = lambda n: f"{n}_{next(_uc)}"
    with contextlib.ExitStack() as st:
        sb = lambda name, shape, ty: st.enter_context(nc.sbuf_tensor(un(name), shape, ty))
        R = sb("R", [128, NT * D], BF16)
        ident = sb("ident", [128, 128], BF16)
        identi = sb("identi", [128, 128], mybir.dt.int32)
        rmat = sb("rmat", [128, 128], BF16)
        masks = sb("masks", [128, 4 * 512], BF16)
        ones1 = sb("ones1", [128, 1], BF16)
        msum = sb("msum", [128, NT_SEQ], F32)
        ssq = sb("ssq", [128, NT_SEQ], F32)
        negm = sb("negm", [128, NT_SEQ], F32)
        rstd = sb("rstd", [128, NT_SEQ], F32)
        stA = sb("stA", [128, NT_SEQ], F32)
        stB = sb("stB", [128, NT_SEQ], F32)
        stC = sb("stC", [128, NT_SEQ], F32)
        bm = sb("bm", [128, 192], F32)
        scq = sb("scq", [128, 192], F32)
        rscq = sb("rscq", [128, 192], F32)
        sc2 = sb("sc2", [128, 192], F32)
        QW = 1024
        qb = dict(
            xn=sb("q_xn", [128, QW], F32),
            a=sb("q_a", [128, QW], F32),
            sg=sb("q_sg", [128, QW], BF16),
            b=sb("q_b", [128, QW], BF16), c=sb("q_c", [128, QW], BF16),
        )
        xqtile = sb("xqtile", [128, D], BF16)
        t1a = xqtile
        T1B = {}
        SEMS = ["dma", "dmg", "ssq", "stats", "std", "rstd", "t1", "xq", "xqtr",
                "xqev", "gu", "guev", "hq", "hread", "hqtr", "hqev", "down", "radd",
                "aux", "at1", "atr", "atev", "qmm", "qev", "rot", "qd",
                "vg", "vgev", "smm", "sev", "omm", "yev", "csum", "phi", "gat",
                "gtr", "gtev", "dl", "dlev", "store", "vch", "sch",
                "xin", "cst1", "cst2", "wta", "wua", "wda", "wtb", "wub", "wdb",
                "tq", "tk", "lwq", "lwk", "lwv", "lwg", "lwo", "qxn", "qab",
                "tqa", "tqb", "tka", "tkb", "tcb", "sevs", "qpd"]
        sms = {n: st.enter_context(nc.semaphore("sem_" + n)) for n in SEMS}
        C = {n: 0 for n in SEMS}
        LIVE = {n: 0 for n in SEMS}
        CHV = {"sem": "vch", "val": 0}
        CHS = {"sem": "sch", "val": 0}

        def pub(ins, name, k=1):
            ins.then_inc(sms[name], k)
            LIVE[name] += k
            return ins

        def gpub(ins, name, k=1):
            return pub(ins, name, k)

        def vop(v, meth, *a, _pub=None, **kw):
            v.wait_ge(sms[CHV["sem"]], CHV["val"])
            ins = meth(*a, **kw)
            nm = _pub if _pub is not None else "vch"
            pub(ins, nm, 1)
            CHV["sem"], CHV["val"] = nm, LIVE[nm]
            return ins

        def sop(s, meth, *a, _pub=None, **kw):
            s.wait_ge(sms[CHS["sem"]], CHS["val"])
            ins = meth(*a, **kw)
            nm = _pub if _pub is not None else "sch"
            pub(ins, nm, 1)
            CHS["sem"], CHS["val"] = nm, LIVE[nm]
            return ins

        QCV = [0]
        QCS = [0]

        def quant_chain(v, src_ap, out_ap, nf, rs_b, sc_b, pub_first=None, pub_last=None):
            xn = qb["xn"][:, :nf]
            A = qb["a"][:, :nf]
            SG = qb["sg"][:, :nf]
            b_, c_ = qb["b"], qb["c"]
            vop(v, v.tensor_tensor, out=_r3(xn), in0=_r3(src_ap), in1=rs_b, op=Op.mult)
            vop(v, v.tensor_scalar, out=SG, in0=src_ap, scalar1=0.0, scalar2=0.5,
                op0=Op.is_gt, op1=Op.subtract, _pub=pub_first)
            vop(v, v.tensor_scalar, out=A, in0=xn, scalar1=-1.0, scalar2=None, op0=Op.mult)
            vop(v, v.tensor_tensor, out=A, in0=xn, in1=A, op=Op.max)
            vop(v, v.tensor_scalar, out=b_[:, :nf], in0=A, scalar1=0.25, scalar2=None, op0=Op.is_gt)
            for thr, w in ((0.75, None), (1.25, None), (1.75, None), (2.5, 2.0), (3.5, 2.0), (5.0, 4.0)):
                if w is None:
                    vop(v, v.tensor_scalar, out=c_[:, :nf], in0=A, scalar1=thr, scalar2=None, op0=Op.is_gt)
                else:
                    vop(v, v.tensor_scalar, out=c_[:, :nf], in0=A, scalar1=thr, scalar2=w, op0=Op.is_gt, op1=Op.mult)
                vop(v, v.tensor_tensor, out=b_[:, :nf], in0=b_[:, :nf], in1=c_[:, :nf], op=Op.add)
            vop(v, v.tensor_tensor, out=b_[:, :nf], in0=b_[:, :nf], in1=SG, op=Op.mult)
            return vop(v, v.tensor_tensor, out=_r3(out_ap), in0=_r3(b_[:, :nf]), in1=sc_b, op=Op.mult, _pub=pub_last)

        def quant_scale(v, bmn, nb):
            vop(v, v.tensor_scalar, out=scq[:, :nb], in0=bmn, scalar1=1.0 / 6.0, scalar2=1e-12, op0=Op.mult, op1=Op.max)
            vop(v, v.reciprocal, rscq[:, :nb], scq[:, :nb])

        with nc.Block() as blk:
            @blk.gpsimd
            def _(g):
                gpub(g.iota(identi[:], [[1, 128]], base=0, channel_multiplier=-1), "aux")
                xi = x_in.rearrange("(t p) d -> p t d", p=128)
                Ri = R[:].rearrange("p (t d) -> p t d", d=D)
                g.dma_start(Ri, xi).then_inc(sms["xin"], 16)
            @blk.sync
            def _(sy):
                sy.dma_start(rmat[:], rmat_d).then_inc(sms["cst1"], 16)
                sy.dma_start(_r3(masks[:], k=512), masks_d.transpose([1, 0, 2])).then_inc(sms["cst2"], 16)
            @blk.vector
            def _(v):
                vop(v, v.memset, ones1[:], 1.0)
                v.wait_ge(sms["aux"], C["aux"] + 1)
                vop(v, v.tensor_scalar, out=ident[:], in0=identi[:], scalar1=0, scalar2=None, op0=Op.is_equal)
            @blk.scalar
            def _(s):
                s.wait_ge(sms["xin"], 16)
                s.wait_ge(sms["cst1"], 16)
                s.wait_ge(sms["cst2"], 16)
        C["xin"] += 16
        C["cst1"] += 16
        C["cst2"] += 16
        C["aux"] += 1

        prep = [(wfg_d, wfgqT, 22, D), (wfu_d, wfuqT, 22, D), (wfd_d, wfdqT, 8, IF)]
        if upto < 1:
            prep = []
        nprep = sum(p[2] for p in prep)
        with nc.Block() as blk:
            with (
                nc.psum_tensor(un("ptrW"), [128, IF], BF16, is_transpose_dest=True) as ptrW,
                nc.sbuf_tensor(un("wld"), [128, IF], F32) as wld,
                nc.sbuf_tensor(un("wqt"), [128, IF], BF16) as wqt,
                nc.sbuf_tensor(un("wtr"), [128, IF], BF16) as wtr,
            ):
                @blk.gpsimd
                def _(g):
                    n = 0
                    for src, dst, nchunk, width in prep:
                        for ch in range(nchunk):
                            if n >= 1:
                                g.wait_ge(sms["aux"], C["aux"] + n)
                            g.dma_start(wld[:, :width], src[ch * 128:(ch + 1) * 128, :]).then_inc(sms["dmg"], 16)
                            n += 1
                @blk.vector
                def _(v):
                    n = 0
                    for src, dst, nchunk, width in prep:
                        for ch in range(nchunk):
                            v.wait_ge(sms["dmg"], C["dmg"] + 16 * (n + 1))
                            if n >= 1:
                                v.wait_ge(sms["xqtr"], C["xqtr"] + n)
                            pieces = [(0, width)] if width <= QW else [(o, 704) for o in range(0, width, 704)]
                            nball = width // 16
                            vop(v, v.tensor_reduce, out=bm[:, :nball], in_=_r3(wld[:, :width]), axis=AX.X,
                                op=Op.max, apply_absolute_value=True)
                            quant_scale(v, bm[:, :nball], nball)
                            for pi, (off, w2) in enumerate(pieces):
                                nb = w2 // 16
                                rs_b = rscq[:, off // 16:off // 16 + nb].unsqueeze(2).broadcast_to([128, nb, 16])
                                sc_b = scq[:, off // 16:off // 16 + nb].unsqueeze(2).broadcast_to([128, nb, 16])
                                quant_chain(v, wld[:, off:off + w2], wqt[:, off:off + w2], w2, rs_b, sc_b,
                                            pub_first="aux" if pi == len(pieces) - 1 else None,
                                            pub_last="xq" if pi == len(pieces) - 1 else None)
                            n += 1
                @blk.tensor
                def _(t_):
                    n = 0
                    for src, dst, nchunk, width in prep:
                        ntr = width // 128
                        for ch in range(nchunk):
                            t_.wait_ge(sms["xq"], C["xq"] + n + 1)
                            if n >= 1:
                                t_.wait_ge(sms["xqev"], C["xqev"] + n)
                            for k in range(ntr):
                                ins = t_.transpose(ptrW[:, k * 128:(k + 1) * 128], wqt[:, k * 128:(k + 1) * 128], ident[:])
                            pub(ins, "xqtr")
                            n += 1
                @blk.scalar
                def _(s):
                    n = 0
                    for src, dst, nchunk, width in prep:
                        ntr = width // 128
                        for ch in range(nchunk):
                            s.wait_ge(sms["xqtr"], C["xqtr"] + n + 1)
                            if n >= 1:
                                s.wait_ge(sms["dma"], C["dma"] + 16 * n)
                            sop(s, s.copy, wtr[:, :ntr * 128], ptrW[:, :ntr * 128], _pub="xqev")
                            n += 1
                @blk.sync
                def _(sy):
                    n = 0
                    for src, dst, nchunk, width in prep:
                        ntr = width // 128
                        for ch in range(nchunk):
                            sy.wait_ge(sms["xqev"], C["xqev"] + n + 1)
                            dsti = dst[0:width, ch * 128:(ch + 1) * 128].rearrange("(k p) c -> p k c", p=128)
                            sy.dma_start(dsti, wtr[:, :ntr * 128].rearrange("p (k c) -> p k c", c=128)).then_inc(sms["dma"], 16)
                            n += 1
                    if n:
                        sy.wait_ge(sms["dma"], C["dma"] + 16 * n)
        C["dmg"] += 16 * nprep
        C["xq"] += nprep
        C["aux"] += nprep
        C["xqtr"] += nprep
        C["xqev"] += nprep
        C["dma"] += 16 * nprep

        WG_MAX = 6
        FB = {}

        def ffn_stage_a(tbase):
            xqT = FB["xqT"]
            t1b = T1B["b"]
            with nc.Block() as blk:
                with nc.psum_tensor(un("ptrA"), [128, 2 * 8 * 128], BF16, is_transpose_dest=True) as ptrA:
                    @blk.vector
                    def _(v):
                        nco = SC_TILES
                        for t in range(SC_TILES):
                            vop(v, v.tensor_reduce, out=msum[:, t:t + 1], in_=R[:, (tbase + t) * D:(tbase + t + 1) * D],
                                axis=AX.X, op=Op.add)
                        v.wait_ge(sms["ssq"], C["ssq"] + 1)
                        vop(v, v.tensor_scalar, out=negm[:, :nco], in0=msum[:, :nco], scalar1=-1.0 / D, scalar2=None, op0=Op.mult)
                        vop(v, v.tensor_scalar, out=stA[:, :nco], in0=ssq[:, :nco], scalar1=1.0 / D, scalar2=None, op0=Op.mult)
                        vop(v, v.tensor_tensor, out=stB[:, :nco], in0=negm[:, :nco], in1=negm[:, :nco], op=Op.mult)
                        vop(v, v.tensor_tensor, out=stC[:, :nco], in0=stA[:, :nco], in1=stB[:, :nco], op=Op.subtract)
                        vop(v, v.tensor_scalar, out=stA[:, :nco], in0=stC[:, :nco], scalar1=EPS, scalar2=None,
                            op0=Op.add, _pub="stats")
                        v.wait_ge(sms["std"], C["std"] + 1)
                        vop(v, v.reciprocal, rstd[:, :nco], stB[:, :nco], _pub="rstd")
                        nb = D // 16
                        for t in range(SC_TILES):
                            v.wait_ge(sms["t1"], C["t1"] + t + 1)
                            if t >= 1:
                                v.wait_ge(sms["xqtr"], C["xqtr"] + t)
                            vop(v, v.tensor_reduce, out=bm[:, :nb], in_=_r3(t1b[:]), axis=AX.X,
                                op=Op.max, apply_absolute_value=True)
                            vop(v, v.tensor_scalar, out=sc2[:, :nb], in0=bm[:, :nb], scalar1=rstd[:, t:t + 1],
                                scalar2=None, op0=Op.mult)
                            quant_scale(v, sc2[:, :nb], nb)
                            vop(v, v.tensor_scalar, out=sc2[:, :nb], in0=rscq[:, :nb], scalar1=rstd[:, t:t + 1],
                                scalar2=None, op0=Op.mult)
                            rs_b = sc2[:, :nb].unsqueeze(2).broadcast_to([128, nb, 16])
                            sc_b = scq[:, :nb].unsqueeze(2).broadcast_to([128, nb, 16])
                            quant_chain(v, t1b[:], xqtile[:], D, rs_b, sc_b, pub_last="xq")
                    @blk.scalar
                    def _(s):
                        for t in range(SC_TILES):
                            sop(s, s.activation, qb["a"][:, :D], R[:, (tbase + t) * D:(tbase + t + 1) * D],
                                AF.Square, accum_out=ssq[:, t:t + 1],
                                _pub="ssq" if t == SC_TILES - 1 else None)
                        s.wait_ge(sms["stats"], C["stats"] + 1)
                        sop(s, s.activation, stB[:, :SC_TILES], stA[:, :SC_TILES], AF.Sqrt, _pub="std")
                        s.wait_ge(sms["rstd"], C["rstd"] + 1)
                        for t in range(SC_TILES):
                            if t >= 1:
                                s.wait_ge(sms["xq"], C["xq"] + t)
                            sop(s, s.activation, t1b[:], R[:, (tbase + t) * D:(tbase + t + 1) * D], AF.Identity,
                                bias=negm[:, t:t + 1], scale=1.0, _pub="t1")
                            if t >= 1:
                                s.wait_ge(sms["xqtr"], C["xqtr"] + t)
                                par = (t - 1) % 2
                                sop(s, s.copy,
                                    xqT[:].rearrange("p (c x) -> p c x", x=SC_TILES * 128)[:, :, (t - 1) * 128:t * 128],
                                    ptrA[:, par * 1024:(par + 1) * 1024].rearrange("p (c x) -> p c x", x=128)
                                    , _pub="xqev")
                        s.wait_ge(sms["xqtr"], C["xqtr"] + SC_TILES)
                        par = (SC_TILES - 1) % 2
                        sop(s, s.copy,
                            xqT[:].rearrange("p (c x) -> p c x", x=SC_TILES * 128)[:, :, (SC_TILES - 1) * 128:SC_TILES * 128],
                            ptrA[:, par * 1024:(par + 1) * 1024].rearrange("p (c x) -> p c x", x=128)
                            , _pub="xqev")
                    @blk.tensor
                    def _(t_):
                        for t in range(SC_TILES):
                            t_.wait_ge(sms["xq"], C["xq"] + t + 1)
                            if t >= 2:
                                t_.wait_ge(sms["xqev"], C["xqev"] + t - 1)
                            par = t % 2
                            for k in range(8):
                                ins = t_.transpose(ptrA[:, par * 1024 + k * 128:par * 1024 + (k + 1) * 128],
                                                   xqtile[:, k * 128:(k + 1) * 128], ident[:])
                            pub(ins, "xqtr")
            C["ssq"] += 1; C["stats"] += 1; C["std"] += 1; C["rstd"] += 1
            C["t1"] += SC_TILES; C["xq"] += SC_TILES; C["xqtr"] += SC_TILES; C["xqev"] += SC_TILES

        def ffn_stage_b(tbase):
            xqT, wgq, wuq, wdq = FB["xqT"], FB["wgq"], FB["wuq"], FB["wdq"]
            sgt, upt, hqt, hqT = FB["sgt"], FB["upt"], FB["hqt"], FB["hqT"]
            with nc.Block() as blk:
                with (
                    nc.psum_tensor(un("pgu"), [128, 4 * 512], F32) as pgu,
                    nc.psum_tensor(un("pd"), [128, 1024], F32) as pd,
                    nc.psum_tensor(un("ptrB"), [128, WG_MAX * 128], BF16, is_transpose_dest=True) as ptrB,
                ):
                    @blk.sync
                    def _(sy):
                        for g in range(4):
                            if g >= 2:
                                sy.wait_ge(sms["gu"], C["gu"] + (g - 1) * SC_TILES)
                                sy.wait_ge(sms["down"], C["down"] + (g - 1) * SC_TILES)
                            nch = FFN_GROUPS[g]
                            off = FFN_GOFF[g]
                            buf = g % 2
                            src_g = wfgqT[:, off * 128:(off + nch) * 128].rearrange("(c p) w -> p c w", p=128)
                            src_u = wfuqT[:, off * 128:(off + nch) * 128].rearrange("(c p) w -> p c w", p=128)
                            wsem = [("wta", "wua", "wda"), ("wtb", "wub", "wdb")][buf]
                            sy.dma_start(wgq[buf][:, :8 * nch * 128].rearrange("p (c w) -> p c w", w=nch * 128), src_g).then_inc(sms[wsem[0]], 16)
                            sy.dma_start(wuq[buf][:, :8 * nch * 128].rearrange("p (c w) -> p c w", w=nch * 128), src_u).then_inc(sms[wsem[1]], 16)
                            src_d = wfdqT[off * 128:(off + nch) * 128, :].rearrange("(c p) w -> p c w", p=128)
                            sy.dma_start(wdq[buf][:, :nch * 1024].rearrange("p (c w) -> p c w", w=1024), src_d).then_inc(sms[wsem[2]], 16)
                    NDN = 4 * SC_TILES
                    gof = lambda dn: dn // SC_TILES

                    def t_gu(t_, dn):
                        g = gof(dn); nch = FFN_GROUPS[g]; s1 = GSUB[nch][1]
                        t = dn % SC_TILES; buf = g % 2
                        if t == 0:
                            wsem = [("wta", "wua", "wda"), ("wtb", "wub", "wdb")][buf]
                            for _w in wsem:
                                t_.wait_ge(sms[_w], C[_w] + 16 * (g // 2 + 1))
                        if dn >= 1:
                            t_.wait_ge(sms["guev"], C["guev"] + dn)
                        for c in range(8):
                            stat = xqT[:, c * SC_TILES * 128 + t * 128: c * SC_TILES * 128 + (t + 1) * 128]
                            t_.matmul(pgu[:, 0:512], stat,
                                      wgq[buf][:, c * nch * 128: c * nch * 128 + 512],
                                      start=(c == 0), stop=(c == 7))
                            t_.matmul(pgu[:, 512:1024], stat,
                                      wuq[buf][:, c * nch * 128: c * nch * 128 + 512],
                                      start=(c == 0), stop=(c == 7))
                            t_.matmul(pgu[:, 1024:1024 + s1], stat,
                                      wgq[buf][:, c * nch * 128 + 512: c * nch * 128 + 512 + s1],
                                      start=(c == 0), stop=(c == 7))
                            ins = t_.matmul(pgu[:, 1536:1536 + s1], stat,
                                            wuq[buf][:, c * nch * 128 + 512: c * nch * 128 + 512 + s1],
                                            start=(c == 0), stop=(c == 7))
                        pub(ins, "gu")

                    def t_trdown(t_, dn):
                        g = gof(dn); nch = FFN_GROUPS[g]; buf = g % 2
                        t_.wait_ge(sms["hq"], C["hq"] + dn + 1)
                        if dn >= 1:
                            t_.wait_ge(sms["hqev"], C["hqev"] + dn)
                        for k in range(nch):
                            ins = t_.transpose(ptrB[:, k * 128:(k + 1) * 128], hqt[:, k * 128:(k + 1) * 128], ident[:])
                        pub(ins, "hqtr")
                        t_.wait_ge(sms["hqev"], C["hqev"] + dn + 1)
                        if dn >= 1:
                            t_.wait_ge(sms["radd"], C["radd"] + dn)
                        for c in range(nch):
                            for half in range(2):
                                ins = t_.matmul(pd[:, half * 512:(half + 1) * 512],
                                                hqT[:, c * 128:(c + 1) * 128],
                                                wdq[buf][:, c * 1024 + half * 512: c * 1024 + (half + 1) * 512],
                                                start=(c == 0), stop=(c == nch - 1))
                        pub(ins, "down")

                    @blk.tensor
                    def _(t_):
                        t_gu(t_, 0)
                        for dn in range(NDN):
                            if dn + 1 < NDN:
                                t_gu(t_, dn + 1)
                            t_trdown(t_, dn)

                    def s_evac(s, dn):
                        g = gof(dn); nch = FFN_GROUPS[g]; s1 = GSUB[nch][1]
                        s.wait_ge(sms["gu"], C["gu"] + dn + 1)
                        if dn >= 1:
                            s.wait_ge(sms["hread"], C["hread"] + dn)
                        sop(s, s.activation, sgt[:, 0:512], pgu[:, 0:512], AF.Silu)
                        sop(s, s.activation, sgt[:, 512:512 + s1], pgu[:, 1024:1024 + s1], AF.Silu)
                        sop(s, s.copy, upt[:, 0:512], pgu[:, 512:1024])
                        sop(s, s.copy, upt[:, 512:512 + s1], pgu[:, 1536:1536 + s1], _pub="guev")

                    @blk.scalar
                    def _(s):
                        s_evac(s, 0)
                        for dn in range(NDN):
                            if dn + 1 < NDN:
                                s_evac(s, dn + 1)
                            nch = FFN_GROUPS[gof(dn)]
                            s.wait_ge(sms["hqtr"], C["hqtr"] + dn + 1)
                            sop(s, s.copy, hqT[:, :nch * 128], ptrB[:, :nch * 128], _pub="hqev")

                    @blk.vector
                    def _(v):
                        for dn in range(NDN):
                            g = gof(dn); nch = FFN_GROUPS[g]
                            width = nch * 128
                            nb = width // 16
                            hbuf = FB["hprod"][:, :width]
                            v.wait_ge(sms["guev"], C["guev"] + dn + 1)
                            vop(v, v.tensor_tensor, out=hbuf, in0=sgt[:, :width],
                                in1=upt[:, :width], op=Op.mult, _pub="hread")
                            if dn >= 1:
                                v.wait_ge(sms["hqtr"], C["hqtr"] + dn)
                            vop(v, v.tensor_reduce, out=bm[:, :nb], in_=_r3(hbuf), axis=AX.X,
                                op=Op.max, apply_absolute_value=True)
                            quant_scale(v, bm[:, :nb], nb)
                            rs_b = rscq[:, :nb].unsqueeze(2).broadcast_to([128, nb, 16])
                            sc_b = scq[:, :nb].unsqueeze(2).broadcast_to([128, nb, 16])
                            quant_chain(v, hbuf, hqt[:, :width], width, rs_b, sc_b, pub_last="hq")
                            if dn >= 1:
                                v.wait_ge(sms["down"], C["down"] + dn)
                                tile = tbase + (dn - 1) % SC_TILES
                                vop(v, v.scalar_tensor_tensor, out=R[:, tile * D:(tile + 1) * D],
                                    in0=pd[:, :1024], scalar=0.5,
                                    in1=R[:, tile * D:(tile + 1) * D], op0=Op.mult, op1=Op.add, _pub="radd")
                        v.wait_ge(sms["down"], C["down"] + NDN)
                        tile = tbase + (NDN - 1) % SC_TILES
                        vop(v, v.scalar_tensor_tensor, out=R[:, tile * D:(tile + 1) * D],
                            in0=pd[:, :1024], scalar=0.5,
                            in1=R[:, tile * D:(tile + 1) * D], op0=Op.mult, op1=Op.add, _pub="radd")
            for _w in ("wta", "wua", "wda", "wtb", "wub", "wdb"):
                C[_w] += 32
            C["gu"] += 4 * SC_TILES
            C["guev"] += 4 * SC_TILES
            C["hq"] += 4 * SC_TILES; C["hread"] += 4 * SC_TILES
            C["hqtr"] += 4 * SC_TILES; C["hqev"] += 4 * SC_TILES
            C["down"] += 4 * SC_TILES; C["radd"] += 4 * SC_TILES

        def emit_ffn():
            import contextlib as _ctx
            with _ctx.ExitStack() as fst:
                fsb = lambda name, shape, ty: fst.enter_context(nc.sbuf_tensor(un(name), shape, ty))
                FB["xqT"] = fsb("xqT", [128, 8 * SC_TILES * 128], BF16)
                T1B["b"] = fsb("t1b", [128, D], F32)
                FB["wgq"] = [fsb(f"wgq{i}", [128, 8 * WG_MAX * 128], BF16) for i in range(2)]
                FB["wuq"] = [fsb(f"wuq{i}", [128, 8 * WG_MAX * 128], BF16) for i in range(2)]
                FB["wdq"] = [fsb(f"wdq{i}", [128, WG_MAX * 1024], BF16) for i in range(2)]
                FB["sgt"] = fsb("sgt", [128, WG_MAX * 128], F32)
                FB["upt"] = fsb("upt", [128, WG_MAX * 128], F32)
                FB["hqt"] = fsb("hqt", [128, WG_MAX * 128], BF16)
                FB["hprod"] = fsb("hprod", [128, WG_MAX * 128], F32)
                FB["hqT"] = fsb("hqT", [128, WG_MAX * 128], BF16)
                for sc in range(int(_os.environ.get("FFNSC", N_SC))):
                    ffn_stage_a(sc * SC_TILES)
                    if _os.environ.get("FFNB", "1") == "1":
                        ffn_stage_b(sc * SC_TILES)

        AB = {}

        def attn_ln(sbase):
            xaT = AB["xaT"]
            with nc.Block() as blk:
                with nc.psum_tensor(un("ptrL"), [128, 2 * 8 * 128], BF16, is_transpose_dest=True) as ptrL:
                    @blk.vector
                    def _(v):
                        nco = NT_SEQ
                        for t in range(NT_SEQ):
                            vop(v, v.tensor_reduce, out=msum[:, t:t + 1], in_=R[:, (sbase + t) * D:(sbase + t + 1) * D],
                                axis=AX.X, op=Op.add)
                        v.wait_ge(sms["ssq"], C["ssq"] + 1)
                        vop(v, v.tensor_scalar, out=negm[:, :nco], in0=msum[:, :nco], scalar1=-1.0 / D, scalar2=None, op0=Op.mult)
                        vop(v, v.tensor_scalar, out=stA[:, :nco], in0=ssq[:, :nco], scalar1=1.0 / D, scalar2=None, op0=Op.mult)
                        vop(v, v.tensor_tensor, out=stB[:, :nco], in0=negm[:, :nco], in1=negm[:, :nco], op=Op.mult)
                        vop(v, v.tensor_tensor, out=stC[:, :nco], in0=stA[:, :nco], in1=stB[:, :nco], op=Op.subtract)
                        vop(v, v.tensor_scalar, out=stA[:, :nco], in0=stC[:, :nco], scalar1=EPS, scalar2=None,
                            op0=Op.add, _pub="stats")
                        v.wait_ge(sms["std"], C["std"] + 1)
                        vop(v, v.reciprocal, rstd[:, :nco], stB[:, :nco])
                        vop(v, v.tensor_tensor, out=stC[:, :nco], in0=negm[:, :nco], in1=rstd[:, :nco],
                            op=Op.mult, _pub="rstd")
                    @blk.scalar
                    def _(s):
                        for t in range(NT_SEQ):
                            sop(s, s.activation, qb["a"][:, :D], R[:, (sbase + t) * D:(sbase + t + 1) * D],
                                AF.Square, accum_out=ssq[:, t:t + 1],
                                _pub="ssq" if t == NT_SEQ - 1 else None)
                        s.wait_ge(sms["stats"], C["stats"] + 1)
                        sop(s, s.activation, stB[:, :NT_SEQ], stA[:, :NT_SEQ], AF.Sqrt, _pub="std")
                        s.wait_ge(sms["rstd"], C["rstd"] + 1)
                        for t in range(NT_SEQ):
                            if t >= 1:
                                s.wait_ge(sms["atr"], C["atr"] + t)
                            sop(s, s.activation, t1a[:], R[:, (sbase + t) * D:(sbase + t + 1) * D], AF.Identity,
                                bias=stC[:, t:t + 1], scale=rstd[:, t:t + 1], _pub="at1")
                            if t >= 1:
                                par = (t - 1) % 2
                                sop(s, s.copy,
                                    xaT[:].rearrange("p (c x) -> p c x", x=S)[:, :, (t - 1) * 128:t * 128],
                                    ptrL[:, par * 1024:(par + 1) * 1024].rearrange("p (c x) -> p c x", x=128)
                                    , _pub="atev")
                        s.wait_ge(sms["atr"], C["atr"] + NT_SEQ)
                        par = (NT_SEQ - 1) % 2
                        sop(s, s.copy,
                            xaT[:].rearrange("p (c x) -> p c x", x=S)[:, :, (NT_SEQ - 1) * 128:NT_SEQ * 128],
                            ptrL[:, par * 1024:(par + 1) * 1024].rearrange("p (c x) -> p c x", x=128)
                            , _pub="atev")
                    @blk.tensor
                    def _(t_):
                        for t in range(NT_SEQ):
                            t_.wait_ge(sms["at1"], C["at1"] + t + 1)
                            if t >= 2:
                                t_.wait_ge(sms["atev"], C["atev"] + t - 1)
                            par = t % 2
                            for k in range(8):
                                ins = t_.transpose(ptrL[:, par * 1024 + k * 128:par * 1024 + (k + 1) * 128],
                                                   t1a[:, k * 128:(k + 1) * 128], ident[:])
                            pub(ins, "atr")
                    if sbase == 0:
                        @blk.gpsimd
                        def _(g):
                            g.dma_start(AB["wqh"][0][:].rearrange("p (c w) -> p c w", w=128),
                                        wq_d[:, 0:128].rearrange("(c p) w -> p c w", p=128)).then_inc(sms["lwq"], 16)
                            g.dma_start(AB["wkh"][0][:].rearrange("p (c w) -> p c w", w=128),
                                        wk_d[:, 0:128].rearrange("(c p) w -> p c w", p=128)).then_inc(sms["lwk"], 16)
            C["ssq"] += 1; C["stats"] += 1; C["std"] += 1; C["rstd"] += 1
            C["at1"] += NT_SEQ; C["atr"] += NT_SEQ; C["atev"] += NT_SEQ

        def emit_head(s_i, h, sbase):
            gi = s_i * NH + h
            nxt = None
            if gi + 1 < SEQ_PER_CORE * NH:
                nxt = ((gi + 1) // NH, (gi + 1) % NH)
            xaT, wqh, wkh, wvh, wgh, woh = AB["xaT"], AB["wqh"], AB["wkh"], AB["wvh"], AB["wgh"], AB["woh"]
            tabq, tabk, qsb, ksb, trot = AB["tabq"], AB["tabk"], AB["qsb"], AB["ksb"], AB["trot"]
            lostage = AB["lostage"]
            qD, kE = qsb, ksb
            vsb, gsb, ysb, sT, csb = AB["vsb"], AB["gsb"], AB["ysb"], AB["sT"], AB["csb"]
            cT, phi, yss, gat, gatT, dlb = AB["cT"], AB["phi"], AB["yss"], AB["gat"], AB["gatT"], AB["dlb"]
            with nc.Block() as blk:
                with (
                    nc.psum_tensor(un("pq"), [128, 2 * 512], F32) as pq,
                    nc.psum_tensor(un("pr"), [128, 2 * 512], F32) as pr,
                    nc.psum_tensor(un("pv2"), [128, 2 * 512], F32) as pv2,
                ):
                    @blk.sync
                    def _(sy):
                        sy.wait_ge(sms["qd"], C["qd"])
                        for idx in range(8):
                            qc, which = idx // 2, idx % 2
                            slot = qc % 2
                            if qc >= 2:
                                sy.wait_ge(sms["qd"], C["qd"] + 2 * (qc - 2) + which + 1)
                            tab_d = rotq_d if which == 0 else rotk_d
                            tab_s = tabq if which == 0 else tabk
                            semn = [("tqa", "tqb"), ("tka", "tkb")][which][slot]
                            sy.dma_start(
                                tab_s[:, slot * 1024:(slot + 1) * 1024].rearrange("p (a x) -> p a x", x=512),
                                tab_d[h][:, :, qc * 512:(qc + 1) * 512].transpose([1, 0, 2])
                            ).then_inc(sms[semn], 16)
                            lo_d = rotqlo_d if which == 0 else rotklo_d
                            n_lo = C["tq"] // 16 + idx
                            sy.wait_ge(sms["tcb"], n_lo)
                            sy.dma_start(
                                lostage[:].rearrange("p (a x) -> p a x", x=512),
                                lo_d[h][:, :, qc * 512:(qc + 1) * 512].transpose([1, 0, 2])
                            ).then_inc(sms["tq"], 16)
                    @blk.gpsimd
                    def _(g):
                        if nxt is not None:
                            s2, h2 = nxt
                            sl2 = (gi + 1) % 2
                            g.wait_ge(sms["qmm"], C["qmm"])
                            g.wait_ge(sms["lwq"], 16 * (gi + 1))
                            g.wait_ge(sms["lwk"], 16 * (gi + 1))
                            g.dma_start(wqh[sl2][:].rearrange("p (c w) -> p c w", w=128),
                                        wq_d[:, h2 * 128:(h2 + 1) * 128].rearrange("(c p) w -> p c w", p=128)).then_inc(sms["lwq"], 16)
                            g.dma_start(wkh[sl2][:].rearrange("p (c w) -> p c w", w=128),
                                        wk_d[:, h2 * 128:(h2 + 1) * 128].rearrange("(c p) w -> p c w", p=128)).then_inc(sms["lwk"], 16)
                        g.wait_ge(sms["vg"], C["vg"])
                        g.dma_start(wvh[:].rearrange("p (c w) -> p c w", w=HD),
                                    wv_d[:, h * HD:(h + 1) * HD].rearrange("(c p) w -> p c w", p=128)).then_inc(sms["lwv"], 16)
                        g.dma_start(wgh[:].rearrange("p (c w) -> p c w", w=HD),
                                    wgt_d[:, h * HD:(h + 1) * HD].rearrange("(c p) w -> p c w", p=128)).then_inc(sms["lwg"], 16)
                        g.wait_ge(sms["dl"], C["dl"])
                        g.dma_start(woh[:].rearrange("p (c w) -> p c w", w=D),
                                    wo_d[h * HD:(h + 1) * HD, :].rearrange("(c p) w -> p c w", p=128)).then_inc(sms["lwo"], 16)
                    @blk.tensor
                    def _(t_):
                        wsl = gi % 2
                        t_.wait_ge(sms["lwq"], 16 * (gi + 1))
                        t_.wait_ge(sms["lwk"], 16 * (gi + 1))
                        for qc in range(4):
                            par = qc % 2
                            if qc >= 1:
                                t_.wait_ge(sms["qev"], C["qev"] + 2 * qc)
                            for c in range(8):
                                ins = t_.matmul(pq[:, par * 512:(par + 1) * 512],
                                                wqh[wsl][:, c * 128:(c + 1) * 128],
                                                xaT[:, c * S + qc * 512:c * S + (qc + 1) * 512], start=(c == 0), stop=(c == 7))
                            pub(ins, "qmm")
                            for c in range(8):
                                ins = t_.matmul(pq[:, (1 - par) * 512:(2 - par) * 512],
                                                wkh[wsl][:, c * 128:(c + 1) * 128],
                                                xaT[:, c * S + qc * 512:c * S + (qc + 1) * 512], start=(c == 0), stop=(c == 7))
                            pub(ins, "qmm")
                        for idx in range(8):
                            qc, which = idx // 2, idx % 2
                            t_.wait_ge(sms["qev"], C["qev"] + 2 * qc + which + 1)
                            if idx >= 2:
                                t_.wait_ge(sms["qd"], C["qd"] + idx - 1)
                            par = idx % 2
                            src = qsb if which == 0 else ksb
                            pub(t_.matmul(pr[:, par * 512:(par + 1) * 512], rmat[:], src[:, qc * 512:(qc + 1) * 512],
                                          start=True, stop=True), "rot")
                        t_.wait_ge(sms["lwv"], C["lwv"] + 16)
                        t_.wait_ge(sms["lwg"], C["lwg"] + 16)
                        t_.wait_ge(sms["lwo"], C["lwo"] + 16)
                        t_.wait_ge(sms["qd"], C["qd"] + 8)
                        for t in range(NT_SEQ):
                            par = t % 2
                            if t >= 2:
                                t_.wait_ge(sms["vgev"], C["vgev"] + t - 1)
                            for c in range(8):
                                stat = xaT[:, c * S + t * 128:c * S + (t + 1) * 128]
                                t_.matmul(pv2[:, par * 512:par * 512 + 256], stat, wvh[:, c * HD:(c + 1) * HD],
                                          start=(c == 0), stop=(c == 7))
                            for c in range(8):
                                stat = xaT[:, c * S + t * 128:c * S + (t + 1) * 128]
                                ins = t_.matmul(pv2[:, par * 512 + 256:(par + 1) * 512], stat, wgh[:, c * HD:(c + 1) * HD],
                                                start=(c == 0), stop=(c == 7))
                            pub(ins, "vg")
                    @blk.scalar
                    def _(s):
                        for idx in range(8):
                            qc, which = idx // 2, idx % 2
                            par = qc % 2 if which == 0 else 1 - (qc % 2)
                            s.wait_ge(sms["qmm"], C["qmm"] + idx + 1)
                            dst = qsb if which == 0 else ksb
                            sop(s, s.copy, dst[:, qc * 512:(qc + 1) * 512], pq[:, par * 512:(par + 1) * 512]
                                , _pub="qev")
                        for t in range(NT_SEQ):
                            par = t % 2
                            s.wait_ge(sms["vg"], C["vg"] + t + 1)
                            sop(s, s.copy, vsb[:, t * HD:(t + 1) * HD], pv2[:, par * 512:par * 512 + 256])
                            sop(s, s.activation, gsb[:, t * HD:(t + 1) * HD], pv2[:, par * 512 + 256:(par + 1) * 512],
                                AF.Silu, _pub="vgev")
                    @blk.vector
                    def _(v):
                        for idx in range(8):
                            qc, which = idx // 2, idx % 2
                            slot = qc % 2
                            semn = [("tqa", "tqb"), ("tka", "tkb")][which][slot]
                            v.wait_ge(sms[semn], C[semn] + 16 * (qc // 2 + 1))
                            v.wait_ge(sms["tq"], C["tq"] + 16 * (idx + 1))
                            tab = tabq if which == 0 else tabk
                            vop(v, v.scalar_tensor_tensor, out=tab[:, slot * 1024:(slot + 1) * 1024],
                                in0=lostage[:], scalar=2.0 ** -40,
                                in1=tab[:, slot * 1024:(slot + 1) * 1024],
                                op0=Op.mult, op1=Op.add, _pub="tcb")
                            v.wait_ge(sms["rot"], C["rot"] + idx + 1)
                            v.wait_ge(sms["qev"], C["qev"] + 2 * qc + which + 1)
                            par = idx % 2
                            tab = tabq if which == 0 else tabk
                            src = qsb if which == 0 else ksb
                            dst = qD if which == 0 else kE
                            sl = slice(qc * 512, (qc + 1) * 512)
                            vop(v, v.tensor_tensor, out=trot[:], in0=pr[:, par * 512:(par + 1) * 512],
                                in1=tab[:, slot * 1024 + 512:slot * 1024 + 1024], op=Op.mult)
                            vop(v, v.tensor_tensor, out=dst[:, sl], in0=src[:, sl],
                                in1=tab[:, slot * 1024:slot * 1024 + 512], op=Op.mult)
                            vop(v, v.tensor_tensor, out=dst[:, sl], in0=dst[:, sl], in1=trot[:],
                                op=Op.add, _pub="qd")
                order = [(qc, kt) for qc in range(4) for kt in range((qc + 1) * 4)]
                skinds = [kt - qc * 4 >= 0 for (qc, kt) in order]
                nT = lambda k: sum(1 for k2 in skinds[:k] if k2)
                nF = lambda k: sum(1 for k2 in skinds[:k] if not k2)
                with (
                    nc.psum_tensor(un("ps"), [128, 3 * 512], F32) as ps,
                    nc.psum_tensor(un("po"), [128, 4 * 512], F32) as po,
                    nc.psum_tensor(un("pc"), [128, 512], F32) as pc,
                ):
                    @blk.tensor
                    def _(t_):
                        t_.wait_ge(sms["qd"], C["qd"] + 8)
                        t_.wait_ge(sms["vgev"], C["vgev"] + NT_SEQ)

                        def smm_emit(j):
                            qc, kt = order[j]
                            sl = j % 3
                            if j >= 3:
                                t_.wait_ge(sms["sev"], C["sev"] + nT(j - 2))
                                t_.wait_ge(sms["sevs"], C["sevs"] + nF(j - 2))
                            pub(t_.matmul(ps[:, sl * 512:(sl + 1) * 512], kE[:, kt * 128:(kt + 1) * 128],
                                          qD[:, qc * 512:(qc + 1) * 512], start=True, stop=True), "smm")

                        def out_emit(j):
                            qc, kt = order[j]
                            sl = j % 3
                            nkt = (qc + 1) * 4
                            if qc >= 1 and kt == 0:
                                t_.wait_ge(sms["yev"], C["yev"] + qc)
                            t_.wait_ge(sms["sev"], C["sev"] + nT(j + 1))
                            t_.wait_ge(sms["sevs"], C["sevs"] + nF(j + 1))
                            last_po = t_.matmul(pc[0:1, :512], ones1[:], sT[sl][:], start=(kt == 0), stop=(kt == nkt - 1))
                            for isub in range(4):
                                if kt <= qc * 4 + isub:
                                    last_po = t_.matmul(po[:, isub * 512:isub * 512 + 256],
                                                        sT[sl][:, isub * 128:(isub + 1) * 128],
                                                        vsb[:, kt * HD:(kt + 1) * HD],
                                                        start=(kt == 0), stop=(kt == qc * 4 + isub))
                            if kt == nkt - 1:
                                pub(last_po, "omm")

                        smm_emit(0)
                        smm_emit(1)
                        for j in range(len(order)):
                            if j + 2 < len(order):
                                smm_emit(j + 2)
                            out_emit(j)
                    @blk.vector
                    def _(v):
                        for j, (qc, kt) in enumerate(order):
                            sl = j % 3
                            delta = kt - qc * 4
                            if delta >= 0:
                                v.wait_ge(sms["smm"], C["smm"] + j + 1)
                                vop(v, v.tensor_tensor, out=sT[sl][:], in0=ps[:, sl * 512:(sl + 1) * 512],
                                    in1=masks[:, delta * 512:(delta + 1) * 512], op=Op.mult, _pub="sev")
                    @blk.scalar
                    def _(s):
                        for j, (qc, kt) in enumerate(order):
                            sl = j % 3
                            delta = kt - qc * 4
                            if delta < 0:
                                s.wait_ge(sms["smm"], C["smm"] + j + 1)
                                sop(s, s.copy, sT[sl][:], ps[:, sl * 512:(sl + 1) * 512], _pub="sevs")
                            if kt == (qc + 1) * 4 - 1:
                                s.wait_ge(sms["omm"], C["omm"] + qc + 1)
                                sop(s, s.copy, csb[0:1, qc * 512:(qc + 1) * 512], pc[0:1, :512])
                                sop(s, s.copy, ysb[:, qc * 4 * HD:(qc + 1) * 4 * HD].rearrange("p (i w) -> p i w", w=256),
                                    po[:].rearrange("p (i z w) -> p i (z w)", z=2, w=256)[:, :, 0:256], _pub="yev")
                with (
                    nc.psum_tensor(un("pt2"), [128, 512], BF16, is_transpose_dest=True) as pt2,
                    nc.psum_tensor(un("pct"), [128, 512], F32) as pct,
                    nc.psum_tensor(un("pdl"), [128, 2 * 1024], F32) as pdl,
                ):
                    @blk.tensor
                    def _(t_):
                        t_.wait_ge(sms["yev"], C["yev"] + 4)
                        for i in range(NT_SEQ):
                            ins = t_.matmul(pct[:, i:i + 1], csb[0:1, i * 128:(i + 1) * 128], ones1[0:1, 0:1],
                                            start=True, stop=True)
                        pub(ins, "csum")
                        for t in range(NT_SEQ):
                            t_.wait_ge(sms["gat"], C["gat"] + t + 1)
                            par = t % 2
                            g4 = t % 4
                            if t >= 2:
                                t_.wait_ge(sms["gtev"], C["gtev"] + t - 1)
                            for k in range(2):
                                ins = t_.transpose(pt2[:, par * 256 + k * 128:par * 256 + (k + 1) * 128],
                                                   gat[:, g4 * HD + k * 128:g4 * HD + (k + 1) * 128], ident[:])
                            pub(ins, "gtr")
                            t_.wait_ge(sms["gtev"], C["gtev"] + t + 1)
                            if t >= 2:
                                t_.wait_ge(sms["dlev"], C["dlev"] + t - 1)
                            for k in range(2):
                                for half in range(2):
                                    ins = t_.matmul(pdl[:, par * 1024 + half * 512:par * 1024 + (half + 1) * 512],
                                                    gatT[:, par * 256 + k * 128:par * 256 + (k + 1) * 128],
                                                    woh[:, k * D + half * 512:k * D + (half + 1) * 512],
                                                    start=(k == 0), stop=(k == 1))
                            pub(ins, "dl")
                    @blk.vector
                    def _(v):
                        v.wait_ge(sms["csum"], C["csum"] + 1)
                        vop(v, v.tensor_copy, cT[:, :NT_SEQ], pct[:, :NT_SEQ])
                        vop(v, v.tensor_reduce, out=stA[:, :NT_SEQ], in_=cT[:, :NT_SEQ].unsqueeze(2), axis=AX.X,
                            op=Op.max, apply_absolute_value=True)
                        vop(v, v.tensor_scalar, out=stB[:, :NT_SEQ], in0=stA[:, :NT_SEQ], scalar1=1.0, scalar2=None, op0=Op.max)
                        vop(v, v.reciprocal, stA[:, :NT_SEQ], stB[:, :NT_SEQ])
                        vop(v, v.tensor_tensor, out=stB[:, :NT_SEQ], in0=stA[:, :NT_SEQ], in1=stA[:, :NT_SEQ], op=Op.mult)
                        v.wait_ge(sms["phi"], C["phi"] + 1)
                        vop(v, v.tensor_tensor, out=stC[:, :NT_SEQ], in0=stB[:, :NT_SEQ], in1=yss[:, :NT_SEQ], op=Op.mult)
                        vop(v, v.tensor_scalar, out=stB[:, :NT_SEQ], in0=stC[:, :NT_SEQ], scalar1=1.0 / HD, scalar2=EPS,
                            op0=Op.mult, op1=Op.add, _pub="aux")
                        v.wait_ge(sms["std"], C["std"] + 1)
                        vop(v, v.reciprocal, stC[:, :NT_SEQ], stB[:, :NT_SEQ])
                        vop(v, v.tensor_tensor, out=phi[:, :NT_SEQ], in0=stA[:, :NT_SEQ], in1=stC[:, :NT_SEQ], op=Op.mult)
                        for t in range(NT_SEQ):
                            g4 = t % 4
                            if t >= 4:
                                v.wait_ge(sms["gtr"], C["gtr"] + t - 3)
                            vop(v, v.scalar_tensor_tensor, out=gat[:, g4 * HD:(g4 + 1) * HD],
                                in0=ysb[:, t * HD:(t + 1) * HD], scalar=phi[:, t:t + 1],
                                in1=gsb[:, t * HD:(t + 1) * HD], op0=Op.mult, op1=Op.mult, _pub="gat")
                            if t >= 1:
                                v.wait_ge(sms["dlev"], C["dlev"] + t)
                                tile = sbase + t - 1
                                vop(v, v.tensor_tensor, out=R[:, tile * D:(tile + 1) * D],
                                    in0=R[:, tile * D:(tile + 1) * D], in1=dlb[(t - 1) % 2][:], op=Op.add, _pub="radd")
                        v.wait_ge(sms["dlev"], C["dlev"] + NT_SEQ)
                        tile = sbase + NT_SEQ - 1
                        vop(v, v.tensor_tensor, out=R[:, tile * D:(tile + 1) * D],
                            in0=R[:, tile * D:(tile + 1) * D], in1=dlb[(NT_SEQ - 1) % 2][:], op=Op.add, _pub="radd")
                    @blk.scalar
                    def _(s):
                        s.wait_ge(sms["yev"], C["yev"] + 4)
                        for t in range(NT_SEQ):
                            sop(s, s.activation, qb["b"][:, :HD], ysb[:, t * HD:(t + 1) * HD], AF.Square,
                                accum_out=yss[:, t:t + 1],
                                _pub="phi" if t == NT_SEQ - 1 else None)
                        s.wait_ge(sms["aux"], C["aux"] + 1)
                        sop(s, s.activation, stB[:, :NT_SEQ], stB[:, :NT_SEQ], AF.Sqrt, _pub="std")
                        for t in range(NT_SEQ):
                            par = t % 2
                            s.wait_ge(sms["gtr"], C["gtr"] + t + 1)
                            sop(s, s.copy, gatT[:, par * 256:(par + 1) * 256], pt2[:, par * 256:(par + 1) * 256]
                                , _pub="gtev")
                            s.wait_ge(sms["dl"], C["dl"] + t + 1)
                            if t >= 2:
                                s.wait_ge(sms["radd"], C["radd"] + t - 1)
                            sop(s, s.copy, dlb[t % 2][:], pdl[:, par * 1024:(par + 1) * 1024], _pub="dlev")
            for _w in ("tqa", "tqb", "tka", "tkb"):
                C[_w] += 32
            C["tq"] += 16 * 8
            C["tcb"] += 8
            for _w in ("lwq", "lwk", "lwv", "lwg", "lwo"):
                C[_w] += 16
            C["qmm"] += 8; C["qev"] += 8; C["rot"] += 8; C["qd"] += 8
            C["vg"] += NT_SEQ; C["vgev"] += NT_SEQ
            C["smm"] += 40; C["sev"] += 16; C["sevs"] += 24; C["omm"] += 4; C["yev"] += 4
            C["csum"] += 1; C["phi"] += 1; C["std"] += 1; C["aux"] += 1
            C["gat"] += NT_SEQ; C["gtr"] += NT_SEQ; C["gtev"] += NT_SEQ
            C["dl"] += NT_SEQ; C["dlev"] += NT_SEQ; C["radd"] += NT_SEQ

        def emit_attn():
            import contextlib as _ctx
            with _ctx.ExitStack() as ast_:
                asb = lambda name, shape, ty: ast_.enter_context(nc.sbuf_tensor(un(name), shape, ty))
                AB["xaT"] = asb("xaT", [128, 8 * S], BF16)
                AB["wqh"] = [asb(f"wqh{i}", [128, 8 * 128], BF16) for i in range(2)]
                AB["wkh"] = [asb(f"wkh{i}", [128, 8 * 128], BF16) for i in range(2)]
                AB["wvh"] = asb("wvh", [128, 8 * HD], BF16)
                AB["wgh"] = asb("wgh", [128, 8 * HD], BF16)
                AB["woh"] = asb("woh", [128, 2 * D], BF16)
                AB["tabq"] = asb("tabq", [128, 2 * 1024], F32)
                AB["tabk"] = asb("tabk", [128, 2 * 1024], F32)
                AB["qsb"] = asb("qsb", [128, S], BF16)
                AB["ksb"] = asb("ksb", [128, S], BF16)
                AB["trot"] = asb("trot", [128, 512], BF16)
                AB["lostage"] = asb("lostage", [128, 1024], BF16)
                AB["vsb"] = asb("vsb", [128, NT_SEQ * HD], BF16)
                AB["gsb"] = asb("gsb", [128, NT_SEQ * HD], BF16)
                AB["ysb"] = asb("ysb", [128, NT_SEQ * HD], BF16)
                AB["sT"] = [asb(f"sT{i}", [128, 512], BF16) for i in range(3)]
                AB["csb"] = asb("csb", [128, 2048], BF16)
                AB["cT"] = asb("cT", [128, NT_SEQ], F32)
                AB["phi"] = asb("phi", [128, NT_SEQ], F32)
                AB["yss"] = asb("yss", [128, NT_SEQ], F32)
                AB["gat"] = asb("gat", [128, 4 * HD], BF16)
                AB["gatT"] = asb("gatT", [128, 2 * 256], BF16)
                AB["dlb"] = [asb(f"dlb{i}", [128, D], BF16) for i in range(2)]
                for s_i in range(int(_os.environ.get("ATTNS", SEQ_PER_CORE))):
                    sbase = s_i * NT_SEQ
                    attn_ln(sbase)
                    for h in range(int(_os.environ.get("ATTNH", NH))):
                        emit_head(s_i, h, sbase)

        def emit_dump(i):
            with nc.Block() as blk:
                @blk.gpsimd
                def _(g):
                    g.dma_start(dbg_d[i].rearrange("(t p) d -> p t d", p=128),
                                R[:].rearrange("p (t d) -> p t d", d=D)).then_inc(sms["store"], 16)
                    g.wait_ge(sms["store"], C["store"] + 16)
            C["store"] += 16

        if upto >= 2:
            emit_ffn()
        if dump:
            emit_dump(0)
        if upto >= 3:
            emit_attn()
        if dump:
            emit_dump(1)
        if upto >= 4:
            emit_ffn()
        if dump:
            emit_dump(2)
        with nc.Block() as blk:
            @blk.gpsimd
            def _(g):
                g.dma_start(out_d.rearrange("(t p) d -> p t d", p=128),
                            R[:].rearrange("p (t d) -> p t d", d=D)).then_inc(sms["store"], 16)
                g.wait_ge(sms["store"], C["store"] + 16)
        C["store"] += 16
    return nc


def make_in_maps(L0_attn_x, L0_attn_wq, L0_attn_wk, L0_attn_wv, L0_attn_wg, L0_attn_wo,
                 L0_ffn_w_gate, L0_ffn_w_up, L0_ffn_w_down):
    rotq, rotk, rotqlo, rotklo, masks, rmat = _host_tables()
    x = np.ascontiguousarray(L0_attn_x, dtype=np.float32).reshape(B, S, D)
    in_maps = []
    for core in range(NCORES):
        xs = np.ascontiguousarray(x[core * SEQ_PER_CORE:(core + 1) * SEQ_PER_CORE].reshape(T, D))
        in_maps.append({
            "x": xs,
            "wq": np.ascontiguousarray(L0_attn_wq, dtype=np.float32),
            "wk": np.ascontiguousarray(L0_attn_wk, dtype=np.float32),
            "wv": np.ascontiguousarray(L0_attn_wv, dtype=np.float32),
            "wgt": np.ascontiguousarray(L0_attn_wg, dtype=np.float32),
            "wo": np.ascontiguousarray(L0_attn_wo, dtype=np.float32),
            "wfg": np.ascontiguousarray(L0_ffn_w_gate, dtype=np.float32),
            "wfu": np.ascontiguousarray(L0_ffn_w_up, dtype=np.float32),
            "wfd": np.ascontiguousarray(L0_ffn_w_down, dtype=np.float32),
            "rotq": rotq, "rotk": rotk, "rotqlo": rotqlo, "rotklo": rotklo,
            "masks": masks, "rmat": rmat,
        })
    return in_maps


def kernel(L0_attn_x, L0_attn_wq, L0_attn_wk, L0_attn_wv, L0_attn_wg, L0_attn_wo,
           L0_ffn_w_gate, L0_ffn_w_up, L0_ffn_w_down):
    nc = build_nc(dump=False)
    in_maps = make_in_maps(L0_attn_x, L0_attn_wq, L0_attn_wk, L0_attn_wv, L0_attn_wg,
                           L0_attn_wo, L0_ffn_w_gate, L0_ffn_w_up, L0_ffn_w_down)
    res = run_bass_kernel_spmd(nc, in_maps, core_ids=list(range(NCORES)))
    shards = [np.asarray(res.results[core]["out"]) for core in range(NCORES)]
    out = np.empty((B, S, D), dtype=shards[0].dtype)
    for core in range(NCORES):
        out[core * SEQ_PER_CORE:(core + 1) * SEQ_PER_CORE] = shards[core].reshape(SEQ_PER_CORE, S, D)
    return out


# version 12
# speedup vs baseline: 1.4651x; 1.0249x over previous
import numpy as np
import ml_dtypes
import concourse.bass as bass
import concourse.mybir as mybir
from concourse.bass_utils import run_bass_kernel_spmd
from concourse.alu_op_type import AluOpType as Op

F32 = mybir.dt.float32
BF16 = mybir.dt.bfloat16
I16 = mybir.dt.int16
AF = mybir.ActivationFunctionType
AX = mybir.AxisListType

NCORES = 8
B, S, D = 16, 2048, 1024
SEQ_PER_CORE = B // NCORES
T = SEQ_PER_CORE * S
NT = T // 128
NT_SEQ = S // 128
IF = 2816
NH, KD, HD, VD = 8, 128, 256, 2048
EPS = 1e-5
SC_TILES = 8
N_SC = NT // SC_TILES
FFN_GROUPS = [6, 6, 5, 5]
FFN_GOFF = [0, 6, 12, 17]
GSUB = {6: [512, 256], 5: [512, 128]}
BF = np.dtype(ml_dtypes.bfloat16)


def _host_tables():
    lin = np.linspace(0.0, 1.0, KD // 2)
    angle = np.repeat(1.0 / 10000.0 ** lin, 2)
    ar32 = np.arange(NH, dtype=np.float32)
    decay = np.log1p(-(np.float32(2.0) ** (np.float32(-5.0) - ar32)).astype(np.float32)
                     ).astype(np.float32).astype(np.float64)
    pos = np.arange(S, dtype=np.float64)
    sin = np.sin(pos[:, None] * angle[None, :])
    cos = np.cos(pos[:, None] * angle[None, :])
    e32 = np.exp(decay[:, None] * np.arange(S)[None, :]).astype(np.float32).astype(np.float64)
    nm = 1.0 / np.sqrt(np.cumsum(e32, axis=1))
    Dp = np.exp(pos[None, :] * decay[:, None]) * nm
    Ep = np.exp(-pos[None, :] * decay[:, None]) * (KD ** -0.5)
    rotq = np.empty((NH, 2, KD, S), dtype=np.float32)
    rotk = np.empty((NH, 2, KD, S), dtype=np.float32)
    rq64 = np.empty((NH, 2, KD, S), dtype=np.float64)
    rk64 = np.empty((NH, 2, KD, S), dtype=np.float64)
    for h in range(NH):
        rq64[h, 0] = cos.T * Dp[h][None, :]
        rq64[h, 1] = sin.T * Dp[h][None, :]
        rk64[h, 0] = cos.T * Ep[h][None, :]
        rk64[h, 1] = sin.T * Ep[h][None, :]
    rotq[:] = rq64.astype(np.float32)
    rotk[:] = rk64.astype(np.float32)
    LOSCALE = 2.0 ** 40
    rotqlo = ((rq64 - rotq.astype(np.float64)) * LOSCALE).astype(BF)
    rotklo = ((rk64 - rotk.astype(np.float64)) * LOSCALE).astype(BF)
    masks = np.zeros((4, 128, 512), dtype=BF)
    jj = np.arange(128)[:, None]
    ii = np.arange(512)[None, :]
    for d in range(4):
        masks[d] = (jj <= ii - 128 * d).astype(BF)
    rmat = np.zeros((KD, KD), dtype=BF)
    for m in range(KD // 2):
        rmat[2 * m + 1, 2 * m] = -1.0
        rmat[2 * m, 2 * m + 1] = 1.0
    return rotq, rotk, rotqlo, rotklo, masks, rmat


def _r3(ap, k=16):
    return ap.rearrange("p (b k) -> p b k", k=k)


def build_nc(dump=False, upto=99):
    import os as _os
    import contextlib
    import itertools
    nc = bass.Bass("TRN2", target_bir_lowering=False)
    dt = lambda name, shape, ty, kind: nc.dram_tensor(name, shape, ty, kind=kind).ap()
    x_in = dt("x", [T, D], F32, "ExternalInput")
    wq_d = dt("wq", [D, D], F32, "ExternalInput")
    wk_d = dt("wk", [D, D], F32, "ExternalInput")
    wv_d = dt("wv", [D, VD], F32, "ExternalInput")
    wgt_d = dt("wgt", [D, VD], F32, "ExternalInput")
    wo_d = dt("wo", [VD, D], F32, "ExternalInput")
    wfg_d = dt("wfg", [IF, D], F32, "ExternalInput")
    wfu_d = dt("wfu", [IF, D], F32, "ExternalInput")
    wfd_d = dt("wfd", [D, IF], F32, "ExternalInput")
    rotq_d = dt("rotq", [NH, 2, KD, S], F32, "ExternalInput")
    rotk_d = dt("rotk", [NH, 2, KD, S], F32, "ExternalInput")
    rotqlo_d = dt("rotqlo", [NH, 2, KD, S], BF16, "ExternalInput")
    rotklo_d = dt("rotklo", [NH, 2, KD, S], BF16, "ExternalInput")
    masks_d = dt("masks", [4, 128, 512], BF16, "ExternalInput")
    rmat_d = dt("rmat", [KD, KD], BF16, "ExternalInput")
    out_d = dt("out", [T, D], F32, "ExternalOutput")
    wfgqT = dt("wfgqT", [D, IF], BF16, "Internal")
    wfuqT = dt("wfuqT", [D, IF], BF16, "Internal")
    wfdqT = dt("wfdqT", [IF, D], BF16, "Internal")
    dbg_d = [dt(f"dbg{i}", [T, D], F32, "ExternalOutput") for i in range(3)] if dump else None

    _uc = itertools.count()
    un = lambda n: f"{n}_{next(_uc)}"
    with contextlib.ExitStack() as st:
        sb = lambda name, shape, ty: st.enter_context(nc.sbuf_tensor(un(name), shape, ty))
        R = sb("R", [128, NT * D], BF16)
        ident = sb("ident", [128, 128], BF16)
        identi = sb("identi", [128, 128], mybir.dt.int32)
        rmat = sb("rmat", [128, 128], BF16)
        masks = sb("masks", [128, 4 * 512], BF16)
        ones1 = sb("ones1", [128, 1], BF16)
        msum = sb("msum", [128, NT_SEQ], F32)
        ssq = sb("ssq", [128, NT_SEQ], F32)
        negm = sb("negm", [128, NT_SEQ], F32)
        rstd = sb("rstd", [128, NT_SEQ], F32)
        stA = sb("stA", [128, NT_SEQ], F32)
        stB = sb("stB", [128, NT_SEQ], F32)
        stC = sb("stC", [128, NT_SEQ], F32)
        bm = sb("bm", [128, 192], F32)
        scq = sb("scq", [128, 192], F32)
        rscq = sb("rscq", [128, 192], F32)
        sc2 = sb("sc2", [128, 192], F32)
        QW = 1024
        qb = dict(
            xn=sb("q_xn", [128, QW], F32),
            a=sb("q_a", [128, QW], F32),
            sg=sb("q_sg", [128, QW], BF16),
            b=sb("q_b", [128, QW], BF16), c=sb("q_c", [128, QW], BF16),
        )
        xqtile = sb("xqtile", [128, D], BF16)
        t1a = xqtile
        T1B = {}
        SEMS = ["dma", "dmg", "ssq", "stats", "std", "rstd", "t1", "xq", "xqtr",
                "xqev", "gu", "guev", "hq", "hread", "hqtr", "hqev", "down", "radd",
                "aux", "at1", "atr", "atev", "qmm", "qev", "rot", "qd",
                "vg", "vgev", "smm", "sev", "omm", "yev", "csum", "phi", "gat",
                "gtr", "gtev", "dl", "dlev", "store", "vch", "sch",
                "xin", "cst1", "cst2", "wta", "wua", "wda", "wtb", "wub", "wdb",
                "tq", "tk", "lwq", "lwk", "lwv", "lwg", "lwo", "qxn", "qab",
                "tqa", "tqb", "tka", "tkb", "tcb", "sevs", "qpd"]
        sms = {n: st.enter_context(nc.semaphore("sem_" + n)) for n in SEMS}
        C = {n: 0 for n in SEMS}
        LIVE = {n: 0 for n in SEMS}
        CHV = {"sem": "vch", "val": 0}
        CHS = {"sem": "sch", "val": 0}

        def pub(ins, name, k=1):
            ins.then_inc(sms[name], k)
            LIVE[name] += k
            return ins

        def gpub(ins, name, k=1):
            return pub(ins, name, k)

        def vop(v, meth, *a, _pub=None, **kw):
            v.wait_ge(sms[CHV["sem"]], CHV["val"])
            ins = meth(*a, **kw)
            nm = _pub if _pub is not None else "vch"
            pub(ins, nm, 1)
            CHV["sem"], CHV["val"] = nm, LIVE[nm]
            return ins

        def sop(s, meth, *a, _pub=None, **kw):
            s.wait_ge(sms[CHS["sem"]], CHS["val"])
            ins = meth(*a, **kw)
            nm = _pub if _pub is not None else "sch"
            pub(ins, nm, 1)
            CHS["sem"], CHS["val"] = nm, LIVE[nm]
            return ins

        QCV = [0]
        QCS = [0]

        def quant_chain(v, src_ap, out_ap, nf, rs_b, sc_b, pub_first=None, pub_last=None):
            xn = qb["xn"][:, :nf]
            A = qb["a"][:, :nf]
            SG = qb["sg"][:, :nf]
            b_, c_ = qb["b"], qb["c"]
            vop(v, v.tensor_tensor, out=_r3(xn), in0=_r3(src_ap), in1=rs_b, op=Op.mult)
            vop(v, v.tensor_scalar, out=SG, in0=src_ap, scalar1=0.0, scalar2=0.5,
                op0=Op.is_gt, op1=Op.subtract, _pub=pub_first)
            vop(v, v.tensor_scalar, out=A, in0=xn, scalar1=-1.0, scalar2=None, op0=Op.mult)
            vop(v, v.tensor_tensor, out=A, in0=xn, in1=A, op=Op.max)
            vop(v, v.tensor_scalar, out=b_[:, :nf], in0=A, scalar1=0.25, scalar2=None, op0=Op.is_gt)
            for thr, w in ((0.75, None), (1.25, None), (1.75, None), (2.5, 2.0), (3.5, 2.0), (5.0, 4.0)):
                if w is None:
                    vop(v, v.tensor_scalar, out=c_[:, :nf], in0=A, scalar1=thr, scalar2=None, op0=Op.is_gt)
                else:
                    vop(v, v.tensor_scalar, out=c_[:, :nf], in0=A, scalar1=thr, scalar2=w, op0=Op.is_gt, op1=Op.mult)
                vop(v, v.tensor_tensor, out=b_[:, :nf], in0=b_[:, :nf], in1=c_[:, :nf], op=Op.add)
            vop(v, v.tensor_tensor, out=b_[:, :nf], in0=b_[:, :nf], in1=SG, op=Op.mult)
            return vop(v, v.tensor_tensor, out=_r3(out_ap), in0=_r3(b_[:, :nf]), in1=sc_b, op=Op.mult, _pub=pub_last)

        def quant_scale(v, bmn, nb):
            vop(v, v.tensor_scalar, out=scq[:, :nb], in0=bmn, scalar1=1.0 / 6.0, scalar2=1e-12, op0=Op.mult, op1=Op.max)
            vop(v, v.reciprocal, rscq[:, :nb], scq[:, :nb])

        with nc.Block() as blk:
            @blk.gpsimd
            def _(g):
                gpub(g.iota(identi[:], [[1, 128]], base=0, channel_multiplier=-1), "aux")
                xi = x_in.rearrange("(t p) d -> p t d", p=128)
                Ri = R[:].rearrange("p (t d) -> p t d", d=D)
                g.dma_start(Ri, xi).then_inc(sms["xin"], 16)
            @blk.sync
            def _(sy):
                sy.dma_start(rmat[:], rmat_d).then_inc(sms["cst1"], 16)
                sy.dma_start(_r3(masks[:], k=512), masks_d.transpose([1, 0, 2])).then_inc(sms["cst2"], 16)
            @blk.vector
            def _(v):
                vop(v, v.memset, ones1[:], 1.0)
                v.wait_ge(sms["aux"], C["aux"] + 1)
                vop(v, v.tensor_scalar, out=ident[:], in0=identi[:], scalar1=0, scalar2=None, op0=Op.is_equal)
            @blk.scalar
            def _(s):
                s.wait_ge(sms["xin"], 16)
                s.wait_ge(sms["cst1"], 16)
                s.wait_ge(sms["cst2"], 16)
        C["xin"] += 16
        C["cst1"] += 16
        C["cst2"] += 16
        C["aux"] += 1

        prep = [(wfg_d, wfgqT, 22, D), (wfu_d, wfuqT, 22, D), (wfd_d, wfdqT, 8, IF)]
        if upto < 1:
            prep = []
        nprep = sum(p[2] for p in prep)
        with nc.Block() as blk:
            with (
                nc.psum_tensor(un("ptrW"), [128, IF], BF16, is_transpose_dest=True) as ptrW,
                nc.sbuf_tensor(un("wld"), [128, IF], F32) as wld,
                nc.sbuf_tensor(un("wqt"), [128, IF], BF16) as wqt,
                nc.sbuf_tensor(un("wtr"), [128, IF], BF16) as wtr,
            ):
                @blk.gpsimd
                def _(g):
                    n = 0
                    for src, dst, nchunk, width in prep:
                        for ch in range(nchunk):
                            if n >= 1:
                                g.wait_ge(sms["aux"], C["aux"] + n)
                            g.dma_start(wld[:, :width], src[ch * 128:(ch + 1) * 128, :]).then_inc(sms["dmg"], 16)
                            n += 1
                @blk.vector
                def _(v):
                    n = 0
                    for src, dst, nchunk, width in prep:
                        for ch in range(nchunk):
                            v.wait_ge(sms["dmg"], C["dmg"] + 16 * (n + 1))
                            if n >= 1:
                                v.wait_ge(sms["xqtr"], C["xqtr"] + n)
                            pieces = [(0, width)] if width <= QW else [(o, 704) for o in range(0, width, 704)]
                            nball = width // 16
                            vop(v, v.tensor_reduce, out=bm[:, :nball], in_=_r3(wld[:, :width]), axis=AX.X,
                                op=Op.max, apply_absolute_value=True)
                            quant_scale(v, bm[:, :nball], nball)
                            for pi, (off, w2) in enumerate(pieces):
                                nb = w2 // 16
                                rs_b = rscq[:, off // 16:off // 16 + nb].unsqueeze(2).broadcast_to([128, nb, 16])
                                sc_b = scq[:, off // 16:off // 16 + nb].unsqueeze(2).broadcast_to([128, nb, 16])
                                quant_chain(v, wld[:, off:off + w2], wqt[:, off:off + w2], w2, rs_b, sc_b,
                                            pub_first="aux" if pi == len(pieces) - 1 else None,
                                            pub_last="xq" if pi == len(pieces) - 1 else None)
                            n += 1
                @blk.tensor
                def _(t_):
                    n = 0
                    for src, dst, nchunk, width in prep:
                        ntr = width // 128
                        for ch in range(nchunk):
                            t_.wait_ge(sms["xq"], C["xq"] + n + 1)
                            if n >= 1:
                                t_.wait_ge(sms["xqev"], C["xqev"] + n)
                            for k in range(ntr):
                                ins = t_.transpose(ptrW[:, k * 128:(k + 1) * 128], wqt[:, k * 128:(k + 1) * 128], ident[:])
                            pub(ins, "xqtr")
                            n += 1
                @blk.scalar
                def _(s):
                    n = 0
                    for src, dst, nchunk, width in prep:
                        ntr = width // 128
                        for ch in range(nchunk):
                            s.wait_ge(sms["xqtr"], C["xqtr"] + n + 1)
                            if n >= 1:
                                s.wait_ge(sms["dma"], C["dma"] + 16 * n)
                            sop(s, s.copy, wtr[:, :ntr * 128], ptrW[:, :ntr * 128], _pub="xqev")
                            n += 1
                @blk.sync
                def _(sy):
                    n = 0
                    for src, dst, nchunk, width in prep:
                        ntr = width // 128
                        for ch in range(nchunk):
                            sy.wait_ge(sms["xqev"], C["xqev"] + n + 1)
                            dsti = dst[0:width, ch * 128:(ch + 1) * 128].rearrange("(k p) c -> p k c", p=128)
                            sy.dma_start(dsti, wtr[:, :ntr * 128].rearrange("p (k c) -> p k c", c=128)).then_inc(sms["dma"], 16)
                            n += 1
                    if n:
                        sy.wait_ge(sms["dma"], C["dma"] + 16 * n)
        C["dmg"] += 16 * nprep
        C["xq"] += nprep
        C["aux"] += nprep
        C["xqtr"] += nprep
        C["xqev"] += nprep
        C["dma"] += 16 * nprep

        WG_MAX = 6
        FB = {}

        def ffn_stage_a(tbase):
            xqT = FB["xqT"]
            t1b = T1B["b"]
            with nc.Block() as blk:
                with nc.psum_tensor(un("ptrA"), [128, 2 * 8 * 128], BF16, is_transpose_dest=True) as ptrA:
                    @blk.vector
                    def _(v):
                        nco = SC_TILES
                        for t in range(SC_TILES):
                            vop(v, v.tensor_reduce, out=msum[:, t:t + 1], in_=R[:, (tbase + t) * D:(tbase + t + 1) * D],
                                axis=AX.X, op=Op.add)
                        v.wait_ge(sms["ssq"], C["ssq"] + 1)
                        vop(v, v.tensor_scalar, out=negm[:, :nco], in0=msum[:, :nco], scalar1=-1.0 / D, scalar2=None, op0=Op.mult)
                        vop(v, v.tensor_scalar, out=stA[:, :nco], in0=ssq[:, :nco], scalar1=1.0 / D, scalar2=None, op0=Op.mult)
                        vop(v, v.tensor_tensor, out=stB[:, :nco], in0=negm[:, :nco], in1=negm[:, :nco], op=Op.mult)
                        vop(v, v.tensor_tensor, out=stC[:, :nco], in0=stA[:, :nco], in1=stB[:, :nco], op=Op.subtract)
                        vop(v, v.tensor_scalar, out=stA[:, :nco], in0=stC[:, :nco], scalar1=EPS, scalar2=None,
                            op0=Op.add, _pub="stats")
                        v.wait_ge(sms["std"], C["std"] + 1)
                        vop(v, v.reciprocal, rstd[:, :nco], stB[:, :nco], _pub="rstd")
                        nb = D // 16
                        for t in range(SC_TILES):
                            v.wait_ge(sms["t1"], C["t1"] + t + 1)
                            if t >= 1:
                                v.wait_ge(sms["xqtr"], C["xqtr"] + t)
                            vop(v, v.tensor_reduce, out=bm[:, :nb], in_=_r3(t1b[:]), axis=AX.X,
                                op=Op.max, apply_absolute_value=True)
                            vop(v, v.tensor_scalar, out=sc2[:, :nb], in0=bm[:, :nb], scalar1=rstd[:, t:t + 1],
                                scalar2=None, op0=Op.mult)
                            quant_scale(v, sc2[:, :nb], nb)
                            vop(v, v.tensor_scalar, out=sc2[:, :nb], in0=rscq[:, :nb], scalar1=rstd[:, t:t + 1],
                                scalar2=None, op0=Op.mult)
                            rs_b = sc2[:, :nb].unsqueeze(2).broadcast_to([128, nb, 16])
                            sc_b = scq[:, :nb].unsqueeze(2).broadcast_to([128, nb, 16])
                            quant_chain(v, t1b[:], xqtile[:], D, rs_b, sc_b, pub_last="xq")
                    @blk.scalar
                    def _(s):
                        for t in range(SC_TILES):
                            sop(s, s.activation, qb["a"][:, :D], R[:, (tbase + t) * D:(tbase + t + 1) * D],
                                AF.Square, accum_out=ssq[:, t:t + 1],
                                _pub="ssq" if t == SC_TILES - 1 else None)
                        s.wait_ge(sms["stats"], C["stats"] + 1)
                        sop(s, s.activation, stB[:, :SC_TILES], stA[:, :SC_TILES], AF.Sqrt, _pub="std")
                        s.wait_ge(sms["rstd"], C["rstd"] + 1)
                        for t in range(SC_TILES):
                            if t >= 1:
                                s.wait_ge(sms["xq"], C["xq"] + t)
                            sop(s, s.activation, t1b[:], R[:, (tbase + t) * D:(tbase + t + 1) * D], AF.Identity,
                                bias=negm[:, t:t + 1], scale=1.0, _pub="t1")
                            if t >= 1:
                                s.wait_ge(sms["xqtr"], C["xqtr"] + t)
                                par = (t - 1) % 2
                                sop(s, s.copy,
                                    xqT[:].rearrange("p (c x) -> p c x", x=SC_TILES * 128)[:, :, (t - 1) * 128:t * 128],
                                    ptrA[:, par * 1024:(par + 1) * 1024].rearrange("p (c x) -> p c x", x=128)
                                    , _pub="xqev")
                        s.wait_ge(sms["xqtr"], C["xqtr"] + SC_TILES)
                        par = (SC_TILES - 1) % 2
                        sop(s, s.copy,
                            xqT[:].rearrange("p (c x) -> p c x", x=SC_TILES * 128)[:, :, (SC_TILES - 1) * 128:SC_TILES * 128],
                            ptrA[:, par * 1024:(par + 1) * 1024].rearrange("p (c x) -> p c x", x=128)
                            , _pub="xqev")
                    @blk.tensor
                    def _(t_):
                        for t in range(SC_TILES):
                            t_.wait_ge(sms["xq"], C["xq"] + t + 1)
                            if t >= 2:
                                t_.wait_ge(sms["xqev"], C["xqev"] + t - 1)
                            par = t % 2
                            for k in range(8):
                                ins = t_.transpose(ptrA[:, par * 1024 + k * 128:par * 1024 + (k + 1) * 128],
                                                   xqtile[:, k * 128:(k + 1) * 128], ident[:])
                            pub(ins, "xqtr")
            C["ssq"] += 1; C["stats"] += 1; C["std"] += 1; C["rstd"] += 1
            C["t1"] += SC_TILES; C["xq"] += SC_TILES; C["xqtr"] += SC_TILES; C["xqev"] += SC_TILES

        def ffn_stage_b(tbase):
            xqT, wgq, wuq, wdq = FB["xqT"], FB["wgq"], FB["wuq"], FB["wdq"]
            sgt, upt, hqt, hqT = FB["sgt"], FB["upt"], FB["hqt"], FB["hqT"]
            with nc.Block() as blk:
                with (
                    nc.psum_tensor(un("pgu"), [128, 4 * 512], F32) as pgu,
                    nc.psum_tensor(un("pd"), [128, 1024], F32) as pd,
                    nc.psum_tensor(un("ptrB"), [128, WG_MAX * 128], BF16, is_transpose_dest=True) as ptrB,
                ):
                    @blk.sync
                    def _(sy):
                        for g in range(4):
                            if g >= 2:
                                sy.wait_ge(sms["gu"], C["gu"] + (g - 1) * SC_TILES)
                                sy.wait_ge(sms["down"], C["down"] + (g - 1) * SC_TILES)
                            nch = FFN_GROUPS[g]
                            off = FFN_GOFF[g]
                            buf = g % 2
                            src_g = wfgqT[:, off * 128:(off + nch) * 128].rearrange("(c p) w -> p c w", p=128)
                            src_u = wfuqT[:, off * 128:(off + nch) * 128].rearrange("(c p) w -> p c w", p=128)
                            wsem = [("wta", "wua", "wda"), ("wtb", "wub", "wdb")][buf]
                            sy.dma_start(wgq[buf][:, :8 * nch * 128].rearrange("p (c w) -> p c w", w=nch * 128), src_g).then_inc(sms[wsem[0]], 16)
                            sy.dma_start(wuq[buf][:, :8 * nch * 128].rearrange("p (c w) -> p c w", w=nch * 128), src_u).then_inc(sms[wsem[1]], 16)
                            src_d = wfdqT[off * 128:(off + nch) * 128, :].rearrange("(c p) w -> p c w", p=128)
                            sy.dma_start(wdq[buf][:, :nch * 1024].rearrange("p (c w) -> p c w", w=1024), src_d).then_inc(sms[wsem[2]], 16)
                    NDN = 4 * SC_TILES
                    gof = lambda dn: dn // SC_TILES

                    def t_gu(t_, dn):
                        g = gof(dn); nch = FFN_GROUPS[g]; s1 = GSUB[nch][1]
                        t = dn % SC_TILES; buf = g % 2
                        if t == 0:
                            wsem = [("wta", "wua", "wda"), ("wtb", "wub", "wdb")][buf]
                            for _w in wsem:
                                t_.wait_ge(sms[_w], C[_w] + 16 * (g // 2 + 1))
                        if dn >= 1:
                            t_.wait_ge(sms["guev"], C["guev"] + dn)
                        for c in range(8):
                            stat = xqT[:, c * SC_TILES * 128 + t * 128: c * SC_TILES * 128 + (t + 1) * 128]
                            t_.matmul(pgu[:, 0:512], stat,
                                      wgq[buf][:, c * nch * 128: c * nch * 128 + 512],
                                      start=(c == 0), stop=(c == 7))
                            t_.matmul(pgu[:, 512:1024], stat,
                                      wuq[buf][:, c * nch * 128: c * nch * 128 + 512],
                                      start=(c == 0), stop=(c == 7))
                            t_.matmul(pgu[:, 1024:1024 + s1], stat,
                                      wgq[buf][:, c * nch * 128 + 512: c * nch * 128 + 512 + s1],
                                      start=(c == 0), stop=(c == 7))
                            ins = t_.matmul(pgu[:, 1536:1536 + s1], stat,
                                            wuq[buf][:, c * nch * 128 + 512: c * nch * 128 + 512 + s1],
                                            start=(c == 0), stop=(c == 7))
                        pub(ins, "gu")

                    def t_trdown(t_, dn):
                        g = gof(dn); nch = FFN_GROUPS[g]; buf = g % 2
                        t_.wait_ge(sms["hq"], C["hq"] + dn + 1)
                        if dn >= 1:
                            t_.wait_ge(sms["hqev"], C["hqev"] + dn)
                        for k in range(nch):
                            ins = t_.transpose(ptrB[:, k * 128:(k + 1) * 128], hqt[:, k * 128:(k + 1) * 128], ident[:])
                        pub(ins, "hqtr")
                        t_.wait_ge(sms["hqev"], C["hqev"] + dn + 1)
                        if dn >= 1:
                            t_.wait_ge(sms["radd"], C["radd"] + dn)
                        tile = tbase + dn % SC_TILES
                        for half in range(2):
                            t_.matmul(pd[:, half * 512:(half + 1) * 512], ident[:],
                                      R[:, tile * D + half * 512:tile * D + (half + 1) * 512],
                                      start=True, stop=False)
                        for c in range(nch):
                            for half in range(2):
                                ins = t_.matmul(pd[:, half * 512:(half + 1) * 512],
                                                hqT[:, c * 128:(c + 1) * 128],
                                                wdq[buf][:, c * 1024 + half * 512: c * 1024 + (half + 1) * 512],
                                                start=False, stop=(c == nch - 1))
                        pub(ins, "down")

                    @blk.tensor
                    def _(t_):
                        t_gu(t_, 0)
                        for dn in range(NDN):
                            if dn + 1 < NDN:
                                t_gu(t_, dn + 1)
                            t_trdown(t_, dn)

                    def s_evac(s, dn):
                        g = gof(dn); nch = FFN_GROUPS[g]; s1 = GSUB[nch][1]
                        s.wait_ge(sms["gu"], C["gu"] + dn + 1)
                        if dn >= 1:
                            s.wait_ge(sms["hread"], C["hread"] + dn)
                        sop(s, s.activation, sgt[:, 0:512], pgu[:, 0:512], AF.Silu)
                        sop(s, s.activation, sgt[:, 512:512 + s1], pgu[:, 1024:1024 + s1], AF.Silu)
                        sop(s, s.copy, upt[:, 0:512], pgu[:, 512:1024])
                        sop(s, s.copy, upt[:, 512:512 + s1], pgu[:, 1536:1536 + s1], _pub="guev")

                    @blk.scalar
                    def _(s):
                        s_evac(s, 0)
                        for dn in range(NDN):
                            if dn + 1 < NDN:
                                s_evac(s, dn + 1)
                            nch = FFN_GROUPS[gof(dn)]
                            s.wait_ge(sms["hqtr"], C["hqtr"] + dn + 1)
                            sop(s, s.copy, hqT[:, :nch * 128], ptrB[:, :nch * 128], _pub="hqev")
                            s.wait_ge(sms["down"], C["down"] + dn + 1)
                            tile = tbase + dn % SC_TILES
                            sop(s, s.copy, R[:, tile * D:(tile + 1) * D], pd[:, :1024], _pub="radd")

                    @blk.vector
                    def _(v):
                        for dn in range(NDN):
                            g = gof(dn); nch = FFN_GROUPS[g]
                            width = nch * 128
                            nb = width // 16
                            hbuf = FB["hprod"][:, :width]
                            v.wait_ge(sms["guev"], C["guev"] + dn + 1)
                            vop(v, v.tensor_tensor, out=hbuf, in0=sgt[:, :width],
                                in1=upt[:, :width], op=Op.mult, _pub="hread")
                            if dn >= 1:
                                v.wait_ge(sms["hqtr"], C["hqtr"] + dn)
                            vop(v, v.tensor_reduce, out=bm[:, :nb], in_=_r3(hbuf), axis=AX.X,
                                op=Op.max, apply_absolute_value=True)
                            quant_scale(v, bm[:, :nb], nb)
                            vop(v, v.tensor_scalar, out=sc2[:, :nb], in0=scq[:, :nb], scalar1=0.5,
                                scalar2=None, op0=Op.mult)
                            rs_b = rscq[:, :nb].unsqueeze(2).broadcast_to([128, nb, 16])
                            sc_b = sc2[:, :nb].unsqueeze(2).broadcast_to([128, nb, 16])
                            quant_chain(v, hbuf, hqt[:, :width], width, rs_b, sc_b, pub_last="hq")
            for _w in ("wta", "wua", "wda", "wtb", "wub", "wdb"):
                C[_w] += 32
            C["gu"] += 4 * SC_TILES
            C["guev"] += 4 * SC_TILES
            C["hq"] += 4 * SC_TILES; C["hread"] += 4 * SC_TILES
            C["hqtr"] += 4 * SC_TILES; C["hqev"] += 4 * SC_TILES
            C["down"] += 4 * SC_TILES; C["radd"] += 4 * SC_TILES

        def emit_ffn():
            import contextlib as _ctx
            with _ctx.ExitStack() as fst:
                fsb = lambda name, shape, ty: fst.enter_context(nc.sbuf_tensor(un(name), shape, ty))
                FB["xqT"] = fsb("xqT", [128, 8 * SC_TILES * 128], BF16)
                T1B["b"] = fsb("t1b", [128, D], F32)
                FB["wgq"] = [fsb(f"wgq{i}", [128, 8 * WG_MAX * 128], BF16) for i in range(2)]
                FB["wuq"] = [fsb(f"wuq{i}", [128, 8 * WG_MAX * 128], BF16) for i in range(2)]
                FB["wdq"] = [fsb(f"wdq{i}", [128, WG_MAX * 1024], BF16) for i in range(2)]
                FB["sgt"] = fsb("sgt", [128, WG_MAX * 128], F32)
                FB["upt"] = fsb("upt", [128, WG_MAX * 128], F32)
                FB["hqt"] = fsb("hqt", [128, WG_MAX * 128], BF16)
                FB["hprod"] = fsb("hprod", [128, WG_MAX * 128], F32)
                FB["hqT"] = fsb("hqT", [128, WG_MAX * 128], BF16)
                for sc in range(int(_os.environ.get("FFNSC", N_SC))):
                    ffn_stage_a(sc * SC_TILES)
                    if _os.environ.get("FFNB", "1") == "1":
                        ffn_stage_b(sc * SC_TILES)

        AB = {}

        def attn_ln(sbase):
            xaT = AB["xaT"]
            with nc.Block() as blk:
                with nc.psum_tensor(un("ptrL"), [128, 2 * 8 * 128], BF16, is_transpose_dest=True) as ptrL:
                    @blk.vector
                    def _(v):
                        nco = NT_SEQ
                        for t in range(NT_SEQ):
                            vop(v, v.tensor_reduce, out=msum[:, t:t + 1], in_=R[:, (sbase + t) * D:(sbase + t + 1) * D],
                                axis=AX.X, op=Op.add)
                        v.wait_ge(sms["ssq"], C["ssq"] + 1)
                        vop(v, v.tensor_scalar, out=negm[:, :nco], in0=msum[:, :nco], scalar1=-1.0 / D, scalar2=None, op0=Op.mult)
                        vop(v, v.tensor_scalar, out=stA[:, :nco], in0=ssq[:, :nco], scalar1=1.0 / D, scalar2=None, op0=Op.mult)
                        vop(v, v.tensor_tensor, out=stB[:, :nco], in0=negm[:, :nco], in1=negm[:, :nco], op=Op.mult)
                        vop(v, v.tensor_tensor, out=stC[:, :nco], in0=stA[:, :nco], in1=stB[:, :nco], op=Op.subtract)
                        vop(v, v.tensor_scalar, out=stA[:, :nco], in0=stC[:, :nco], scalar1=EPS, scalar2=None,
                            op0=Op.add, _pub="stats")
                        v.wait_ge(sms["std"], C["std"] + 1)
                        vop(v, v.reciprocal, rstd[:, :nco], stB[:, :nco])
                        vop(v, v.tensor_tensor, out=stC[:, :nco], in0=negm[:, :nco], in1=rstd[:, :nco],
                            op=Op.mult, _pub="rstd")
                    @blk.scalar
                    def _(s):
                        for t in range(NT_SEQ):
                            sop(s, s.activation, qb["a"][:, :D], R[:, (sbase + t) * D:(sbase + t + 1) * D],
                                AF.Square, accum_out=ssq[:, t:t + 1],
                                _pub="ssq" if t == NT_SEQ - 1 else None)
                        s.wait_ge(sms["stats"], C["stats"] + 1)
                        sop(s, s.activation, stB[:, :NT_SEQ], stA[:, :NT_SEQ], AF.Sqrt, _pub="std")
                        s.wait_ge(sms["rstd"], C["rstd"] + 1)
                        for t in range(NT_SEQ):
                            if t >= 1:
                                s.wait_ge(sms["atr"], C["atr"] + t)
                            sop(s, s.activation, t1a[:], R[:, (sbase + t) * D:(sbase + t + 1) * D], AF.Identity,
                                bias=stC[:, t:t + 1], scale=rstd[:, t:t + 1], _pub="at1")
                            if t >= 1:
                                par = (t - 1) % 2
                                sop(s, s.copy,
                                    xaT[:].rearrange("p (c x) -> p c x", x=S)[:, :, (t - 1) * 128:t * 128],
                                    ptrL[:, par * 1024:(par + 1) * 1024].rearrange("p (c x) -> p c x", x=128)
                                    , _pub="atev")
                        s.wait_ge(sms["atr"], C["atr"] + NT_SEQ)
                        par = (NT_SEQ - 1) % 2
                        sop(s, s.copy,
                            xaT[:].rearrange("p (c x) -> p c x", x=S)[:, :, (NT_SEQ - 1) * 128:NT_SEQ * 128],
                            ptrL[:, par * 1024:(par + 1) * 1024].rearrange("p (c x) -> p c x", x=128)
                            , _pub="atev")
                    @blk.tensor
                    def _(t_):
                        for t in range(NT_SEQ):
                            t_.wait_ge(sms["at1"], C["at1"] + t + 1)
                            if t >= 2:
                                t_.wait_ge(sms["atev"], C["atev"] + t - 1)
                            par = t % 2
                            for k in range(8):
                                ins = t_.transpose(ptrL[:, par * 1024 + k * 128:par * 1024 + (k + 1) * 128],
                                                   t1a[:, k * 128:(k + 1) * 128], ident[:])
                            pub(ins, "atr")
                    if sbase == 0:
                        @blk.gpsimd
                        def _(g):
                            g.dma_start(AB["wqh"][0][:].rearrange("p (c w) -> p c w", w=128),
                                        wq_d[:, 0:128].rearrange("(c p) w -> p c w", p=128)).then_inc(sms["lwq"], 16)
                            g.dma_start(AB["wkh"][0][:].rearrange("p (c w) -> p c w", w=128),
                                        wk_d[:, 0:128].rearrange("(c p) w -> p c w", p=128)).then_inc(sms["lwk"], 16)
            C["ssq"] += 1; C["stats"] += 1; C["std"] += 1; C["rstd"] += 1
            C["at1"] += NT_SEQ; C["atr"] += NT_SEQ; C["atev"] += NT_SEQ

        def emit_head(s_i, h, sbase):
            gi = s_i * NH + h
            nxt = None
            if gi + 1 < SEQ_PER_CORE * NH:
                nxt = ((gi + 1) // NH, (gi + 1) % NH)
            xaT, wqh, wkh, wvh, wgh, woh = AB["xaT"], AB["wqh"], AB["wkh"], AB["wvh"], AB["wgh"], AB["woh"]
            tabq, tabk, qsb, ksb, trot = AB["tabq"], AB["tabk"], AB["qsb"], AB["ksb"], AB["trot"]
            lostage = AB["lostage"]
            qD, kE = qsb, ksb
            vsb, gsb, ysb, sT, csb = AB["vsb"], AB["gsb"], AB["ysb"], AB["sT"], AB["csb"]
            cT, phi, yss, gat, gatT = AB["cT"], AB["phi"], AB["yss"], AB["gat"], AB["gatT"]
            with nc.Block() as blk:
                with (
                    nc.psum_tensor(un("pq"), [128, 2 * 512], F32) as pq,
                    nc.psum_tensor(un("pr"), [128, 2 * 512], F32) as pr,
                    nc.psum_tensor(un("pv2"), [128, 2 * 512], F32) as pv2,
                ):
                    @blk.sync
                    def _(sy):
                        sy.wait_ge(sms["qd"], C["qd"])
                        for idx in range(8):
                            qc, which = idx // 2, idx % 2
                            slot = qc % 2
                            if qc >= 2:
                                sy.wait_ge(sms["qd"], C["qd"] + 2 * (qc - 2) + which + 1)
                            tab_d = rotq_d if which == 0 else rotk_d
                            tab_s = tabq if which == 0 else tabk
                            semn = [("tqa", "tqb"), ("tka", "tkb")][which][slot]
                            sy.dma_start(
                                tab_s[:, slot * 1024:(slot + 1) * 1024].rearrange("p (a x) -> p a x", x=512),
                                tab_d[h][:, :, qc * 512:(qc + 1) * 512].transpose([1, 0, 2])
                            ).then_inc(sms[semn], 16)
                            lo_d = rotqlo_d if which == 0 else rotklo_d
                            n_lo = C["tq"] // 16 + idx
                            sy.wait_ge(sms["tcb"], n_lo)
                            sy.dma_start(
                                lostage[:].rearrange("p (a x) -> p a x", x=512),
                                lo_d[h][:, :, qc * 512:(qc + 1) * 512].transpose([1, 0, 2])
                            ).then_inc(sms["tq"], 16)
                    @blk.gpsimd
                    def _(g):
                        if nxt is not None:
                            s2, h2 = nxt
                            sl2 = (gi + 1) % 2
                            g.wait_ge(sms["qmm"], C["qmm"])
                            g.wait_ge(sms["lwq"], 16 * (gi + 1))
                            g.wait_ge(sms["lwk"], 16 * (gi + 1))
                            g.dma_start(wqh[sl2][:].rearrange("p (c w) -> p c w", w=128),
                                        wq_d[:, h2 * 128:(h2 + 1) * 128].rearrange("(c p) w -> p c w", p=128)).then_inc(sms["lwq"], 16)
                            g.dma_start(wkh[sl2][:].rearrange("p (c w) -> p c w", w=128),
                                        wk_d[:, h2 * 128:(h2 + 1) * 128].rearrange("(c p) w -> p c w", p=128)).then_inc(sms["lwk"], 16)
                        g.wait_ge(sms["vg"], C["vg"])
                        g.dma_start(wvh[:].rearrange("p (c w) -> p c w", w=HD),
                                    wv_d[:, h * HD:(h + 1) * HD].rearrange("(c p) w -> p c w", p=128)).then_inc(sms["lwv"], 16)
                        g.dma_start(wgh[:].rearrange("p (c w) -> p c w", w=HD),
                                    wgt_d[:, h * HD:(h + 1) * HD].rearrange("(c p) w -> p c w", p=128)).then_inc(sms["lwg"], 16)
                        g.wait_ge(sms["dl"], C["dl"])
                        g.dma_start(woh[:].rearrange("p (c w) -> p c w", w=D),
                                    wo_d[h * HD:(h + 1) * HD, :].rearrange("(c p) w -> p c w", p=128)).then_inc(sms["lwo"], 16)
                    @blk.tensor
                    def _(t_):
                        wsl = gi % 2
                        t_.wait_ge(sms["lwq"], 16 * (gi + 1))
                        t_.wait_ge(sms["lwk"], 16 * (gi + 1))
                        for qc in range(4):
                            par = qc % 2
                            if qc >= 1:
                                t_.wait_ge(sms["qev"], C["qev"] + 2 * qc)
                            for c in range(8):
                                ins = t_.matmul(pq[:, par * 512:(par + 1) * 512],
                                                wqh[wsl][:, c * 128:(c + 1) * 128],
                                                xaT[:, c * S + qc * 512:c * S + (qc + 1) * 512], start=(c == 0), stop=(c == 7))
                            pub(ins, "qmm")
                            for c in range(8):
                                ins = t_.matmul(pq[:, (1 - par) * 512:(2 - par) * 512],
                                                wkh[wsl][:, c * 128:(c + 1) * 128],
                                                xaT[:, c * S + qc * 512:c * S + (qc + 1) * 512], start=(c == 0), stop=(c == 7))
                            pub(ins, "qmm")
                        for idx in range(8):
                            qc, which = idx // 2, idx % 2
                            t_.wait_ge(sms["qev"], C["qev"] + 2 * qc + which + 1)
                            if idx >= 2:
                                t_.wait_ge(sms["qd"], C["qd"] + idx - 1)
                            par = idx % 2
                            src = qsb if which == 0 else ksb
                            pub(t_.matmul(pr[:, par * 512:(par + 1) * 512], rmat[:], src[:, qc * 512:(qc + 1) * 512],
                                          start=True, stop=True), "rot")
                        t_.wait_ge(sms["lwv"], C["lwv"] + 16)
                        t_.wait_ge(sms["lwg"], C["lwg"] + 16)
                        t_.wait_ge(sms["lwo"], C["lwo"] + 16)
                        t_.wait_ge(sms["qd"], C["qd"] + 8)
                        for t in range(NT_SEQ):
                            par = t % 2
                            if t >= 2:
                                t_.wait_ge(sms["vgev"], C["vgev"] + t - 1)
                            for c in range(8):
                                stat = xaT[:, c * S + t * 128:c * S + (t + 1) * 128]
                                t_.matmul(pv2[:, par * 512:par * 512 + 256], stat, wvh[:, c * HD:(c + 1) * HD],
                                          start=(c == 0), stop=(c == 7))
                            for c in range(8):
                                stat = xaT[:, c * S + t * 128:c * S + (t + 1) * 128]
                                ins = t_.matmul(pv2[:, par * 512 + 256:(par + 1) * 512], stat, wgh[:, c * HD:(c + 1) * HD],
                                                start=(c == 0), stop=(c == 7))
                            pub(ins, "vg")
                    @blk.scalar
                    def _(s):
                        for idx in range(8):
                            qc, which = idx // 2, idx % 2
                            par = qc % 2 if which == 0 else 1 - (qc % 2)
                            s.wait_ge(sms["qmm"], C["qmm"] + idx + 1)
                            dst = qsb if which == 0 else ksb
                            sop(s, s.copy, dst[:, qc * 512:(qc + 1) * 512], pq[:, par * 512:(par + 1) * 512]
                                , _pub="qev")
                        for t in range(NT_SEQ):
                            par = t % 2
                            s.wait_ge(sms["vg"], C["vg"] + t + 1)
                            sop(s, s.copy, vsb[:, t * HD:(t + 1) * HD], pv2[:, par * 512:par * 512 + 256])
                            sop(s, s.activation, gsb[:, t * HD:(t + 1) * HD], pv2[:, par * 512 + 256:(par + 1) * 512],
                                AF.Silu, _pub="vgev")
                    @blk.vector
                    def _(v):
                        for idx in range(8):
                            qc, which = idx // 2, idx % 2
                            slot = qc % 2
                            semn = [("tqa", "tqb"), ("tka", "tkb")][which][slot]
                            v.wait_ge(sms[semn], C[semn] + 16 * (qc // 2 + 1))
                            v.wait_ge(sms["tq"], C["tq"] + 16 * (idx + 1))
                            tab = tabq if which == 0 else tabk
                            vop(v, v.scalar_tensor_tensor, out=tab[:, slot * 1024:(slot + 1) * 1024],
                                in0=lostage[:], scalar=2.0 ** -40,
                                in1=tab[:, slot * 1024:(slot + 1) * 1024],
                                op0=Op.mult, op1=Op.add, _pub="tcb")
                            v.wait_ge(sms["rot"], C["rot"] + idx + 1)
                            v.wait_ge(sms["qev"], C["qev"] + 2 * qc + which + 1)
                            par = idx % 2
                            tab = tabq if which == 0 else tabk
                            src = qsb if which == 0 else ksb
                            dst = qD if which == 0 else kE
                            sl = slice(qc * 512, (qc + 1) * 512)
                            vop(v, v.tensor_tensor, out=trot[:], in0=pr[:, par * 512:(par + 1) * 512],
                                in1=tab[:, slot * 1024 + 512:slot * 1024 + 1024], op=Op.mult)
                            vop(v, v.tensor_tensor, out=dst[:, sl], in0=src[:, sl],
                                in1=tab[:, slot * 1024:slot * 1024 + 512], op=Op.mult)
                            vop(v, v.tensor_tensor, out=dst[:, sl], in0=dst[:, sl], in1=trot[:],
                                op=Op.add, _pub="qd")
                order = [(qc, kt) for qc in range(4) for kt in range((qc + 1) * 4)]
                skinds = [kt - qc * 4 >= 0 for (qc, kt) in order]
                nT = lambda k: sum(1 for k2 in skinds[:k] if k2)
                nF = lambda k: sum(1 for k2 in skinds[:k] if not k2)
                with (
                    nc.psum_tensor(un("ps"), [128, 3 * 512], F32) as ps,
                    nc.psum_tensor(un("po"), [128, 4 * 512], F32) as po,
                    nc.psum_tensor(un("pc"), [128, 512], F32) as pc,
                ):
                    @blk.tensor
                    def _(t_):
                        t_.wait_ge(sms["qd"], C["qd"] + 8)
                        t_.wait_ge(sms["vgev"], C["vgev"] + NT_SEQ)

                        def smm_emit(j):
                            qc, kt = order[j]
                            sl = j % 3
                            if j >= 3:
                                t_.wait_ge(sms["sev"], C["sev"] + nT(j - 2))
                                t_.wait_ge(sms["sevs"], C["sevs"] + nF(j - 2))
                            pub(t_.matmul(ps[:, sl * 512:(sl + 1) * 512], kE[:, kt * 128:(kt + 1) * 128],
                                          qD[:, qc * 512:(qc + 1) * 512], start=True, stop=True), "smm")

                        def out_emit(j):
                            qc, kt = order[j]
                            sl = j % 3
                            nkt = (qc + 1) * 4
                            if qc >= 1 and kt == 0:
                                t_.wait_ge(sms["yev"], C["yev"] + qc)
                            t_.wait_ge(sms["sev"], C["sev"] + nT(j + 1))
                            t_.wait_ge(sms["sevs"], C["sevs"] + nF(j + 1))
                            last_po = t_.matmul(pc[0:1, :512], ones1[:], sT[sl][:], start=(kt == 0), stop=(kt == nkt - 1))
                            for isub in range(4):
                                if kt <= qc * 4 + isub:
                                    last_po = t_.matmul(po[:, isub * 512:isub * 512 + 256],
                                                        sT[sl][:, isub * 128:(isub + 1) * 128],
                                                        vsb[:, kt * HD:(kt + 1) * HD],
                                                        start=(kt == 0), stop=(kt == qc * 4 + isub))
                            if kt == nkt - 1:
                                pub(last_po, "omm")

                        smm_emit(0)
                        smm_emit(1)
                        for j in range(len(order)):
                            if j + 2 < len(order):
                                smm_emit(j + 2)
                            out_emit(j)
                    @blk.vector
                    def _(v):
                        for j, (qc, kt) in enumerate(order):
                            sl = j % 3
                            delta = kt - qc * 4
                            if delta >= 0:
                                v.wait_ge(sms["smm"], C["smm"] + j + 1)
                                vop(v, v.tensor_tensor, out=sT[sl][:], in0=ps[:, sl * 512:(sl + 1) * 512],
                                    in1=masks[:, delta * 512:(delta + 1) * 512], op=Op.mult, _pub="sev")
                    @blk.scalar
                    def _(s):
                        for j, (qc, kt) in enumerate(order):
                            sl = j % 3
                            delta = kt - qc * 4
                            if delta < 0:
                                s.wait_ge(sms["smm"], C["smm"] + j + 1)
                                sop(s, s.copy, sT[sl][:], ps[:, sl * 512:(sl + 1) * 512], _pub="sevs")
                            if kt == (qc + 1) * 4 - 1:
                                s.wait_ge(sms["omm"], C["omm"] + qc + 1)
                                sop(s, s.copy, csb[0:1, qc * 512:(qc + 1) * 512], pc[0:1, :512])
                                sop(s, s.copy, ysb[:, qc * 4 * HD:(qc + 1) * 4 * HD].rearrange("p (i w) -> p i w", w=256),
                                    po[:].rearrange("p (i z w) -> p i (z w)", z=2, w=256)[:, :, 0:256], _pub="yev")
                with (
                    nc.psum_tensor(un("pt2"), [128, 512], BF16, is_transpose_dest=True) as pt2,
                    nc.psum_tensor(un("pct"), [128, 512], F32) as pct,
                    nc.psum_tensor(un("pdl"), [128, 2 * 1024], F32) as pdl,
                ):
                    @blk.tensor
                    def _(t_):
                        t_.wait_ge(sms["yev"], C["yev"] + 4)
                        for i in range(NT_SEQ):
                            ins = t_.matmul(pct[:, i:i + 1], csb[0:1, i * 128:(i + 1) * 128], ones1[0:1, 0:1],
                                            start=True, stop=True)
                        pub(ins, "csum")
                        for t in range(NT_SEQ):
                            t_.wait_ge(sms["gat"], C["gat"] + t + 1)
                            par = t % 2
                            g4 = t % 4
                            if t >= 2:
                                t_.wait_ge(sms["gtev"], C["gtev"] + t - 1)
                            for k in range(2):
                                ins = t_.transpose(pt2[:, par * 256 + k * 128:par * 256 + (k + 1) * 128],
                                                   gat[:, g4 * HD + k * 128:g4 * HD + (k + 1) * 128], ident[:])
                            pub(ins, "gtr")
                            t_.wait_ge(sms["gtev"], C["gtev"] + t + 1)
                            if t >= 2:
                                t_.wait_ge(sms["radd"], C["radd"] + t - 1)
                            tile0 = sbase + t
                            for half in range(2):
                                t_.matmul(pdl[:, par * 1024 + half * 512:par * 1024 + (half + 1) * 512],
                                          ident[:],
                                          R[:, tile0 * D + half * 512:tile0 * D + (half + 1) * 512],
                                          start=True, stop=False)
                            for k in range(2):
                                for half in range(2):
                                    ins = t_.matmul(pdl[:, par * 1024 + half * 512:par * 1024 + (half + 1) * 512],
                                                    gatT[:, par * 256 + k * 128:par * 256 + (k + 1) * 128],
                                                    woh[:, k * D + half * 512:k * D + (half + 1) * 512],
                                                    start=False, stop=(k == 1))
                            pub(ins, "dl")
                    @blk.vector
                    def _(v):
                        v.wait_ge(sms["csum"], C["csum"] + 1)
                        vop(v, v.tensor_copy, cT[:, :NT_SEQ], pct[:, :NT_SEQ])
                        vop(v, v.tensor_reduce, out=stA[:, :NT_SEQ], in_=cT[:, :NT_SEQ].unsqueeze(2), axis=AX.X,
                            op=Op.max, apply_absolute_value=True)
                        vop(v, v.tensor_scalar, out=stB[:, :NT_SEQ], in0=stA[:, :NT_SEQ], scalar1=1.0, scalar2=None, op0=Op.max)
                        vop(v, v.reciprocal, stA[:, :NT_SEQ], stB[:, :NT_SEQ])
                        vop(v, v.tensor_tensor, out=stB[:, :NT_SEQ], in0=stA[:, :NT_SEQ], in1=stA[:, :NT_SEQ], op=Op.mult)
                        v.wait_ge(sms["phi"], C["phi"] + 1)
                        vop(v, v.tensor_tensor, out=stC[:, :NT_SEQ], in0=stB[:, :NT_SEQ], in1=yss[:, :NT_SEQ], op=Op.mult)
                        vop(v, v.tensor_scalar, out=stB[:, :NT_SEQ], in0=stC[:, :NT_SEQ], scalar1=1.0 / HD, scalar2=EPS,
                            op0=Op.mult, op1=Op.add, _pub="aux")
                        v.wait_ge(sms["std"], C["std"] + 1)
                        vop(v, v.reciprocal, stC[:, :NT_SEQ], stB[:, :NT_SEQ])
                        vop(v, v.tensor_tensor, out=phi[:, :NT_SEQ], in0=stA[:, :NT_SEQ], in1=stC[:, :NT_SEQ], op=Op.mult)
                        for t in range(NT_SEQ):
                            g4 = t % 4
                            if t >= 4:
                                v.wait_ge(sms["gtr"], C["gtr"] + t - 3)
                            vop(v, v.scalar_tensor_tensor, out=gat[:, g4 * HD:(g4 + 1) * HD],
                                in0=ysb[:, t * HD:(t + 1) * HD], scalar=phi[:, t:t + 1],
                                in1=gsb[:, t * HD:(t + 1) * HD], op0=Op.mult, op1=Op.mult, _pub="gat")

                    @blk.scalar
                    def _(s):
                        s.wait_ge(sms["yev"], C["yev"] + 4)
                        for t in range(NT_SEQ):
                            sop(s, s.activation, qb["b"][:, :HD], ysb[:, t * HD:(t + 1) * HD], AF.Square,
                                accum_out=yss[:, t:t + 1],
                                _pub="phi" if t == NT_SEQ - 1 else None)
                        s.wait_ge(sms["aux"], C["aux"] + 1)
                        sop(s, s.activation, stB[:, :NT_SEQ], stB[:, :NT_SEQ], AF.Sqrt, _pub="std")
                        for t in range(NT_SEQ):
                            par = t % 2
                            s.wait_ge(sms["gtr"], C["gtr"] + t + 1)
                            sop(s, s.copy, gatT[:, par * 256:(par + 1) * 256], pt2[:, par * 256:(par + 1) * 256]
                                , _pub="gtev")
                            s.wait_ge(sms["dl"], C["dl"] + t + 1)
                            tile0 = sbase + t
                            sop(s, s.copy, R[:, tile0 * D:(tile0 + 1) * D], pdl[:, par * 1024:(par + 1) * 1024], _pub="radd")
            for _w in ("tqa", "tqb", "tka", "tkb"):
                C[_w] += 32
            C["tq"] += 16 * 8
            C["tcb"] += 8
            for _w in ("lwq", "lwk", "lwv", "lwg", "lwo"):
                C[_w] += 16
            C["qmm"] += 8; C["qev"] += 8; C["rot"] += 8; C["qd"] += 8
            C["vg"] += NT_SEQ; C["vgev"] += NT_SEQ
            C["smm"] += 40; C["sev"] += 16; C["sevs"] += 24; C["omm"] += 4; C["yev"] += 4
            C["csum"] += 1; C["phi"] += 1; C["std"] += 1; C["aux"] += 1
            C["gat"] += NT_SEQ; C["gtr"] += NT_SEQ; C["gtev"] += NT_SEQ
            C["dl"] += NT_SEQ; C["radd"] += NT_SEQ

        def emit_attn():
            import contextlib as _ctx
            with _ctx.ExitStack() as ast_:
                asb = lambda name, shape, ty: ast_.enter_context(nc.sbuf_tensor(un(name), shape, ty))
                AB["xaT"] = asb("xaT", [128, 8 * S], BF16)
                AB["wqh"] = [asb(f"wqh{i}", [128, 8 * 128], BF16) for i in range(2)]
                AB["wkh"] = [asb(f"wkh{i}", [128, 8 * 128], BF16) for i in range(2)]
                AB["wvh"] = asb("wvh", [128, 8 * HD], BF16)
                AB["wgh"] = asb("wgh", [128, 8 * HD], BF16)
                AB["woh"] = asb("woh", [128, 2 * D], BF16)
                AB["tabq"] = asb("tabq", [128, 2 * 1024], F32)
                AB["tabk"] = asb("tabk", [128, 2 * 1024], F32)
                AB["qsb"] = asb("qsb", [128, S], BF16)
                AB["ksb"] = asb("ksb", [128, S], BF16)
                AB["trot"] = asb("trot", [128, 512], BF16)
                AB["lostage"] = asb("lostage", [128, 1024], BF16)
                AB["vsb"] = asb("vsb", [128, NT_SEQ * HD], BF16)
                AB["gsb"] = asb("gsb", [128, NT_SEQ * HD], BF16)
                AB["ysb"] = asb("ysb", [128, NT_SEQ * HD], BF16)
                AB["sT"] = [asb(f"sT{i}", [128, 512], BF16) for i in range(3)]
                AB["csb"] = asb("csb", [128, 2048], BF16)
                AB["cT"] = asb("cT", [128, NT_SEQ], F32)
                AB["phi"] = asb("phi", [128, NT_SEQ], F32)
                AB["yss"] = asb("yss", [128, NT_SEQ], F32)
                AB["gat"] = asb("gat", [128, 4 * HD], BF16)
                AB["gatT"] = asb("gatT", [128, 2 * 256], BF16)
                for s_i in range(int(_os.environ.get("ATTNS", SEQ_PER_CORE))):
                    sbase = s_i * NT_SEQ
                    attn_ln(sbase)
                    for h in range(int(_os.environ.get("ATTNH", NH))):
                        emit_head(s_i, h, sbase)

        def emit_dump(i):
            with nc.Block() as blk:
                @blk.gpsimd
                def _(g):
                    g.dma_start(dbg_d[i].rearrange("(t p) d -> p t d", p=128),
                                R[:].rearrange("p (t d) -> p t d", d=D)).then_inc(sms["store"], 16)
                    g.wait_ge(sms["store"], C["store"] + 16)
            C["store"] += 16

        if upto >= 2:
            emit_ffn()
        if dump:
            emit_dump(0)
        if upto >= 3:
            emit_attn()
        if dump:
            emit_dump(1)
        if upto >= 4:
            emit_ffn()
        if dump:
            emit_dump(2)
        with nc.Block() as blk:
            @blk.gpsimd
            def _(g):
                g.dma_start(out_d.rearrange("(t p) d -> p t d", p=128),
                            R[:].rearrange("p (t d) -> p t d", d=D)).then_inc(sms["store"], 16)
                g.wait_ge(sms["store"], C["store"] + 16)
        C["store"] += 16
    return nc


def make_in_maps(L0_attn_x, L0_attn_wq, L0_attn_wk, L0_attn_wv, L0_attn_wg, L0_attn_wo,
                 L0_ffn_w_gate, L0_ffn_w_up, L0_ffn_w_down):
    rotq, rotk, rotqlo, rotklo, masks, rmat = _host_tables()
    x = np.ascontiguousarray(L0_attn_x, dtype=np.float32).reshape(B, S, D)
    in_maps = []
    for core in range(NCORES):
        xs = np.ascontiguousarray(x[core * SEQ_PER_CORE:(core + 1) * SEQ_PER_CORE].reshape(T, D))
        in_maps.append({
            "x": xs,
            "wq": np.ascontiguousarray(L0_attn_wq, dtype=np.float32),
            "wk": np.ascontiguousarray(L0_attn_wk, dtype=np.float32),
            "wv": np.ascontiguousarray(L0_attn_wv, dtype=np.float32),
            "wgt": np.ascontiguousarray(L0_attn_wg, dtype=np.float32),
            "wo": np.ascontiguousarray(L0_attn_wo, dtype=np.float32),
            "wfg": np.ascontiguousarray(L0_ffn_w_gate, dtype=np.float32),
            "wfu": np.ascontiguousarray(L0_ffn_w_up, dtype=np.float32),
            "wfd": np.ascontiguousarray(L0_ffn_w_down, dtype=np.float32),
            "rotq": rotq, "rotk": rotk, "rotqlo": rotqlo, "rotklo": rotklo,
            "masks": masks, "rmat": rmat,
        })
    return in_maps


def kernel(L0_attn_x, L0_attn_wq, L0_attn_wk, L0_attn_wv, L0_attn_wg, L0_attn_wo,
           L0_ffn_w_gate, L0_ffn_w_up, L0_ffn_w_down):
    nc = build_nc(dump=False)
    in_maps = make_in_maps(L0_attn_x, L0_attn_wq, L0_attn_wk, L0_attn_wv, L0_attn_wg,
                           L0_attn_wo, L0_ffn_w_gate, L0_ffn_w_up, L0_ffn_w_down)
    res = run_bass_kernel_spmd(nc, in_maps, core_ids=list(range(NCORES)))
    shards = [np.asarray(res.results[core]["out"]) for core in range(NCORES)]
    out = np.empty((B, S, D), dtype=shards[0].dtype)
    for core in range(NCORES):
        out[core * SEQ_PER_CORE:(core + 1) * SEQ_PER_CORE] = shards[core].reshape(SEQ_PER_CORE, S, D)
    return out
